# version 11
# baseline (speedup 1.0000x reference)
import numpy as np
import ml_dtypes
import concourse.bass as bass
import concourse.mybir as mybir
from concourse.bass_utils import run_bass_kernel_spmd
from contextlib import ExitStack

F32 = mybir.dt.float32
BF16 = mybir.dt.bfloat16
AF = mybir.ActivationFunctionType
ALU = mybir.AluOpType
AX = mybir.AxisListType

B, T, C = 2, 8192, 2048
HQ, HKV, HD = 16, 4, 128
HF = 8192
NDC = C // 128
NHC = HF // 128
NSLOT = 4
LN_EPS = 1e-5
RMS_EPS = float(np.finfo(np.float32).eps)
QTHRESH = 0.5009765625

DEBUG = False


class Sem:
    def __init__(self, nc, es, name):
        self.h = es.enter_context(nc.semaphore(name))
        self.v = 0

    def inc(self, inst, k=1):
        inst.then_inc(self.h, k)
        self.v += k
        return self.v

    def dinc(self, inst):
        return self.inc(inst, 16)



def _rope_perm():
    p = np.empty(128, dtype=np.int64)
    p[:64] = np.arange(64) * 2
    p[64:] = np.arange(64) * 2 + 1
    return p


def _rope_tables():
    i = np.arange(0, HD, 2, dtype=np.float32) / np.float32(HD)
    theta = (np.float64(1.0) / np.float64(10000.0) ** i.astype(np.float64)).astype(np.float32)
    t = np.arange(T, dtype=np.float32)
    ang = (t[:, None] * theta[None, :]).astype(np.float32)
    cos = np.cos(ang.astype(np.float64)).astype(np.float32)
    sin = np.sin(ang.astype(np.float64)).astype(np.float32)
    Ctab = np.ascontiguousarray(np.concatenate([cos.T, cos.T], axis=0))
    Stab = np.ascontiguousarray(np.concatenate([-sin.T, sin.T], axis=0))
    return Ctab, Stab


def _masks_for_core(k):
    out = np.zeros((NSLOT, 16, 128, 512), dtype=np.float32)
    for s in range(NSLOT):
        b = 4 * s + k
        Es = 512 * (4 * s + 4)
        q_abs = 512 * b + np.arange(512)
        for idx in range(16):
            k0 = Es - 2048 + idx * 128
            key_abs = k0 + np.arange(128)
            out[s, idx] = (key_abs[:, None] <= q_abs[None, :]).astype(np.float32)
    return np.ascontiguousarray(out.astype(ml_dtypes.bfloat16))


def _host_prep(inputs):
    x = np.asarray(inputs["L0_attn_x"], dtype=np.float32)
    Wq = np.asarray(inputs["L0_attn_Wq"], dtype=np.float32)
    bq = np.asarray(inputs["L0_attn_bq"], dtype=np.float32)
    Wk = np.asarray(inputs["L0_attn_Wk"], dtype=np.float32)
    bk = np.asarray(inputs["L0_attn_bk"], dtype=np.float32)
    Wv = np.asarray(inputs["L0_attn_Wv"], dtype=np.float32)
    bv = np.asarray(inputs["L0_attn_bv"], dtype=np.float32)
    Wo = np.asarray(inputs["L0_attn_Wo"], dtype=np.float32)
    bo = np.asarray(inputs["L0_attn_bo"], dtype=np.float32)
    Wgu = np.asarray(inputs["L0_ffn_w_gate_up"], dtype=np.float32)
    Wp = np.asarray(inputs["L0_ffn_w_proj"], dtype=np.float32)

    perm = _rope_perm()
    qperm = np.concatenate([h * 128 + perm for h in range(HQ)])
    kperm = np.concatenate([h * 128 + perm for h in range(HKV)])
    Ctab, Stab = _rope_tables()

    shared = {
        "wqp": np.ascontiguousarray(Wq[:, qperm]),
        "wkp": np.ascontiguousarray(Wk[:, kperm]),
        "wv": Wv,
        "wo": Wo,
        "bqt": np.ascontiguousarray(bq[qperm].reshape(HQ, 128).T),
        "bkt": np.ascontiguousarray(bk[kperm].reshape(HKV, 128).T),
        "bvt": np.ascontiguousarray(bv.reshape(HKV, 128).T),
        "bot": np.ascontiguousarray(bo.reshape(NDC, 128).T),

        "cosk": Ctab,
        "sink": Stab,
        "identf": np.eye(128, dtype=np.float32),
        "identb": np.ascontiguousarray(np.eye(128, dtype=np.float32).astype(ml_dtypes.bfloat16)),
        "ind2": np.ascontiguousarray(
            (np.kron(np.eye(2, dtype=np.float32), np.ones((64, 1), np.float32)) / 64.0
             ).astype(ml_dtypes.bfloat16)),
        "e2": np.ascontiguousarray(
            np.kron(np.eye(2, dtype=np.float32), np.ones((1, 64), np.float32)
                    ).astype(ml_dtypes.bfloat16)),
        "ones128b": np.ones((128, 1), dtype=ml_dtypes.bfloat16),
        "onesrowf": np.ones((64, 128), dtype=np.float32),
    }

    WguT = np.ascontiguousarray(Wgu.T)
    WpT = np.ascontiguousarray(Wp.T)
    per_core = []
    for c in range(8):
        b, k = c // 4, c % 4
        own = np.concatenate([np.arange(512) + (4 * s + k) * 512 for s in range(NSLOT)])
        d = dict(shared)
        d["wgut"] = np.ascontiguousarray(WguT[:, c * 2048:(c + 1) * 2048])
        d["wpt"] = np.ascontiguousarray(WpT[c * 1024:(c + 1) * 1024, :])
        d["xfull"] = np.ascontiguousarray(x[b])
        d["xown"] = np.ascontiguousarray(x[b][own])
        d["cosq"] = np.ascontiguousarray((0.25 * Ctab[:, own]).astype(np.float32))
        d["sinq"] = np.ascontiguousarray((0.25 * Stab[:, own]).astype(np.float32))
        d["masks"] = _masks_for_core(k)
        per_core.append(d)
    return per_core



def build_nc():
    nc = bass.Bass("TRN2", target_bir_lowering=False)
    es = ExitStack()

    def din(name, shape, dt=F32):
        return nc.dram_tensor(name, list(shape), dt, kind="ExternalInput").ap()

    xfull = din("xfull", [T, C])
    xown = din("xown", [NSLOT * 512, C])
    wqp = din("wqp", [C, C])
    wkp = din("wkp", [C, 512])
    wv = din("wv", [C, 512])
    wo = din("wo", [C, C])
    bqt = din("bqt", [128, HQ])
    bkt = din("bkt", [128, HKV])
    bvt = din("bvt", [128, HKV])
    bot = din("bot", [128, NDC])
    wgut = din("wgut", [C, 2048])
    wpt = din("wpt", [1024, C])
    cosk = din("cosk", [128, T])
    sink = din("sink", [128, T])
    cosq = din("cosq", [128, NSLOT * 512])
    sinq = din("sinq", [128, NSLOT * 512])
    masks = din("masks", [NSLOT, 16, 128, 512], BF16)
    identf = din("identf", [128, 128])
    identb = din("identb", [128, 128], BF16)
    ind2 = din("ind2", [128, 2], BF16)
    e2 = din("e2", [2, 128], BF16)
    ones128b = din("ones128b", [128, 1], BF16)
    onesrowf = din("onesrowf", [64, 128])

    y = nc.dram_tensor("y", [NSLOT * 512, C], F32, kind="ExternalOutput").ap()

    skind = "ExternalOutput" if DEBUG else "Internal"
    xnscr = nc.dram_tensor("xnscr", [NSLOT, 128, NDC, 512], BF16, kind=skind).ap()
    qscr = nc.dram_tensor("qscr", [NSLOT, 128, HQ, 512], BF16, kind=skind).ap()
    kscr = nc.dram_tensor("kscr", [HKV, 128, T], BF16, kind=skind).ap()
    vtscr = nc.dram_tensor("vtscr", [HKV, 128, T], BF16, kind=skind).ap()
    wguq_sh = nc.dram_tensor("wguq_sh", [16, 128, NDC, 128], BF16, kind="Internal").ap()
    wpq_sh = nc.dram_tensor("wpq_sh", [NDC, 128, 8, 128], BF16, kind="Internal").ap()
    wguq = nc.dram_tensor("wguq", [2 * HF // 128, 128, NDC, 128], BF16,
                          kind="Internal", addr_space="Shared").ap()
    wpq = nc.dram_tensor("wpq", [8, NDC, 128, 8, 128], BF16,
                         kind="Internal", addr_space="Shared").ap()
    sscr = nc.dram_tensor("sscr", [NSLOT, 4, 128, C], F32, kind=skind).ap()
    oscr = nc.dram_tensor("oscr", [NSLOT, 128, HQ * 512], BF16, kind=skind).ap() if DEBUG else None
    dnscr = nc.dram_tensor("dnscr", [NSLOT, HKV, 2, 64, 512], F32, kind=skind).ap() if DEBUG else None

    sems = {n: Sem(nc, es, n) for n in
            ["ve", "ac", "te", "gp", "dln", "dw", "dwt", "dst",
             "dk0", "dk1", "dq", "dg", "dp", "dxs", "dy"]}
    sVE, sAC, sTE, sGP = sems["ve"], sems["ac"], sems["te"], sems["gp"]
    sDLN, sDW, sDWT, sDST = sems["dln"], sems["dw"], sems["dwt"], sems["dst"]

    def gsb(name, shape, dt=F32):
        return es.enter_context(nc.sbuf_tensor(name, shape, dt))
    cbias = gsb("cbias", [128, 1])
    maxq2 = gsb("maxq2", [1, 1])
    maxk2 = gsb("maxk2", [1, 1])
    identb_sb = gsb("identb_sb", [128, 128], BF16)
    identf_sb = gsb("identf_sb", [128, 128])
    ind2_sb = gsb("ind2_sb", [128, 2], BF16)
    e2_sb = gsb("e2_sb", [2, 128], BF16)
    ones128b_sb = gsb("ones128b_sb", [128, 1], BF16)
    onesrowf_sb = gsb("onesrowf_sb", [64, 128])
    bq_sb = gsb("bq_sb", [128, HQ])
    bk_sb = gsb("bk_sb", [128, HKV])
    bv_sb = gsb("bv_sb", [128, HKV])
    bo_sb = gsb("bo_sb", [128, NDC])

    def emit_ln(blk, xbufs, stat4, xn_tm, tok_x):
        sums, ssq, mean, t1, msq, var, sd, rstd, scr = (
            stat4[k] for k in ["sums", "ssq", "mean", "t1", "msq", "var",
                               "sd", "rstd", "scratch"])

        def v1(v):
            for u in range(4):
                v.wait_ge(sDLN.h, tok_x[u])
                v.reduce_sum(out=sums[:, u:u + 1], in_=xbufs[u][:], axis=AX.X)
                v.scalar_tensor_tensor(out=scr[:], in0=xbufs[u][:], scalar=1.0,
                                       in1=xbufs[u][:], op0=ALU.mult, op1=ALU.mult,
                                       accum_out=ssq[:, u:u + 1])
            v.drain()
            v.tensor_scalar(out=mean[:], in0=sums[:], scalar1=1.0 / 2048,
                            scalar2=None, op0=ALU.mult)
            v.tensor_scalar(out=t1[:], in0=ssq[:], scalar1=1.0 / 2048,
                            scalar2=LN_EPS, op0=ALU.mult, op1=ALU.add)
            v.drain()
            v.tensor_tensor(out=msq[:], in0=mean[:], in1=mean[:], op=ALU.mult)
            v.drain()
            sVE.inc(v.tensor_tensor(out=var[:], in0=t1[:], in1=msq[:],
                                    op=ALU.subtract))
        blk.vector(v1)
        tok_var = sVE.v

        def s1(sc):
            sc.wait_ge(sVE.h, tok_var)
            sAC.inc(sc.sqrt(out=sd[:], in_=var[:]))
        blk.scalar(s1)
        tok_sd = sAC.v

        def v2(v):
            v.wait_ge(sAC.h, tok_sd)
            v.reciprocal(out=rstd[:], in_=sd[:])
            v.drain()
            for u in range(4):
                v.tensor_scalar(
                    out=xn_tm[u][:], in0=xbufs[u][:],
                    scalar1=mean[:, u:u + 1], scalar2=rstd[:, u:u + 1],
                    op0=ALU.subtract, op1=ALU.mult)
                sVE.inc(v.drain())
        blk.vector(v2)
        return sVE.v

    def emit_tm2fm(blk, xn_tm, xn_fm, tpb, tok_xn_last):
        tp_toks = []

        def tb(t):
            for u in range(4):
                t.wait_ge(sVE.h, tok_xn_last - 3 + u)
                for r in range(2):
                    if tp_toks:
                        t.wait_ge(sVE.h, ev_base + len(tp_toks))
                    for q in range(8):
                        dc = r * 8 + q
                        mm = t.transpose(tpb[:, q * 128:(q + 1) * 128],
                                         xn_tm[u][:, dc * 128:(dc + 1) * 128],
                                         identb_sb[:])
                    tp_toks.append(sTE.inc(mm))
        ev_base = sVE.v

        def vb(v):
            fmv = xn_fm[:].rearrange("p (dc f) -> p dc f", dc=NDC)
            ti = 0
            for u in range(4):
                for r in range(2):
                    v.wait_ge(sTE.h, tp_toks[ti]); ti += 1
                    dst = fmv[:, r * 8:(r + 1) * 8, u * 128:(u + 1) * 128]
                    v.tensor_copy(out=dst, in_=tpb[:].rearrange(
                        "p (q f) -> p q f", q=8))
                    sVE.inc(v.drain())
        blk.tensor(tb)
        blk.vector(vb)
        return sVE.v

    def soft_barrier(blk):
        a, v, t = sAC.v, sVE.v, sTE.v

        def tb_(e):
            e.wait_ge(sAC.h, a)
            e.wait_ge(sVE.h, v)

        def sb_(e):
            e.wait_ge(sTE.h, t)
            e.wait_ge(sVE.h, v)

        def vb_(e):
            e.wait_ge(sTE.h, t)
            e.wait_ge(sAC.h, a)
        blk.tensor(tb_)
        blk.scalar(sb_)
        blk.vector(vb_)

    with ExitStack() as ps:
        def sb(name, shape, dt=F32):
            return ps.enter_context(nc.sbuf_tensor(name, shape, dt))
        wq_sb = sb("wq_sb", [128, NDC * C], BF16)
        xbufs = [sb(f"a0xb{u}", [128, C]) for u in range(4)]
        stat4 = {k: sb("a0st_" + k, [128, 4]) for k in
                 ["sums", "ssq", "mean", "t1", "msq", "var", "sd", "rstd"]}
        stat4["scratch"] = sb("a0st_scratch", [128, C])
        xn_tm = [sb(f"a0xntm{u}", [128, C], BF16) for u in range(4)]
        xn_fm = sb("a0xnfm", [128, NDC * 512], BF16)
        qf = [sb(f"a0qf{i}", [128, 512]) for i in range(2)]
        rsw = sb("a0rsw", [128, 512])
        rt1 = sb("a0rt1", [128, 512])
        rt2 = sb("a0rt2", [128, 512])
        qstage = sb("a0qstage", [128, HQ * 512], BF16)
        sq16 = sb("a0sq16", [128, 512], BF16)
        qn_red = sb("a0qn_red", [1, 1])
        cq_sb = sb("a0cq", [128, NSLOT * 512])
        sq_sb = sb("a0sq", [128, NSLOT * 512])

        qps = [ps.enter_context(nc.psum_tensor(f"a0qps{i}", [128, 512], F32))
               for i in range(2)]
        tpb = ps.enter_context(nc.psum_tensor("a0tpb", [128, 1024], BF16))
        qn_ps = ps.enter_context(nc.psum_tensor("a0qn_ps", [1, 512], F32))

        with nc.Block("a0") as blk:
            def sy0(sy):
                for dst, src in [(identb_sb, identb), (identf_sb, identf),
                                 (ind2_sb, ind2), (e2_sb, e2),
                                 (ones128b_sb, ones128b), (onesrowf_sb, onesrowf),
                                 (bq_sb, bqt), (bk_sb, bkt), (bv_sb, bvt),
                                 (bo_sb, bot), (cq_sb, cosq), (sq_sb, sinq)]:
                    sDW.dinc(sy.dma_start(out=dst[:], in_=src))
            blk.sync(sy0)
            tok_const = sDW.v

            def gp0(g):
                for dc in range(NDC):
                    sDWT.dinc(g.dma_start(out=wq_sb[:, dc * C:(dc + 1) * C],
                                          in_=wqp[dc * 128:(dc + 1) * 128, :]))
                g.memset(qn_red[:], 0.0)
                sGP.inc(g.memset(maxq2[:], 0.0))
            blk.gpsimd(gp0)
            tok_wq = sDWT.v
            tok_gz = sGP.v

            for s in range(NSLOT):
                soft_barrier(blk)
                tok_x = []
                ve_now = sVE.v

                def syx(sy, s=s, ve_now=ve_now):
                    sy.wait_ge(sVE.h, ve_now)
                    for u in range(4):
                        tok_x.append(sDLN.dinc(sy.dma_start(
                            out=xbufs[u][:],
                            in_=xown[s * 512 + u * 128: s * 512 + (u + 1) * 128, :])))
                blk.sync(syx)

                tok_xn = emit_ln(blk, xbufs, stat4, xn_tm, tok_x)
                tok_fm = emit_tm2fm(blk, xn_tm, xn_fm, tpb, tok_xn)

                def sxn(sy, s=s):
                    sy.wait_ge(sVE.h, tok_fm)
                    sDST.dinc(sy.dma_start(
                        out=xnscr[s],
                        in_=xn_fm[:].rearrange("p (dc f) -> p dc f", dc=NDC)))
                blk.sync(sxn)

                qmm_tok = []
                qev_base = sAC.v
                rope_base = sVE.v

                def tq(t, s=s):
                    t.wait_ge(sDWT.h, tok_wq)
                    t.wait_ge(sVE.h, tok_fm)
                    for g in range(HQ):
                        if g >= 2:
                            t.wait_ge(sAC.h, qev_base + (g - 2) + 1)
                        for dc in range(NDC):
                            mm = t.matmul(
                                qps[g % 2][:],
                                wq_sb[:, dc * C + g * 128: dc * C + (g + 1) * 128],
                                xn_fm[:, dc * 512:(dc + 1) * 512],
                                start=(dc == 0), stop=(dc == NDC - 1))
                        qmm_tok.append(sTE.inc(mm))
                blk.tensor(tq)

                def sq(sc, s=s):
                    sc.wait_ge(sDW.h, tok_const)
                    for g in range(HQ):
                        sc.wait_ge(sTE.h, qmm_tok[g])
                        if g >= 2:
                            sc.wait_ge(sVE.h, rope_base + (g - 2) + 1)
                        sAC.inc(sc.activation(out=qf[g % 2][:], in_=qps[g % 2][:],
                                              func=AF.Identity, bias=bq_sb[:, g:g + 1],
                                              scale=1.0))
                blk.scalar(sq)

                def vq(v, s=s):
                    v.wait_ge(sDW.h, tok_const)
                    for g in range(HQ):
                        v.wait_ge(sAC.h, qev_base + g + 1)
                        q = qf[g % 2]
                        v.tensor_copy(out=rsw[0:64, :], in_=q[64:128, :])
                        v.tensor_copy(out=rsw[64:128, :], in_=q[0:64, :])
                        v.tensor_tensor(out=rt1[:], in0=q[:],
                                        in1=cq_sb[:, s * 512:(s + 1) * 512], op=ALU.mult)
                        v.tensor_tensor(out=rt2[:], in0=rsw[:],
                                        in1=sq_sb[:, s * 512:(s + 1) * 512], op=ALU.mult)
                        sVE.inc(v.tensor_tensor(
                            out=qstage[:, g * 512:(g + 1) * 512],
                            in0=rt1[:], in1=rt2[:], op=ALU.add))
                    v.drain()
                blk.vector(vq)
                tok_rope_last = sVE.v

                for g in range(HQ):
                    def s2(v, g=g, s=s):
                        v.wait_ge(sVE.h, tok_rope_last - (HQ - 1) + g)
                        if g >= 1:
                            v.wait_ge(sTE.h, qn_mm_tok[g - 1])
                        qsl = qstage[:, g * 512:(g + 1) * 512]
                        sVE.inc(v.tensor_tensor(out=sq16[:], in0=qsl, in1=qsl,
                                                op=ALU.mult))
                    blk.vector(s2)
                    tok_sq16 = sVE.v

                    def t2(t, g=g):
                        t.wait_ge(sDW.h, tok_const)
                        t.wait_ge(sVE.h, tok_sq16)
                        qn_mm_tok[g] = sTE.inc(t.matmul(
                            qn_ps[:], ones128b_sb[:], sq16[:], start=True, stop=True))
                    if g == 0:
                        qn_mm_tok = {}
                    blk.tensor(t2)

                    def v2(v, g=g):
                        v.wait_ge(sTE.h, qn_mm_tok[g])
                        v.wait_ge(sGP.h, tok_gz)
                        v.tensor_reduce(out=qn_red[:], in_=qn_ps[:], axis=AX.X,
                                        op=ALU.max)
                        v.drain()
                        sVE.inc(v.tensor_tensor(out=maxq2[:], in0=maxq2[:],
                                                in1=qn_red[:], op=ALU.max))
                        v.drain()
                    blk.vector(v2)

                def syq(sy, s=s):
                    sy.wait_ge(sVE.h, sVE.v)
                    sDST.dinc(sy.dma_start(
                        out=qscr[s],
                        in_=qstage[:].rearrange("p (g f) -> p g f", g=HQ)))
                blk.sync(syq)

            def syf(sy):
                sy.wait_ge(sDST.h, sDST.v)
            blk.sync(syf)

    with ExitStack() as ps:
        def sb(name, shape, dt=F32):
            return ps.enter_context(nc.sbuf_tensor(name, shape, dt))
        wk_sb = sb("wk_sb", [128, NDC * 512], BF16)
        wv_sb = sb("wv_sb", [128, NDC * 512], BF16)
        ck_sb = [sb(f"ck_sb{i}", [128, 512]) for i in range(2)]
        sk_sb = [sb(f"sk_sb{i}", [128, 512]) for i in range(2)]
        xbufs = [sb(f"a1xb{u}", [128, C]) for u in range(8)]
        stat4 = {k: sb("a1st_" + k, [128, 4]) for k in
                 ["sums", "ssq", "mean", "t1", "msq", "var", "sd", "rstd"]}
        stat4["scratch"] = sb("a1st_scratch", [128, C])
        xn_tm = [sb(f"a1xntm{u}", [128, C], BF16) for u in range(8)]
        xn_fm = [sb(f"a1xnfm{i}", [128, NDC * 512], BF16) for i in range(2)]
        kf = [sb(f"a1kf{i}", [128, 512]) for i in range(2)]
        rsw = sb("a1rsw", [128, 512])
        rt1 = sb("a1rt1", [128, 512])
        rt2 = sb("a1rt2", [128, 512])
        krope = [sb(f"a1krope{i}", [128, 512], BF16) for i in range(2)]
        vf16 = [sb(f"a1vf16{i}", [128, 512], BF16) for i in range(2)]
        vt_stage = [sb(f"a1vt{i}", [128, 512], BF16) for i in range(2)]
        sq16 = [sb(f"a1sq16{i}", [128, 512], BF16) for i in range(2)]
        kn_red = sb("a1kn_red", [1, 1])
        ctmp = sb("a1ctmp", [1, 1])
        ctmp2 = sb("a1ctmp2", [1, 1])

        kps = [ps.enter_context(nc.psum_tensor(f"a1kps{i}", [128, 512], F32))
               for i in range(2)]
        tpb = ps.enter_context(nc.psum_tensor("a1tpb", [128, 1024], BF16))
        kn_ps = ps.enter_context(nc.psum_tensor("a1kn_ps", [1, 512], F32))
        cb_ps = ps.enter_context(nc.psum_tensor("a1cb_ps", [128, 1], F32))

        with nc.Block("a1") as blk:


            def gp0(g):
                for dc in range(NDC):
                    sDWT.dinc(g.dma_start(out=wk_sb[:, dc * 512:(dc + 1) * 512],
                                          in_=wkp[dc * 128:(dc + 1) * 128, :]))
                    sDWT.dinc(g.dma_start(out=wv_sb[:, dc * 512:(dc + 1) * 512],
                                          in_=wv[dc * 128:(dc + 1) * 128, :]))
                g.memset(kn_red[:], 0.0)
                sGP.inc(g.memset(maxk2[:], 0.0))
            blk.gpsimd(gp0)
            tok_wkv = sDWT.v
            tok_gz = sGP.v

            kev_tok = {}
            krope_tok = {}
            kst_tok = {}
            ksq_tok = {}
    
            vev_tok = {}
            vtp_tok = {}
            vts_tok = {}
            vst_tok = {}
            kn_mm_tok = {}
            kn_v_tok = {}
            xn_used = {}
            fm_tok = {}
            fm_tok_v = {}
            tab_tok = {}
            for ti in range(16):
                tok_x = []
                ve_now = sVE.v

                def syx(sy, ti=ti, ve_now=ve_now):
                    if ti >= 2:
                        sy.wait_ge(sVE.h, xn_ve_done[ti - 2])
                    for u in range(4):
                        tok_x.append(sDLN.dinc(sy.dma_start(
                            out=xbufs[(ti % 2) * 4 + u][:],
                            in_=xfull[ti * 512 + u * 128: ti * 512 + (u + 1) * 128, :])))
                    sy.wait_ge(sVE.h, ve_now)
                    sDW.dinc(sy.dma_start(out=ck_sb[ti % 2][:],
                                          in_=cosk[:, ti * 512:(ti + 1) * 512]))
                    tab_tok[ti] = sDW.dinc(sy.dma_start(
                        out=sk_sb[ti % 2][:], in_=sink[:, ti * 512:(ti + 1) * 512]))
                blk.sync(syx)
                if ti == 0:
                    xn_ve_done = {}

                xb = xbufs[(ti % 2) * 4:(ti % 2) * 4 + 4]
                xt = xn_tm[(ti % 2) * 4:(ti % 2) * 4 + 4]
                if ti >= 2:
                    def vgate(v, ti=ti):
                        v.wait_ge(sTE.h, fm_tok[ti - 2])
                    blk.vector(vgate)
                tok_xn = emit_ln(blk, xb, stat4, xt, tok_x)
                xn_ve_done[ti] = tok_xn
                if ti >= 2:
                    def vgate2(v, ti=ti):
                        v.wait_ge(sTE.h, xn_used[ti - 2])
                    blk.vector(vgate2)
                tok_fm = emit_tm2fm(blk, xt, xn_fm[ti % 2], tpb, tok_xn)
                fm_tok[ti] = sTE.v
                fm_tok_v[ti] = tok_fm

                for j in range(HKV):
                    n = ti * HKV + j
                    fb = n % 2

                    def tk(t, ti=ti, j=j, n=n, fb=fb):
                        t.wait_ge(sDWT.h, tok_wkv)
                        t.wait_ge(sVE.h, fm_tok_v[ti])
                        if n - 1 in kev_tok:
                            t.wait_ge(sAC.h, kev_tok[n - 1])
                        for dc in range(NDC):
                            mm = t.matmul(
                                kps[0][:],
                                wk_sb[:, dc * 512 + j * 128: dc * 512 + (j + 1) * 128],
                                xn_fm[ti % 2][:, dc * 512:(dc + 1) * 512],
                                start=(dc == 0), stop=(dc == NDC - 1))
                        sTE.inc(mm)
                    blk.tensor(tk)
                    tok_kmm = sTE.v

                    def sk(sc, j=j, n=n, fb=fb):
                        sc.wait_ge(sTE.h, tok_kmm)
                        if n - 2 in krope_tok:
                            sc.wait_ge(sVE.h, krope_tok[n - 2])
                        kev_tok[n] = sAC.inc(sc.activation(
                            out=kf[fb][:], in_=kps[0][:], func=AF.Identity,
                            bias=bk_sb[:, j:j + 1], scale=1.0))
                    blk.scalar(sk)

                    def vk(v, j=j, ti=ti, n=n, fb=fb):
                        v.wait_ge(sAC.h, kev_tok[n])
                        v.wait_ge(sDW.h, tab_tok[ti])
                        if n - 2 in kst_tok:
                            v.wait_ge(sDST.h, kst_tok[n - 2])
                        q = kf[fb]
                        v.tensor_copy(out=rsw[0:64, :], in_=q[64:128, :])
                        v.tensor_copy(out=rsw[64:128, :], in_=q[0:64, :])
                        v.tensor_tensor(out=rt1[:], in0=q[:],
                                        in1=ck_sb[ti % 2][:], op=ALU.mult)
                        v.tensor_tensor(out=rt2[:], in0=rsw[:],
                                        in1=sk_sb[ti % 2][:], op=ALU.mult)
                        v.tensor_tensor(out=krope[fb][:], in0=rt1[:], in1=rt2[:],
                                        op=ALU.add)
                        if n - 2 in kn_mm_tok:
                            v.wait_ge(sTE.h, kn_mm_tok[n - 2])
                        v.tensor_tensor(out=sq16[fb][:], in0=krope[fb][:],
                                        in1=krope[fb][:], op=ALU.mult)
                        krope_tok[n] = sVE.inc(v.drain())
                    blk.vector(vk)

                    def syk(sy, j=j, ti=ti, n=n, fb=fb):
                        sy.wait_ge(sVE.h, krope_tok[n])
                        kst_tok[n] = sDST.dinc(sy.dma_start(
                            out=kscr[j][:, ti * 512:(ti + 1) * 512], in_=krope[fb][:]))
                    blk.sync(syk)

                    def t2(t, n=n, fb=fb):
                        t.wait_ge(sVE.h, krope_tok[n])
                        if n - 1 in kn_v_tok:
                            t.wait_ge(sVE.h, kn_v_tok[n - 1])
                        kn_mm_tok[n] = sTE.inc(t.matmul(
                            kn_ps[:], ones128b_sb[:], sq16[fb][:],
                            start=True, stop=True))
                    blk.tensor(t2)

                    def v2(v, n=n):
                        v.wait_ge(sTE.h, kn_mm_tok[n])
                        v.wait_ge(sGP.h, tok_gz)
                        v.tensor_reduce(out=kn_red[:], in_=kn_ps[:], axis=AX.X,
                                        op=ALU.max)
                        v.drain()
                        kn_v_tok[n] = sVE.inc(v.tensor_tensor(
                            out=maxk2[:], in0=maxk2[:], in1=kn_red[:], op=ALU.max))
                        v.drain()
                    blk.vector(v2)

                    def tv(t, j=j, ti=ti, n=n):
                        if n - 1 in vev_tok:
                            t.wait_ge(sAC.h, vev_tok[n - 1])
                        for dc in range(NDC):
                            mm = t.matmul(
                                kps[1][:],
                                wv_sb[:, dc * 512 + j * 128: dc * 512 + (j + 1) * 128],
                                xn_fm[ti % 2][:, dc * 512:(dc + 1) * 512],
                                start=(dc == 0), stop=(dc == NDC - 1))
                        sTE.inc(mm)
                        xn_used[ti] = sTE.v
                    blk.tensor(tv)
                    tok_vmm = sTE.v

                    def sv(sc, j=j, n=n, fb=fb):
                        sc.wait_ge(sTE.h, tok_vmm)
                        if n - 2 in vtp_tok:
                            sc.wait_ge(sTE.h, vtp_tok[n - 2])
                        vev_tok[n] = sAC.inc(sc.activation(
                            out=vf16[fb][:], in_=kps[1][:], func=AF.Identity,
                            bias=bv_sb[:, j:j + 1], scale=1.0))
                    blk.scalar(sv)

                    def tv2(t, j=j, n=n, fb=fb):
                        t.wait_ge(sAC.h, vev_tok[n])
                        if n - 1 in vts_tok:
                            t.wait_ge(sVE.h, vts_tok[n - 1])
                        for q in range(4):
                            mm = t.transpose(tpb[:, q * 128:(q + 1) * 128],
                                             vf16[fb][:, q * 128:(q + 1) * 128],
                                             identb_sb[:])
                        vtp_tok[n] = sTE.inc(mm)
                    blk.tensor(tv2)

                    def vv(v, j=j, n=n, fb=fb):
                        v.wait_ge(sTE.h, vtp_tok[n])
                        if n - 2 in vst_tok:
                            v.wait_ge(sDST.h, vst_tok[n - 2])
                        v.tensor_copy(out=vt_stage[fb][:], in_=tpb[:, 0:512])
                        vts_tok[n] = sVE.inc(v.drain())
                    blk.vector(vv)

                    def syv(sy, j=j, ti=ti, n=n, fb=fb):
                        sy.wait_ge(sVE.h, vts_tok[n])
                        vst_tok[n] = sDST.dinc(sy.dma_start(
                            out=vtscr[j][:, ti * 512:(ti + 1) * 512],
                            in_=vt_stage[fb][:]))
                    blk.sync(syv)

            def sc_c(sc):
                sc.wait_ge(sVE.h, sVE.v)
                sAC.inc(sc.sqrt(out=ctmp[:], in_=maxq2[:]))
                sAC.inc(sc.sqrt(out=ctmp2[:], in_=maxk2[:]))
            blk.scalar(sc_c)
            tok_cs = sAC.v

            def v_c(v):
                v.wait_ge(sAC.h, tok_cs)
                v.tensor_tensor(out=ctmp[:], in0=ctmp[:], in1=ctmp2[:], op=ALU.mult)
                v.drain()
                sVE.inc(v.tensor_scalar(out=ctmp[:], in0=ctmp[:], scalar1=-1.0,
                                        scalar2=None, op0=ALU.mult))
                v.drain()
            blk.vector(v_c)
            tok_cn = sVE.v

            def t_c(t):
                t.wait_ge(sVE.h, tok_cn)
                sTE.inc(t.matmul(cb_ps[:], onesrowf_sb[0:1, :], ctmp[:],
                                 start=True, stop=True))
            blk.tensor(t_c)
            tok_cb = sTE.v

            def v_c2(v):
                v.wait_ge(sTE.h, tok_cb)
                sVE.inc(v.tensor_copy(out=cbias[:], in_=cb_ps[:]))
            blk.vector(v_c2)

            def syf(sy):
                sy.wait_ge(sDST.h, sDST.v)
                sy.wait_ge(sVE.h, sVE.v)
            blk.sync(syf)

    with ExitStack() as ps:
        def sb(name, shape, dt=F32):
            return ps.enter_context(nc.sbuf_tensor(name, shape, dt))
        wb = [sb(f"qwb{i}", [128, 2048], BF16) for i in range(2)]
        aw = sb("qaw", [128, 2048], BF16)
        sgn = sb("qsgn", [128, 2048], BF16)
        sc16 = sb("qsc16", [64, 2048], BF16)
        thr = sb("qthr", [128, 2048])
        scbc = sb("qscbc", [128, 2048], BF16)
        mgt = sb("qmgt", [128, 2048], BF16)
        qv = sb("qqv", [128, 2048], BF16)
        wqv = sb("qwqv", [128, 2048], BF16)
        d1t = sb("qd1", [128, 2048], BF16)
        rr = [sb(f"qrr{i}", [128, 2048], BF16) for i in range(2)]

        scl_ps = ps.enter_context(nc.psum_tensor("qscl_ps", [2, 2048], F32))
        bc_ps = ps.enter_context(nc.psum_tensor("qbc_ps", [128, 2048], F32))

        tiles = []
        for dc in range(NDC):
            tiles.append(("gu", dc, 0))
        for hcl in range(8):
            tiles.append(("p", hcl, 0))
        NT = len(tiles)

        tok_ld = [None] * NT
        tok_sums = [None] * NT
        tok_scev = [None] * NT
        tok_bc = [None] * NT
        tok_bcev = [None] * NT
        tok_thr = [None] * NT
        tok_sgn = [None] * NT
        tok_aw = [None] * NT
        tok_q = [None] * NT
        tok_wq = [None] * NT
        tok_r = [None] * NT
        tok_st = [None] * NT

        with nc.Block("q") as blk:
            for ti2, (kind, a, b_) in enumerate(tiles):
                par = ti2 % 2

                def gl(g, ti2=ti2, kind=kind, a=a, b_=b_, par=par):
                    if kind == "gu":
                        src = wgut[a * 128:(a + 1) * 128, :]
                    else:
                        src = wpt[a * 128:(a + 1) * 128, :]
                    if ti2 >= 2:
                        g.wait_ge(sVE.h, tok_r[ti2 - 2])
                    tok_ld[ti2] = sDWT.dinc(g.dma_start(out=wb[par][:], in_=src))
                blk.gpsimd(gl)

                def s1(sc, ti2=ti2, par=par):
                    sc.wait_ge(sDWT.h, tok_ld[ti2])
                    if ti2 > 0:
                        sc.wait_ge(sTE.h, tok_sums[ti2 - 1])
                        sc.wait_ge(sVE.h, tok_q[ti2 - 1])
                    tok_aw[ti2] = sAC.inc(sc.activation(out=aw[:], in_=wb[par][:],
                                                        func=AF.Abs))
                    tok_sgn[ti2] = sAC.inc(sc.sign(out=sgn[:], in_=wb[par][:]))
                blk.scalar(s1)

                def t1(t, ti2=ti2):
                    t.wait_ge(sAC.h, tok_aw[ti2])
                    if ti2 > 0:
                        t.wait_ge(sVE.h, tok_scev[ti2 - 1])
                    for i in range(4):
                        mm = t.matmul(scl_ps[:, i * 512:(i + 1) * 512], ind2_sb[:],
                                      aw[:, i * 512:(i + 1) * 512],
                                      start=True, stop=True)
                    tok_sums[ti2] = sTE.inc(mm)
                blk.tensor(t1)

                def v1(v, ti2=ti2):
                    v.wait_ge(sTE.h, tok_sums[ti2])
                    if ti2 > 0:
                        v.wait_ge(sTE.h, tok_bc[ti2 - 1])
                    v.tensor_copy(out=sc16[0:2, :], in_=scl_ps[:])
                    tok_scev[ti2] = sVE.inc(v.tensor_scalar(
                        out=sc16[0:2, :], in0=sc16[0:2, :], scalar1=1e-8,
                        scalar2=None, op0=ALU.max))
                blk.vector(v1)

                def t2(t, ti2=ti2):
                    t.wait_ge(sVE.h, tok_scev[ti2])
                    if ti2 > 0:
                        t.wait_ge(sAC.h, tok_bcev[ti2 - 1])
                    for i in range(4):
                        mm = t.matmul(bc_ps[:, i * 512:(i + 1) * 512], e2_sb[:],
                                      sc16[0:2, i * 512:(i + 1) * 512],
                                      start=True, stop=True)
                    tok_bc[ti2] = sTE.inc(mm)
                blk.tensor(t2)

                def s2(sc, ti2=ti2):
                    sc.wait_ge(sTE.h, tok_bc[ti2])
                    if ti2 > 0:
                        sc.wait_ge(sVE.h, tok_wq[ti2 - 1])
                    tok_thr[ti2] = sAC.inc(sc.activation(out=thr[:], in_=bc_ps[:],
                                                         func=AF.Copy, scale=QTHRESH))
                    tok_bcev[ti2] = sAC.inc(sc.activation(out=scbc[:], in_=bc_ps[:],
                                                          func=AF.Copy))
                blk.scalar(s2)

                def v2(v, ti2=ti2, par=par):
                    v.wait_ge(sAC.h, tok_bcev[ti2])
                    v.wait_ge(sAC.h, tok_sgn[ti2])
                    if ti2 >= 2:
                        v.wait_ge(sDST.h, tok_st[ti2 - 2])
                    v.tensor_tensor(out=mgt[:], in0=aw[:], in1=thr[:], op=ALU.is_gt)
                    tok_q[ti2] = sVE.inc(v.tensor_tensor(out=qv[:], in0=mgt[:],
                                                         in1=sgn[:], op=ALU.mult))
                    tok_wq[ti2] = sVE.inc(v.tensor_tensor(out=wqv[:], in0=qv[:],
                                                          in1=scbc[:], op=ALU.mult))
                    v.tensor_tensor(out=d1t[:], in0=wqv[:], in1=wb[par][:],
                                    op=ALU.subtract)
                    tok_r[ti2] = sVE.inc(v.tensor_tensor(out=rr[par][:], in0=wb[par][:],
                                                         in1=d1t[:], op=ALU.add))
                blk.vector(v2)

                def sy1(sy, ti2=ti2, kind=kind, a=a, b_=b_, par=par):
                    sy.wait_ge(sVE.h, tok_r[ti2])
                    src = rr[par][:].rearrange("p (i f) -> p i f", i=16)
                    if kind == "gu":
                        dst = wguq_sh[:, :, a, :].transpose([1, 0, 2])
                    else:
                        dst = wpq_sh[:, :, a, :].transpose([1, 0, 2])
                    tok_st[ti2] = sDST.dinc(sy.dma_start(out=dst, in_=src))
                blk.sync(sy1)

            def gcc(g):
                g.wait_ge(sDST.h, sDST.v)
                g.collective_compute(
                    "AllGather", ALU.bypass,
                    replica_groups=[list(range(8))],
                    ins=[wguq_sh.rearrange("a p c f -> a (p c f)").opt()],
                    outs=[wguq.rearrange("a p c f -> a (p c f)").opt()],
                ).then_inc(sems["gp"].h, 1)
                sems["gp"].v += 1
                g.collective_compute(
                    "AllGather", ALU.bypass,
                    replica_groups=[list(range(8))],
                    ins=[wpq_sh.rearrange("d p c f -> (d p c f)").unsqueeze(0).opt()],
                    outs=[wpq.rearrange("s d p c f -> s (d p c f)").opt()],
                ).then_inc(sems["gp"].h, 1)
                sems["gp"].v += 1
                g.wait_ge(sems["gp"].h, sems["gp"].v)
            blk.gpsimd(gcc)

            def syf(sy):
                sy.wait_ge(sDST.h, sDST.v)
                sy.wait_ge(sems["gp"].h, sems["gp"].v)
            blk.sync(syf)

    with ExitStack() as ps:
        def sb(name, shape, dt=F32):
            return ps.enter_context(nc.sbuf_tensor(name, shape, dt))
        kbuf = [sb(f"bk{i}", [128, T], BF16) for i in range(2)]
        vtbuf = [sb(f"bvt{i}", [128, T], BF16) for i in range(2)]
        qbuf = sb("bq", [128, HQ * 512], BF16)
        mask_sb = sb("bmask", [128, 16 * 512], BF16)
        pbuf = sb("bp", [128, 2048], BF16)
        o_all = sb("boall", [128, HQ * 512], BF16)
        den2 = sb("bden2", [64, 8 * 512])
        wo_sb = [sb(f"bwo{i}", [128, NDC * 512], BF16) for i in range(2)]
        sf_ev = sb("bsfev", [128, 512])
        s_tm = sb("bstm", [128, 4 * C])

        qk_ps = ps.enter_context(nc.psum_tensor("bqk", [128, 2048], F32))
        pv_ps = [ps.enter_context(nc.psum_tensor(f"bpv{i}", [128, 512], F32))
                 for i in range(2)]
        den_ps = ps.enter_context(nc.psum_tensor("bdenp", [128, 512], F32))
        tpf_ps = den_ps

        sK = [sems["dk0"], sems["dk1"]]
        sDQ = sems["dq"]
        it_list = [(s, j) for s in range(NSLOT) for j in range(HKV)]
        kv_tok = {}
        q_tok = {}
        it_done = {}

        with nc.Block("b") as blk:
            def pro(sy):
                E0 = 2048
                sK[0].dinc(sy.dma_start(out=kbuf[0][:, 0:E0], in_=kscr[0][:, 0:E0]))
                sK[0].dinc(sy.dma_start(out=vtbuf[0][:, 0:E0], in_=vtscr[0][:, 0:E0]))
                sK[1].dinc(sy.dma_start(out=kbuf[1][:, 0:E0], in_=kscr[1][:, 0:E0]))
                sK[1].dinc(sy.dma_start(out=vtbuf[1][:, 0:E0], in_=vtscr[1][:, 0:E0]))
                sDQ.dinc(sy.dma_start(out=qbuf[:],
                                      in_=qscr[0].rearrange("p g f -> p (g f)")))
                sDQ.dinc(sy.dma_start(
                    out=mask_sb[:].rearrange("p (i f) -> p i f", i=16),
                    in_=masks[0].transpose([1, 0, 2])))
            blk.sync(pro)
            kv_tok[0] = (sK[0], sK[0].v)
            kv_tok[1] = (sK[1], sK[1].v)
            q_tok[0] = sDQ.v

            pair_done_ve = [None]
            wo_done_te = [None]
            prev_s_store = [None]
            norm_done_ve = [None]

            for n, (s, j) in enumerate(it_list):
                if j == 0:
                    soft_barrier(blk)
                E = 512 * (4 * s + 4)
                nsub = E // 128
                par = n % 2
                masked = list(range(max(0, nsub - 16), nsub))

                for p in range(2):
                    exp_base = sAC.v
                    pvd_base = sTE.v
                    msk_base = sVE.v
                    msk_tok = {}
                    mi = 0
                    for i in masked:
                        msk_tok[i] = msk_base + mi + 1
                        mi += 1

                    def tok_qk(i, pvd_base=pvd_base):
                        return pvd_base + 2 * i + 1

                    def tok_pvd(i, pvd_base=pvd_base):
                        return pvd_base + 2 * i + 2

                    def tok_exp(i, exp_base=exp_base):
                        return exp_base + i + 1

                    def tb(t, s=s, j=j, p=p, E=E, nsub=nsub, par=par, n=n,
                           exp_base=exp_base, msk_tok=msk_tok):
                        ksem, ktok = kv_tok[n]
                        t.wait_ge(ksem.h, ktok)
                        if p == 0:
                            t.wait_ge(sDQ.h, q_tok[s])
                        if pair_done_ve[0] is not None:
                            t.wait_ge(sVE.h, pair_done_ve[0])
                        for i in range(nsub):
                            ip = i % 2
                            if i >= 2:
                                t.wait_ge(sAC.h, tok_exp(i - 2))
                            t.ldweights(kbuf[par][:, i * 128:(i + 1) * 128])
                            for gg in range(2):
                                h = (2 * p + gg) * 4 + j
                                mm = t.matmul(
                                    qk_ps[:, ip * 1024 + gg * 512:
                                          ip * 1024 + (gg + 1) * 512],
                                    kbuf[par][:, i * 128:(i + 1) * 128],
                                    qbuf[:, h * 512:(h + 1) * 512],
                                    start=True, stop=True)
                            sTE.inc(mm)
                            if i in msk_tok:
                                t.wait_ge(sVE.h, msk_tok[i])
                            else:
                                t.wait_ge(sAC.h, tok_exp(i))
                            t.ldweights(vtbuf[par][:, i * 128:(i + 1) * 128])
                            for gg in range(2):
                                t.matmul(pv_ps[gg][:],
                                         vtbuf[par][:, i * 128:(i + 1) * 128],
                                         pbuf[:, ip * 1024 + gg * 512:
                                              ip * 1024 + (gg + 1) * 512],
                                         start=(i == 0), stop=(i == nsub - 1))
                            t.ldweights(ones128b_sb[:], tile_position=(0, 0))
                            t.ldweights(ones128b_sb[:], tile_position=(0, 32))
                            for gg in range(2):
                                mm = t.matmul(
                                    den_ps[gg * 32:gg * 32 + 1, :],
                                    ones128b_sb[:],
                                    pbuf[:, ip * 1024 + gg * 512:
                                         ip * 1024 + (gg + 1) * 512],
                                    start=(i == 0), stop=(i == nsub - 1),
                                    tile_position=(0, gg * 32))
                            sTE.inc(mm)
                    blk.tensor(tb)
                    assert sTE.v == pvd_base + 2 * nsub

                    def sb_(sc, nsub=nsub, exp_base=exp_base, pvd_base=pvd_base):
                        for i in range(nsub):
                            ip = i % 2
                            sc.wait_ge(sTE.h, tok_qk(i))
                            if i >= 2:
                                sc.wait_ge(sTE.h, tok_pvd(i - 2))
                            sAC.inc(sc.activation(
                                out=pbuf[:, ip * 1024:(ip + 1) * 1024],
                                in_=qk_ps[:, ip * 1024:(ip + 1) * 1024],
                                func=AF.Exp, bias=cbias[:], scale=1.0))
                    blk.scalar(sb_)
                    assert sAC.v == exp_base + nsub

                    def vb(v, s=s, nsub=nsub, exp_base=exp_base, masked=masked):
                        v.wait_ge(sDQ.h, q_tok[s])
                        for i in masked:
                            ip = i % 2
                            idx = i - (nsub - 16)
                            v.wait_ge(sAC.h, tok_exp(i))
                            v.tensor_tensor(
                                out=pbuf[:, ip * 1024:ip * 1024 + 512],
                                in0=pbuf[:, ip * 1024:ip * 1024 + 512],
                                in1=mask_sb[:, idx * 512:(idx + 1) * 512],
                                op=ALU.mult)
                            sVE.inc(v.tensor_tensor(
                                out=pbuf[:, ip * 1024 + 512:ip * 1024 + 1024],
                                in0=pbuf[:, ip * 1024 + 512:ip * 1024 + 1024],
                                in1=mask_sb[:, idx * 512:(idx + 1) * 512],
                                op=ALU.mult))
                    blk.vector(vb)
                    assert sVE.v == msk_base + len(masked)

                    tok_all_pvd = sTE.v

                    def vep(v, s=s, j=j, p=p, tok_all_pvd=tok_all_pvd):
                        v.wait_ge(sTE.h, tok_all_pvd)
                        if wo_done_te[0] is not None:
                            v.wait_ge(sTE.h, wo_done_te[0])
                        col = (p * 4 + j) * 512
                        v.tensor_copy(out=den2[0:1, col:col + 512],
                                      in_=den_ps[0:1, :])
                        v.tensor_copy(out=den2[32:33, col:col + 512],
                                      in_=den_ps[32:33, :])
                        for gg in range(2):
                            h = (2 * p + gg) * 4 + j
                            mm = v.tensor_copy(out=o_all[:, h * 512:(h + 1) * 512],
                                               in_=pv_ps[gg][:])
                        sVE.inc(mm)
                    blk.vector(vep)
                    pair_done_ve[0] = sVE.v

                it_done[n] = sTE.v

                if n + 2 < len(it_list):
                    s2_, j2_ = it_list[n + 2]
                    E2 = 512 * (4 * s2_ + 4)

                    def syk(sy, s2_=s2_, j2_=j2_, E2=E2, par=par, n=n):
                        sy.wait_ge(sTE.h, it_done[n])
                        sK[par].dinc(sy.dma_start(out=kbuf[par][:, 0:E2],
                                                  in_=kscr[j2_][:, 0:E2]))
                        sK[par].dinc(sy.dma_start(out=vtbuf[par][:, 0:E2],
                                                  in_=vtscr[j2_][:, 0:E2]))
                    blk.sync(syk)
                    kv_tok[n + 2] = (sK[par], sK[par].v)

                if j == HKV - 1:
                    tok_pairs = sVE.v

                    def vnr(v, s=s, tok_pairs=tok_pairs):
                        v.wait_ge(sVE.h, tok_pairs)
                        v.reciprocal(out=den2[0:1, :], in_=den2[0:1, :])
                        v.drain()
                        v.reciprocal(out=den2[32:33, :], in_=den2[32:33, :])
                        sVE.inc(v.drain())
                    blk.vector(vnr)
                    tok_rd = sVE.v

                    nrm_mm_tok = {}
                    nrm_mu_tok = {}
                    for rnd in range(4):
                        def tnr(t, s=s, rnd=rnd):
                            t.wait_ge(sVE.h, tok_rd)
                            if rnd == 0:
                                pass
                            else:
                                t.wait_ge(sVE.h, nrm_mu_tok[rnd - 1])
                            for q4 in range(4):
                                h = rnd * 4 + q4
                                gg = h // 8
                                ggr = (h // 4) % 2
                                pcol = ((h // 8) * 4 + (h % 4)) * 512
                                mm = t.matmul(qk_ps[:, q4 * 512:(q4 + 1) * 512],
                                              onesrowf_sb[ggr * 32:ggr * 32 + 1, :],
                                              den2[ggr * 32:ggr * 32 + 1,
                                                   pcol:pcol + 512],
                                              start=True, stop=True)
                            nrm_mm_tok[rnd] = sTE.inc(mm)
                        blk.tensor(tnr)

                        def vnr2(v, s=s, rnd=rnd):
                            v.wait_ge(sTE.h, nrm_mm_tok[rnd])
                            v.tensor_tensor(
                                out=o_all[:, rnd * 2048:(rnd + 1) * 2048],
                                in0=o_all[:, rnd * 2048:(rnd + 1) * 2048],
                                in1=qk_ps[:], op=ALU.mult)
                            nrm_mu_tok[rnd] = sVE.inc(v.drain())
                        blk.vector(vnr2)
                    tok_oall = sVE.v
                    norm_done_ve[0] = tok_oall
                    if DEBUG:
                        def syo(sy, s=s):
                            sy.wait_ge(sVE.h, tok_oall)
                            sDST.dinc(sy.dma_start(out=oscr[s], in_=o_all[:]))
                        blk.sync(syo)

                    wo_tok = {}
                    wo_used = {}
                    sev_tok = {}
                    tp_tok = {}
                    tr_tok = {}

                    def gw0(g, s=s):
                        if wo_done_te[0] is not None:
                            g.wait_ge(sTE.h, wo_done_te[0])
                        wov = wo.rearrange("(dc p) m -> p dc m", p=128)
                        for blki in range(2):
                            wo_tok[blki] = sDWT.dinc(g.dma_start(
                                out=wo_sb[blki][:].rearrange(
                                    "p (dc f) -> p dc f", dc=NDC),
                                in_=wov[:, :, blki * 512:(blki + 1) * 512]))
                    blk.gpsimd(gw0)

                    for blki in range(4):
                        bpar = blki % 2
                        if blki >= 2:
                            def gw(g, blki=blki, bpar=bpar, s=s):
                                g.wait_ge(sTE.h, wo_used[blki - 2])
                                wov = wo.rearrange("(dc p) m -> p dc m", p=128)
                                wo_tok[blki] = sDWT.dinc(g.dma_start(
                                    out=wo_sb[bpar][:].rearrange(
                                        "p (dc f) -> p dc f", dc=NDC),
                                    in_=wov[:, :, blki * 512:(blki + 1) * 512]))
                            blk.gpsimd(gw)

                        ppieces = [qk_ps[:, 0:512], qk_ps[:, 512:1024],
                                   qk_ps[:, 1024:1536], qk_ps[:, 1536:2048]]
                        mm_base = sTE.v

                        def tw(t, s=s, blki=blki, bpar=bpar, tok_oall=tok_oall):
                            t.wait_ge(sDWT.h, wo_tok[blki])
                            t.wait_ge(sVE.h, tok_oall)
                            for dl in range(4):
                                if blki > 0:
                                    t.wait_ge(sAC.h, sev_tok[(blki - 1, dl)])
                                for dc in range(NDC):
                                    mm = t.matmul(
                                        ppieces[dl],
                                        wo_sb[bpar][:, dc * 512 + dl * 128:
                                                    dc * 512 + (dl + 1) * 128],
                                        o_all[:, dc * 512:(dc + 1) * 512],
                                        start=(dc == 0), stop=(dc == NDC - 1))
                                sTE.inc(mm)
                            wo_used[blki] = sTE.v
                        blk.tensor(tw)
                        assert sTE.v == mm_base + 4

                        tp_base = sTE.v
                        for dl in range(4):
                            tp_tok[(blki, dl)] = tp_base + dl + 1
                        ev_base = sAC.v
                        for dl in range(4):
                            sev_tok[(blki, dl)] = ev_base + dl + 1
                        trv_base = sVE.v
                        for dl in range(4):
                            tr_tok[(blki, dl)] = trv_base + dl + 1

                        def sw(sc, s=s, blki=blki, mm_base=mm_base,
                               ppieces=ppieces):
                            for dl in range(4):
                                dout = blki * 4 + dl
                                sc.wait_ge(sTE.h, mm_base + dl + 1)
                                if (blki, dl) != (0, 0):
                                    prev = (blki, dl - 1) if dl > 0 else (blki - 1, 3)
                                    sc.wait_ge(sTE.h, tp_tok[prev])
                                sAC.inc(sc.activation(
                                    out=sf_ev[:], in_=ppieces[dl], func=AF.Identity,
                                    bias=bo_sb[:, dout:dout + 1], scale=1.0))
                        blk.scalar(sw)
                        assert sAC.v == ev_base + 4

                        def tw2(t, s=s, blki=blki):
                            for dl in range(4):
                                t.wait_ge(sAC.h, sev_tok[(blki, dl)])
                                if (blki, dl) != (0, 0):
                                    prev = (blki, dl - 1) if dl > 0 else (blki - 1, 3)
                                    t.wait_ge(sVE.h, tr_tok[prev])
                                else:
                                    t.wait_ge(sVE.h, pair_done_ve[0])
                                for u in range(4):
                                    mm = t.transpose(tpf_ps[:, u * 128:(u + 1) * 128],
                                                     sf_ev[:, u * 128:(u + 1) * 128],
                                                     identf_sb[:])
                                sTE.inc(mm)
                        blk.tensor(tw2)
                        assert sTE.v == tp_base + 4

                        def vw(v, s=s, blki=blki):
                            stv = s_tm[:].rearrange("p (u f) -> p u f", u=4)
                            for dl in range(4):
                                dout = blki * 4 + dl
                                v.wait_ge(sTE.h, tp_tok[(blki, dl)])
                                if blki == 0 and dl == 0 and prev_s_store[0] is not None:
                                    v.wait_ge(sDST.h, prev_s_store[0])
                                dst = stv[:, :, dout * 128:(dout + 1) * 128]
                                if dl == 3:
                                    v.tensor_copy(
                                        out=dst, in_=tpf_ps[:].rearrange(
                                            "p (u f) -> p u f", u=4))
                                    sVE.inc(v.drain())
                                else:
                                    sVE.inc(v.tensor_copy(
                                        out=dst, in_=tpf_ps[:].rearrange(
                                            "p (u f) -> p u f", u=4)))
                        blk.vector(vw)
                        assert sVE.v == trv_base + 4

                    def sys(sy, s=s):
                        sy.wait_ge(sVE.h, tr_tok[(3, 3)])
                        for u in range(4):
                            sDST.dinc(sy.dma_start(out=sscr[s][u],
                                                   in_=s_tm[:, u * C:(u + 1) * C]))
                    blk.sync(sys)
                    prev_s_store[0] = sDST.v
                    wo_done_te[0] = wo_used[3]

                    if s + 1 < NSLOT:
                        def syq(sy, s=s):
                            sy.wait_ge(sTE.h, it_done[n])
                            sDQ.dinc(sy.dma_start(
                                out=qbuf[:],
                                in_=qscr[s + 1].rearrange("p g f -> p (g f)")))
                            sDQ.dinc(sy.dma_start(
                                out=mask_sb[:].rearrange("p (i f) -> p i f", i=16),
                                in_=masks[s + 1].transpose([1, 0, 2])))
                        blk.sync(syq)
                        q_tok[s + 1] = sDQ.v

            def syf(sy):
                sy.wait_ge(sDST.h, sDST.v)
            blk.sync(syf)

    with ExitStack() as ps:
        def sb(name, shape, dt=F32):
            return ps.enter_context(nc.sbuf_tensor(name, shape, dt))
        xnb = sb("cxnb", [128, NDC * 512], BF16)
        wg_g = [sb(f"cwg{i}", [128, 2048], BF16) for i in range(2)]
        wg_u = [sb(f"cwu{i}", [128, 2048], BF16) for i in range(2)]
        sg = [sb(f"csg{i}", [128, 512]) for i in range(2)]
        hsq = [sb(f"chsq{i}", [128, 512], BF16) for i in range(2)]
        h_all = sb("chall", [128, NHC * 512], BF16)
        wp_sb = [sb(f"cwp{i}", [128, 32 * 128], BF16) for i in range(2)]
        rs1 = sb("crs1", [1, 512])
        rs2 = sb("crs2", [1, 512])
        rdbc = sb("crdbc", [128, 512])
        tff = sb("ctff", [128, 512])
        gfm = [sb(f"cgfm{i}", [128, 512]) for i in range(2)]
        g_tm = sb("cgtm", [128, 4 * C])
        xs = sb("cxs", [128, C])
        ss = sb("css", [128, C])
        ys = [sb(f"cys{i}", [128, C]) for i in range(2)]

        gps = [ps.enter_context(nc.psum_tensor(f"cgps{i}", [128, 512], F32))
               for i in range(2)]
        ups = [ps.enter_context(nc.psum_tensor(f"cups{i}", [128, 512], F32))
               for i in range(2)]
        h2_ps = ps.enter_context(nc.psum_tensor("ch2", [1, 512], F32))
        pps = [ps.enter_context(nc.psum_tensor(f"cpps{i}", [128, 512], F32))
               for i in range(2)]
        tpf2_ps = ps.enter_context(nc.psum_tensor("ctpf2", [128, 512], F32))

        sDG, sDP, sDXS, sDY = sems["dg"], sems["dp"], sems["dxs"], sems["dy"]

        with nc.Block("c") as blk:
            prev_slot_gu_te = [None]
            prev_slot_pj_te = [None]
            prev_slot_cb_ve = [None]
            prev_yst = [None, None]

            for s in range(NSLOT):
                soft_barrier(blk)

                def syxn(sy, s=s):
                    if prev_slot_gu_te[0] is not None:
                        sy.wait_ge(sTE.h, prev_slot_gu_te[0])
                    sDQ_ = sems["dq"]
                    sDQ_.dinc(sy.dma_start(
                        out=xnb[:], in_=xnscr[s].rearrange("p dc f -> p (dc f)")))
                blk.sync(syxn)
                tok_xnb = sems["dq"].v

                gld_tok = {}
                gmm_tok = {}
                umm_tok = {}
                silu_tok = {}
                h_tok = {}
                hsq_tok = {}
                h2_tok = {}

                def syg0(sy, s=s):
                    if prev_slot_gu_te[0] is not None:
                        sy.wait_ge(sTE.h, prev_slot_gu_te[0])
                    guv = wguq.rearrange("fc p dc f -> fc p (dc f)")
                    sDG.dinc(sy.dma_start(out=wg_g[0][:], in_=guv[0]))
                    gld_tok[0] = sDG.dinc(sy.dma_start(out=wg_u[0][:], in_=guv[64]))
                blk.sync(syg0)

                for fc in range(NHC):
                    fpar = fc % 2
                    if fc + 1 < NHC:
                        def syg(sy, s=s, fc=fc):
                            if fc >= 1:
                                sy.wait_ge(sTE.h, umm_tok[fc - 1])
                            guv = wguq.rearrange("fc p dc f -> fc p (dc f)")
                            sDG.dinc(sy.dma_start(out=wg_g[(fc + 1) % 2][:],
                                                  in_=guv[fc + 1]))
                            gld_tok[fc + 1] = sDG.dinc(sy.dma_start(
                                out=wg_u[(fc + 1) % 2][:], in_=guv[64 + fc + 1]))
                        blk.sync(syg)

                    def tg(t, s=s, fc=fc, fpar=fpar):
                        t.wait_ge(sDG.h, gld_tok[fc])
                        t.wait_ge(sems["dq"].h, tok_xnb)
                        if prev_slot_pj_te[0] is not None and fc == 0:
                            pass
                        if fc >= 2:
                            t.wait_ge(sAC.h, silu_tok[fc - 2])
                            t.wait_ge(sVE.h, h_tok[fc - 2])
                        for dc in range(NDC):
                            mm = t.matmul(gps[fpar][:],
                                          wg_g[fpar][:, dc * 128:(dc + 1) * 128],
                                          xnb[:, dc * 512:(dc + 1) * 512],
                                          start=(dc == 0), stop=(dc == NDC - 1))
                        gmm_tok[fc] = sTE.inc(mm)
                        for dc in range(NDC):
                            mm = t.matmul(ups[fpar][:],
                                          wg_u[fpar][:, dc * 128:(dc + 1) * 128],
                                          xnb[:, dc * 512:(dc + 1) * 512],
                                          start=(dc == 0), stop=(dc == NDC - 1))
                        umm_tok[fc] = sTE.inc(mm)
                        if fc >= 2:
                            t.wait_ge(sVE.h, hsq_tok[fc - 2])
                            h2_tok[fc - 2] = sTE.inc(t.matmul(
                                h2_ps[:], ones128b_sb[:], hsq[(fc - 2) % 2][:],
                                start=(fc - 2 == 0), stop=False))
                    blk.tensor(tg)

                    def sg_(sc, s=s, fc=fc, fpar=fpar):
                        sc.wait_ge(sTE.h, gmm_tok[fc])
                        if fc >= 2:
                            sc.wait_ge(sVE.h, h_tok[fc - 2])
                        silu_tok[fc] = sAC.inc(sc.activation(
                            out=sg[fpar][:], in_=gps[fpar][:], func=AF.Silu))
                    blk.scalar(sg_)

                    def vg(v, s=s, fc=fc, fpar=fpar):
                        v.wait_ge(sAC.h, silu_tok[fc])
                        v.wait_ge(sTE.h, umm_tok[fc])
                        if prev_slot_pj_te[0] is not None:
                            v.wait_ge(sTE.h, prev_slot_pj_te[0])
                        h_tok[fc] = sVE.inc(v.tensor_tensor(
                            out=h_all[:, fc * 512:(fc + 1) * 512],
                            in0=sg[fpar][:], in1=ups[fpar][:], op=ALU.mult))
                        if fc >= 1:
                            if fc >= 3:
                                v.wait_ge(sTE.h, h2_tok[fc - 3])
                            hsl = h_all[:, (fc - 1) * 512:fc * 512]
                            hsq_tok[fc - 1] = sVE.inc(v.tensor_tensor(
                                out=hsq[(fc - 1) % 2][:], in0=hsl, in1=hsl,
                                op=ALU.mult))
                    blk.vector(vg)

                def sgt(v, s=s):
                    v.wait_ge(sTE.h, h2_tok[NHC - 3])
                    hsl = h_all[:, (NHC - 1) * 512:NHC * 512]
                    hsq_tok[NHC - 1] = sVE.inc(v.tensor_tensor(
                        out=hsq[(NHC - 1) % 2][:], in0=hsl, in1=hsl, op=ALU.mult))
                blk.vector(sgt)

                def tgt(t, s=s):
                    t.wait_ge(sVE.h, hsq_tok[NHC - 2])
                    h2_tok[NHC - 2] = sTE.inc(t.matmul(
                        h2_ps[:], ones128b_sb[:], hsq[(NHC - 2) % 2][:],
                        start=False, stop=False))
                    t.wait_ge(sVE.h, hsq_tok[NHC - 1])
                    h2_tok[NHC - 1] = sTE.inc(t.matmul(
                        h2_ps[:], ones128b_sb[:], hsq[(NHC - 1) % 2][:],
                        start=False, stop=True))
                blk.tensor(tgt)
                tok_h2 = sTE.v

                def vr(v, s=s):
                    v.wait_ge(sTE.h, tok_h2)
                    sVE.inc(v.tensor_scalar(out=rs1[:], in0=h2_ps[:],
                                            scalar1=1.0 / HF, scalar2=RMS_EPS,
                                            op0=ALU.mult, op1=ALU.add))
                    v.drain()
                blk.vector(vr)
                tok_rs = sVE.v

                def sr(sc, s=s):
                    sc.wait_ge(sVE.h, tok_rs)
                    sAC.inc(sc.sqrt(out=rs2[:], in_=rs1[:]))
                blk.scalar(sr)
                tok_rsq = sAC.v

                def vr2(v, s=s):
                    v.wait_ge(sAC.h, tok_rsq)
                    sVE.inc(v.reciprocal(out=rs1[:], in_=rs2[:]))
                    v.drain()
                blk.vector(vr2)
                tok_rr = sVE.v

                def tr(t, s=s):
                    t.wait_ge(sVE.h, tok_rr)
                    sTE.inc(t.matmul(tpf2_ps[:], onesrowf_sb[0:1, :], rs1[:],
                                     start=True, stop=True))
                blk.tensor(tr)
                tok_rbc = sTE.v

                def vr3(v, s=s):
                    v.wait_ge(sTE.h, tok_rbc)
                    sVE.inc(v.tensor_copy(out=rdbc[:], in_=tpf2_ps[:]))
                blk.vector(vr3)
                tok_rdbc = sVE.v

                pld_tok = {}
                pmm_tok = {}
                half_mm_tok = {}
                tffv_tok = {}
                gsil_tok = {}
                gtp_tok = {}
                gtr_tok = {}
                def wp_src(do, hh):
                    return wpq[hh * 4:(hh + 1) * 4, do].transpose([1, 0, 2, 3])

                def wp_dst(buf):
                    return buf[:].rearrange("p (a b f) -> p a b f", a=4, b=8)

                def syp0(sy, s=s):
                    if prev_slot_pj_te[0] is not None:
                        sy.wait_ge(sTE.h, prev_slot_pj_te[0])
                    sDP.dinc(sy.dma_start(out=wp_dst(wp_sb[0]), in_=wp_src(0, 0)))
                    pld_tok[0] = sDP.v
                    sDP.dinc(sy.dma_start(out=wp_dst(wp_sb[1]), in_=wp_src(0, 1)))
                    pld_tok[1] = sDP.v
                blk.sync(syp0)

                for do in range(NDC):
                    ppar = do % 2
                    for hh in range(2):
                        un = 2 * do + hh
                        buf = wp_sb[un % 2]
                        if un + 2 < 2 * NDC:
                            do2, hh2 = (un + 2) // 2, (un + 2) % 2

                            def syp(sy, s=s, un=un, do2=do2, hh2=hh2):
                                sy.wait_ge(sTE.h, half_mm_tok[un])
                                sDP.dinc(sy.dma_start(
                                    out=wp_dst(wp_sb[un % 2]),
                                    in_=wp_src(do2, hh2)))
                                pld_tok[un + 2] = sDP.v

                        def tp(t, s=s, do=do, hh=hh, un=un, ppar=ppar, buf=buf):
                            t.wait_ge(sDP.h, pld_tok[un])
                            if hh == 0 and do >= 2:
                                t.wait_ge(sVE.h, tffv_tok[do - 2])
                            for hl in range(32):
                                hc = hh * 32 + hl
                                mm = t.matmul(pps[ppar][:],
                                              buf[:, hl * 128:(hl + 1) * 128],
                                              h_all[:, hc * 512:(hc + 1) * 512],
                                              start=(hc == 0), stop=(hc == NHC - 1))
                            half_mm_tok[un] = sTE.inc(mm)
                        blk.tensor(tp)
                        if un + 2 < 2 * NDC:
                            blk.sync(syp)
                    pmm_tok[do] = half_mm_tok[2 * do + 1]

                    def vp(v, s=s, do=do, ppar=ppar):
                        v.wait_ge(sTE.h, pmm_tok[do])
                        v.wait_ge(sVE.h, tok_rdbc)
                        if do >= 1:
                            v.wait_ge(sAC.h, gsil_tok[do - 1])
                        v.tensor_tensor(out=tff[:], in0=pps[ppar][:],
                                        in1=rdbc[:], op=ALU.mult)
                        tffv_tok[do] = sVE.inc(v.drain())
                    blk.vector(vp)

                    def sp(sc, s=s, do=do):
                        sc.wait_ge(sVE.h, tffv_tok[do])
                        if do >= 2:
                            sc.wait_ge(sTE.h, gtp_tok[do - 2])
                        gsil_tok[do] = sAC.inc(sc.activation(
                            out=gfm[do % 2][:], in_=tff[:], func=AF.Silu))
                    blk.scalar(sp)

                    def tp2(t, s=s, do=do):
                        t.wait_ge(sAC.h, gsil_tok[do])
                        if do >= 1:
                            t.wait_ge(sVE.h, gtr_tok[do - 1])
                        else:
                            t.wait_ge(sVE.h, tok_rdbc)
                        if prev_slot_cb_ve[0] is not None:
                            t.wait_ge(sVE.h, prev_slot_cb_ve[0])
                        for u in range(4):
                            mm = t.transpose(tpf2_ps[:, u * 128:(u + 1) * 128],
                                             gfm[do % 2][:, u * 128:(u + 1) * 128],
                                             identf_sb[:])
                        gtp_tok[do] = sTE.inc(mm)
                    blk.tensor(tp2)

                    def vp2(v, s=s, do=do):
                        gv = g_tm[:].rearrange("p (u f) -> p u f", u=4)
                        v.wait_ge(sTE.h, gtp_tok[do])
                        gtr_tok[do] = sVE.inc(v.tensor_copy(
                            out=gv[:, :, do * 128:(do + 1) * 128],
                            in_=tpf2_ps[:].rearrange("p (u f) -> p u f", u=4)))
                    blk.vector(vp2)

                prev_slot_gu_te[0] = umm_tok[NHC - 1]
                prev_slot_pj_te[0] = pmm_tok[NDC - 1]

                cmb_tok = {}
                yst_tok = {}
                for u in range(4):
                    def syc1(sy, s=s, u=u):
                        if u >= 1:
                            sy.wait_ge(sVE.h, cmb_tok[u - 1])
                        elif prev_slot_cb_ve[0] is not None:
                            sy.wait_ge(sVE.h, prev_slot_cb_ve[0])
                        sDXS.dinc(sy.dma_start(
                            out=xs[:], in_=xown[s * 512 + u * 128:
                                              s * 512 + (u + 1) * 128, :]))
                        sDXS.dinc(sy.dma_start(out=ss[:], in_=sscr[s][u]))
                    blk.sync(syc1)
                    tok_ld2 = sDXS.v

                    def vc(v, s=s, u=u, tok_ld2=tok_ld2):
                        v.wait_ge(sDXS.h, tok_ld2)
                        v.wait_ge(sVE.h, gtr_tok[NDC - 1])
                        if u >= 2:
                            v.wait_ge(sDY.h, yst_tok[u - 2])
                        elif prev_yst[u] is not None:
                            v.wait_ge(sDY.h, prev_yst[u])
                        v.tensor_tensor(out=ss[:], in0=ss[:],
                                        in1=g_tm[:, u * C:(u + 1) * C], op=ALU.mult)
                        v.drain()
                        v.tensor_tensor(out=ys[u % 2][:], in0=xs[:], in1=ss[:],
                                        op=ALU.add)
                        cmb_tok[u] = sVE.inc(v.drain())
                    blk.vector(vc)

                    def syy(sy, s=s, u=u):
                        sy.wait_ge(sVE.h, cmb_tok[u])
                        yst_tok[u] = sDY.dinc(sy.dma_start(
                            out=y[s * 512 + u * 128: s * 512 + (u + 1) * 128, :],
                            in_=ys[u % 2][:]))
                    blk.sync(syy)
                prev_slot_cb_ve[0] = cmb_tok[3]
                prev_yst = [yst_tok[2], yst_tok[3]]

            def syf(sy):
                sy.wait_ge(sDY.h, sems["dy"].v)
            blk.sync(syf)

    es.close()
    return nc


_NC_CACHE = None


def kernel(**inputs):
    global _NC_CACHE
    per_core = _host_prep(inputs)
    if _NC_CACHE is None:
        _NC_CACHE = build_nc()
    res = run_bass_kernel_spmd(_NC_CACHE, per_core, core_ids=list(range(8)))
    out = np.zeros((B, T, C), dtype=np.float32)
    for c in range(8):
        b, k = c // 4, c % 4
        yc = np.asarray(res.results[c]["y"])
        for s in range(NSLOT):
            r0 = (4 * s + k) * 512
            out[b, r0:r0 + 512] = yc[s * 512:(s + 1) * 512]
    return out


# version 12
# speedup vs baseline: 1.4841x; 1.4841x over previous
import numpy as np
import ml_dtypes
import concourse.bass as bass
import concourse.mybir as mybir
from concourse.bass_utils import run_bass_kernel_spmd
from contextlib import ExitStack

F32 = mybir.dt.float32
BF16 = mybir.dt.bfloat16
AF = mybir.ActivationFunctionType
ALU = mybir.AluOpType
AX = mybir.AxisListType

B, T, C = 2, 8192, 2048
HQ, HKV, HD = 16, 4, 128
HF = 8192
NDC = C // 128
NHC = HF // 128
NSLOT = 4
LN_EPS = 1e-5
RMS_EPS = float(np.finfo(np.float32).eps)
QTHRESH = 0.5009765625

DEBUG = False


class Sem:
    def __init__(self, nc, es, name):
        self.h = es.enter_context(nc.semaphore(name))
        self.v = 0

    def inc(self, inst, k=1):
        inst.then_inc(self.h, k)
        self.v += k
        return self.v

    def dinc(self, inst):
        return self.inc(inst, 16)



def _rope_perm():
    p = np.empty(128, dtype=np.int64)
    p[:64] = np.arange(64) * 2
    p[64:] = np.arange(64) * 2 + 1
    return p


def _rope_tables():
    i = np.arange(0, HD, 2, dtype=np.float32) / np.float32(HD)
    theta = (np.float64(1.0) / np.float64(10000.0) ** i.astype(np.float64)).astype(np.float32)
    t = np.arange(T, dtype=np.float32)
    ang = (t[:, None] * theta[None, :]).astype(np.float32)
    cos = np.cos(ang.astype(np.float64)).astype(np.float32)
    sin = np.sin(ang.astype(np.float64)).astype(np.float32)
    Ctab = np.ascontiguousarray(np.concatenate([cos.T, cos.T], axis=0))
    Stab = np.ascontiguousarray(np.concatenate([-sin.T, sin.T], axis=0))
    return Ctab, Stab


def _masks_for_core(k):
    out = np.zeros((NSLOT, 16, 128, 512), dtype=np.float32)
    for s in range(NSLOT):
        b = 4 * s + k
        Es = 512 * (4 * s + 4)
        q_abs = 512 * b + np.arange(512)
        for idx in range(16):
            k0 = Es - 2048 + idx * 128
            key_abs = k0 + np.arange(128)
            out[s, idx] = (key_abs[:, None] <= q_abs[None, :]).astype(np.float32)
    return np.ascontiguousarray(out.astype(ml_dtypes.bfloat16))


def _host_prep(inputs):
    x = np.asarray(inputs["L0_attn_x"], dtype=np.float32)
    Wq = np.asarray(inputs["L0_attn_Wq"], dtype=np.float32)
    bq = np.asarray(inputs["L0_attn_bq"], dtype=np.float32)
    Wk = np.asarray(inputs["L0_attn_Wk"], dtype=np.float32)
    bk = np.asarray(inputs["L0_attn_bk"], dtype=np.float32)
    Wv = np.asarray(inputs["L0_attn_Wv"], dtype=np.float32)
    bv = np.asarray(inputs["L0_attn_bv"], dtype=np.float32)
    Wo = np.asarray(inputs["L0_attn_Wo"], dtype=np.float32)
    bo = np.asarray(inputs["L0_attn_bo"], dtype=np.float32)
    Wgu = np.asarray(inputs["L0_ffn_w_gate_up"], dtype=np.float32)
    Wp = np.asarray(inputs["L0_ffn_w_proj"], dtype=np.float32)

    perm = _rope_perm()
    qperm = np.concatenate([h * 128 + perm for h in range(HQ)])
    kperm = np.concatenate([h * 128 + perm for h in range(HKV)])
    Ctab, Stab = _rope_tables()

    shared = {
        "wqp": np.ascontiguousarray(Wq[:, qperm]),
        "wkp": np.ascontiguousarray(Wk[:, kperm]),
        "wv": Wv,
        "wo": Wo,
        "bqt": np.ascontiguousarray(bq[qperm].reshape(HQ, 128).T),
        "bkt": np.ascontiguousarray(bk[kperm].reshape(HKV, 128).T),
        "bvt": np.ascontiguousarray(bv.reshape(HKV, 128).T),
        "bot": np.ascontiguousarray(bo.reshape(NDC, 128).T),

        "cosk": Ctab,
        "sink": Stab,
        "identf": np.eye(128, dtype=np.float32),
        "identb": np.ascontiguousarray(np.eye(128, dtype=np.float32).astype(ml_dtypes.bfloat16)),
        "ind2": np.ascontiguousarray(
            (np.kron(np.eye(2, dtype=np.float32), np.ones((64, 1), np.float32)) / 64.0
             ).astype(ml_dtypes.bfloat16)),
        "e2": np.ascontiguousarray(
            np.kron(np.eye(2, dtype=np.float32), np.ones((1, 64), np.float32)
                    ).astype(ml_dtypes.bfloat16)),
        "ones128b": np.ones((128, 1), dtype=ml_dtypes.bfloat16),
        "onesrowf": np.ones((64, 128), dtype=np.float32),
    }

    WguT = np.ascontiguousarray(Wgu.T)
    WpT = np.ascontiguousarray(Wp.T)
    per_core = []
    for c in range(8):
        b, k = c // 4, c % 4
        own = np.concatenate([np.arange(512) + (4 * s + k) * 512 for s in range(NSLOT)])
        d = dict(shared)
        d["wgut"] = np.ascontiguousarray(WguT[:, c * 2048:(c + 1) * 2048])
        d["wpt"] = np.ascontiguousarray(WpT[c * 1024:(c + 1) * 1024, :])
        d["xfull"] = np.ascontiguousarray(x[b])
        d["xown"] = np.ascontiguousarray(x[b][own])
        d["cosq"] = np.ascontiguousarray((0.25 * Ctab[:, own]).astype(np.float32))
        d["sinq"] = np.ascontiguousarray((0.25 * Stab[:, own]).astype(np.float32))
        d["masks"] = _masks_for_core(k)
        per_core.append(d)
    return per_core



def build_nc():
    nc = bass.Bass("TRN2", target_bir_lowering=False)
    es = ExitStack()

    def din(name, shape, dt=F32):
        return nc.dram_tensor(name, list(shape), dt, kind="ExternalInput").ap()

    xfull = din("xfull", [T, C])
    xown = din("xown", [NSLOT * 512, C])
    wqp = din("wqp", [C, C])
    wkp = din("wkp", [C, 512])
    wv = din("wv", [C, 512])
    wo = din("wo", [C, C])
    bqt = din("bqt", [128, HQ])
    bkt = din("bkt", [128, HKV])
    bvt = din("bvt", [128, HKV])
    bot = din("bot", [128, NDC])
    wgut = din("wgut", [C, 2048])
    wpt = din("wpt", [1024, C])
    cosk = din("cosk", [128, T])
    sink = din("sink", [128, T])
    cosq = din("cosq", [128, NSLOT * 512])
    sinq = din("sinq", [128, NSLOT * 512])
    masks = din("masks", [NSLOT, 16, 128, 512], BF16)
    identf = din("identf", [128, 128])
    identb = din("identb", [128, 128], BF16)
    ind2 = din("ind2", [128, 2], BF16)
    e2 = din("e2", [2, 128], BF16)
    ones128b = din("ones128b", [128, 1], BF16)
    onesrowf = din("onesrowf", [64, 128])

    y = nc.dram_tensor("y", [NSLOT * 512, C], F32, kind="ExternalOutput").ap()

    skind = "ExternalOutput" if DEBUG else "Internal"
    xnscr = nc.dram_tensor("xnscr", [NSLOT, 128, NDC, 512], BF16, kind=skind).ap()
    qscr = nc.dram_tensor("qscr", [NSLOT, 128, HQ, 512], BF16, kind=skind).ap()
    kscr = nc.dram_tensor("kscr", [HKV, 128, T], BF16, kind=skind).ap()
    vtscr = nc.dram_tensor("vtscr", [HKV, 128, T], BF16, kind=skind).ap()
    wguq_sh = nc.dram_tensor("wguq_sh", [16, 128, NDC, 128], BF16, kind="Internal").ap()
    wpq_sh = nc.dram_tensor("wpq_sh", [NDC, 128, 8, 128], BF16, kind="Internal").ap()
    wguq = nc.dram_tensor("wguq", [2 * HF // 128, 128, NDC, 128], BF16,
                          kind="Internal", addr_space="Shared").ap()
    wpq = nc.dram_tensor("wpq", [8, NDC, 128, 8, 128], BF16,
                         kind="Internal", addr_space="Shared").ap()
    sscr = nc.dram_tensor("sscr", [NSLOT, 4, 128, C], F32, kind=skind).ap()
    oscr = nc.dram_tensor("oscr", [NSLOT, 128, HQ * 512], BF16, kind=skind).ap() if DEBUG else None
    dnscr = nc.dram_tensor("dnscr", [NSLOT, HKV, 2, 64, 512], F32, kind=skind).ap() if DEBUG else None

    sems = {n: Sem(nc, es, n) for n in
            ["ve", "ac", "te", "gp", "dln", "dw", "dwt", "dst",
             "dk0", "dk1", "dq", "dg", "dp", "dxs", "dy"]}
    sVE, sAC, sTE, sGP = sems["ve"], sems["ac"], sems["te"], sems["gp"]
    sDLN, sDW, sDWT, sDST = sems["dln"], sems["dw"], sems["dwt"], sems["dst"]

    def gsb(name, shape, dt=F32):
        return es.enter_context(nc.sbuf_tensor(name, shape, dt))
    cbias = gsb("cbias", [128, 1])
    maxq2 = gsb("maxq2", [1, 1])
    maxk2 = gsb("maxk2", [1, 1])
    identb_sb = gsb("identb_sb", [128, 128], BF16)
    identf_sb = gsb("identf_sb", [128, 128])
    ind2_sb = gsb("ind2_sb", [128, 2], BF16)
    e2_sb = gsb("e2_sb", [2, 128], BF16)
    ones128b_sb = gsb("ones128b_sb", [128, 1], BF16)
    onesrowf_sb = gsb("onesrowf_sb", [64, 128])
    bq_sb = gsb("bq_sb", [128, HQ])
    bk_sb = gsb("bk_sb", [128, HKV])
    bv_sb = gsb("bv_sb", [128, HKV])
    bo_sb = gsb("bo_sb", [128, NDC])

    def emit_ln(blk, xbufs, stat4, xn_tm, tok_x):
        sums, ssq, mean, t1, msq, var, sd, rstd, scr = (
            stat4[k] for k in ["sums", "ssq", "mean", "t1", "msq", "var",
                               "sd", "rstd", "scratch"])

        def v1(v):
            for u in range(4):
                v.wait_ge(sDLN.h, tok_x[u])
                v.reduce_sum(out=sums[:, u:u + 1], in_=xbufs[u][:], axis=AX.X)
                v.scalar_tensor_tensor(out=scr[:], in0=xbufs[u][:], scalar=1.0,
                                       in1=xbufs[u][:], op0=ALU.mult, op1=ALU.mult,
                                       accum_out=ssq[:, u:u + 1])
            v.drain()
            v.tensor_scalar(out=mean[:], in0=sums[:], scalar1=1.0 / 2048,
                            scalar2=None, op0=ALU.mult)
            v.tensor_scalar(out=t1[:], in0=ssq[:], scalar1=1.0 / 2048,
                            scalar2=LN_EPS, op0=ALU.mult, op1=ALU.add)
            v.drain()
            v.tensor_tensor(out=msq[:], in0=mean[:], in1=mean[:], op=ALU.mult)
            v.drain()
            sVE.inc(v.tensor_tensor(out=var[:], in0=t1[:], in1=msq[:],
                                    op=ALU.subtract))
        blk.vector(v1)
        tok_var = sVE.v

        def s1(sc):
            sc.wait_ge(sVE.h, tok_var)
            sAC.inc(sc.sqrt(out=sd[:], in_=var[:]))
        blk.scalar(s1)
        tok_sd = sAC.v

        def v2(v):
            v.wait_ge(sAC.h, tok_sd)
            v.reciprocal(out=rstd[:], in_=sd[:])
            v.drain()
            for u in range(4):
                v.tensor_scalar(
                    out=xn_tm[u][:], in0=xbufs[u][:],
                    scalar1=mean[:, u:u + 1], scalar2=rstd[:, u:u + 1],
                    op0=ALU.subtract, op1=ALU.mult)
                sVE.inc(v.drain())
        blk.vector(v2)
        return sVE.v

    def emit_tm2fm(blk, xn_tm, xn_fm, tpb, tok_xn_last):
        tp_toks = []

        def tb(t):
            for u in range(4):
                t.wait_ge(sVE.h, tok_xn_last - 3 + u)
                for r in range(2):
                    if tp_toks:
                        t.wait_ge(sVE.h, ev_base + len(tp_toks))
                    for q in range(8):
                        dc = r * 8 + q
                        mm = t.transpose(tpb[:, q * 128:(q + 1) * 128],
                                         xn_tm[u][:, dc * 128:(dc + 1) * 128],
                                         identb_sb[:])
                    tp_toks.append(sTE.inc(mm))
        ev_base = sVE.v

        def vb(v):
            fmv = xn_fm[:].rearrange("p (dc f) -> p dc f", dc=NDC)
            ti = 0
            for u in range(4):
                for r in range(2):
                    v.wait_ge(sTE.h, tp_toks[ti]); ti += 1
                    dst = fmv[:, r * 8:(r + 1) * 8, u * 128:(u + 1) * 128]
                    v.tensor_copy(out=dst, in_=tpb[:].rearrange(
                        "p (q f) -> p q f", q=8))
                    sVE.inc(v.drain())
        blk.tensor(tb)
        blk.vector(vb)
        return sVE.v

    def soft_barrier(blk):
        a, v, t = sAC.v, sVE.v, sTE.v

        def tb_(e):
            e.wait_ge(sAC.h, a)
            e.wait_ge(sVE.h, v)

        def sb_(e):
            e.wait_ge(sTE.h, t)
            e.wait_ge(sVE.h, v)

        def vb_(e):
            e.wait_ge(sTE.h, t)
            e.wait_ge(sAC.h, a)
        blk.tensor(tb_)
        blk.scalar(sb_)
        blk.vector(vb_)

    with ExitStack() as ps:
        def sb(name, shape, dt=F32):
            return ps.enter_context(nc.sbuf_tensor(name, shape, dt))
        wq_sb = sb("wq_sb", [128, NDC * C], BF16)
        xbufs = [sb(f"a0xb{u}", [128, C]) for u in range(4)]
        stat4 = {k: sb("a0st_" + k, [128, 4]) for k in
                 ["sums", "ssq", "mean", "t1", "msq", "var", "sd", "rstd"]}
        stat4["scratch"] = sb("a0st_scratch", [128, C])
        xn_tm = [sb(f"a0xntm{u}", [128, C], BF16) for u in range(4)]
        xn_fm = sb("a0xnfm", [128, NDC * 512], BF16)
        qf = [sb(f"a0qf{i}", [128, 512]) for i in range(2)]
        rsw = sb("a0rsw", [128, 512])
        rt1 = sb("a0rt1", [128, 512])
        rt2 = sb("a0rt2", [128, 512])
        qstage = sb("a0qstage", [128, HQ * 512], BF16)
        sq16 = sb("a0sq16", [128, 512], BF16)
        qn_red = sb("a0qn_red", [1, 1])
        cq_sb = sb("a0cq", [128, NSLOT * 512])
        sq_sb = sb("a0sq", [128, NSLOT * 512])

        qps = [ps.enter_context(nc.psum_tensor(f"a0qps{i}", [128, 512], F32))
               for i in range(2)]
        tpb = ps.enter_context(nc.psum_tensor("a0tpb", [128, 1024], BF16))
        qn_ps = ps.enter_context(nc.psum_tensor("a0qn_ps", [1, 512], F32))

        with nc.Block("a0") as blk:
            def sy0(sy):
                for dst, src in [(identb_sb, identb), (identf_sb, identf),
                                 (ind2_sb, ind2), (e2_sb, e2),
                                 (ones128b_sb, ones128b), (onesrowf_sb, onesrowf),
                                 (bq_sb, bqt), (bk_sb, bkt), (bv_sb, bvt),
                                 (bo_sb, bot), (cq_sb, cosq), (sq_sb, sinq)]:
                    sDW.dinc(sy.dma_start(out=dst[:], in_=src))
            blk.sync(sy0)
            tok_const = sDW.v

            def gp0(g):
                for dc in range(NDC):
                    sDWT.dinc(g.dma_start(out=wq_sb[:, dc * C:(dc + 1) * C],
                                          in_=wqp[dc * 128:(dc + 1) * 128, :]))
                g.memset(qn_red[:], 0.0)
                sGP.inc(g.memset(maxq2[:], 0.0))
            blk.gpsimd(gp0)
            tok_wq = sDWT.v
            tok_gz = sGP.v

            for s in range(NSLOT):
                soft_barrier(blk)
                tok_x = []
                ve_now = sVE.v

                def syx(sy, s=s, ve_now=ve_now):
                    sy.wait_ge(sVE.h, ve_now)
                    for u in range(4):
                        tok_x.append(sDLN.dinc(sy.dma_start(
                            out=xbufs[u][:],
                            in_=xown[s * 512 + u * 128: s * 512 + (u + 1) * 128, :])))
                blk.sync(syx)

                tok_xn = emit_ln(blk, xbufs, stat4, xn_tm, tok_x)
                tok_fm = emit_tm2fm(blk, xn_tm, xn_fm, tpb, tok_xn)

                def sxn(sy, s=s):
                    sy.wait_ge(sVE.h, tok_fm)
                    sDST.dinc(sy.dma_start(
                        out=xnscr[s],
                        in_=xn_fm[:].rearrange("p (dc f) -> p dc f", dc=NDC)))
                blk.sync(sxn)

                qmm_tok = []
                qev_base = sAC.v
                rope_base = sVE.v

                def tq(t, s=s):
                    t.wait_ge(sDWT.h, tok_wq)
                    t.wait_ge(sVE.h, tok_fm)
                    for g in range(HQ):
                        if g >= 2:
                            t.wait_ge(sAC.h, qev_base + (g - 2) + 1)
                        for dc in range(NDC):
                            mm = t.matmul(
                                qps[g % 2][:],
                                wq_sb[:, dc * C + g * 128: dc * C + (g + 1) * 128],
                                xn_fm[:, dc * 512:(dc + 1) * 512],
                                start=(dc == 0), stop=(dc == NDC - 1))
                        qmm_tok.append(sTE.inc(mm))
                blk.tensor(tq)

                def sq(sc, s=s):
                    sc.wait_ge(sDW.h, tok_const)
                    for g in range(HQ):
                        sc.wait_ge(sTE.h, qmm_tok[g])
                        if g >= 2:
                            sc.wait_ge(sVE.h, rope_base + (g - 2) + 1)
                        sAC.inc(sc.activation(out=qf[g % 2][:], in_=qps[g % 2][:],
                                              func=AF.Identity, bias=bq_sb[:, g:g + 1],
                                              scale=1.0))
                blk.scalar(sq)

                def vq(v, s=s):
                    v.wait_ge(sDW.h, tok_const)
                    for g in range(HQ):
                        v.wait_ge(sAC.h, qev_base + g + 1)
                        q = qf[g % 2]
                        v.tensor_copy(out=rsw[0:64, :], in_=q[64:128, :])
                        v.tensor_copy(out=rsw[64:128, :], in_=q[0:64, :])
                        v.tensor_tensor(out=rt1[:], in0=q[:],
                                        in1=cq_sb[:, s * 512:(s + 1) * 512], op=ALU.mult)
                        v.tensor_tensor(out=rt2[:], in0=rsw[:],
                                        in1=sq_sb[:, s * 512:(s + 1) * 512], op=ALU.mult)
                        sVE.inc(v.tensor_tensor(
                            out=qstage[:, g * 512:(g + 1) * 512],
                            in0=rt1[:], in1=rt2[:], op=ALU.add))
                    v.drain()
                blk.vector(vq)
                tok_rope_last = sVE.v

                for g in range(HQ):
                    def s2(v, g=g, s=s):
                        v.wait_ge(sVE.h, tok_rope_last - (HQ - 1) + g)
                        if g >= 1:
                            v.wait_ge(sTE.h, qn_mm_tok[g - 1])
                        qsl = qstage[:, g * 512:(g + 1) * 512]
                        sVE.inc(v.tensor_tensor(out=sq16[:], in0=qsl, in1=qsl,
                                                op=ALU.mult))
                    blk.vector(s2)
                    tok_sq16 = sVE.v

                    def t2(t, g=g):
                        t.wait_ge(sDW.h, tok_const)
                        t.wait_ge(sVE.h, tok_sq16)
                        qn_mm_tok[g] = sTE.inc(t.matmul(
                            qn_ps[:], ones128b_sb[:], sq16[:], start=True, stop=True))
                    if g == 0:
                        qn_mm_tok = {}
                    blk.tensor(t2)

                    def v2(v, g=g):
                        v.wait_ge(sTE.h, qn_mm_tok[g])
                        v.wait_ge(sGP.h, tok_gz)
                        v.tensor_reduce(out=qn_red[:], in_=qn_ps[:], axis=AX.X,
                                        op=ALU.max)
                        v.drain()
                        sVE.inc(v.tensor_tensor(out=maxq2[:], in0=maxq2[:],
                                                in1=qn_red[:], op=ALU.max))
                        v.drain()
                    blk.vector(v2)

                def syq(sy, s=s):
                    sy.wait_ge(sVE.h, sVE.v)
                    sDST.dinc(sy.dma_start(
                        out=qscr[s],
                        in_=qstage[:].rearrange("p (g f) -> p g f", g=HQ)))
                blk.sync(syq)

            def syf(sy):
                sy.wait_ge(sDST.h, sDST.v)
            blk.sync(syf)

    with ExitStack() as ps:
        def sb(name, shape, dt=F32):
            return ps.enter_context(nc.sbuf_tensor(name, shape, dt))
        wk_sb = sb("wk_sb", [128, NDC * 512], BF16)
        wv_sb = sb("wv_sb", [128, NDC * 512], BF16)
        ck_sb = [sb(f"ck_sb{i}", [128, 512]) for i in range(2)]
        sk_sb = [sb(f"sk_sb{i}", [128, 512]) for i in range(2)]
        xbufs = [sb(f"a1xb{u}", [128, C]) for u in range(8)]
        stat4 = {k: sb("a1st_" + k, [128, 4]) for k in
                 ["sums", "ssq", "mean", "t1", "msq", "var", "sd", "rstd"]}
        stat4["scratch"] = sb("a1st_scratch", [128, C])
        xn_tm = [sb(f"a1xntm{u}", [128, C], BF16) for u in range(8)]
        xn_fm = [sb(f"a1xnfm{i}", [128, NDC * 512], BF16) for i in range(2)]
        kf = [sb(f"a1kf{i}", [128, 512]) for i in range(2)]
        rsw = sb("a1rsw", [128, 512])
        rt1 = sb("a1rt1", [128, 512])
        rt2 = sb("a1rt2", [128, 512])
        krope = [sb(f"a1krope{i}", [128, 512], BF16) for i in range(2)]
        vf16 = [sb(f"a1vf16{i}", [128, 512], BF16) for i in range(2)]
        vt_stage = [sb(f"a1vt{i}", [128, 512], BF16) for i in range(2)]
        sq16 = [sb(f"a1sq16{i}", [128, 512], BF16) for i in range(2)]
        kn_red = sb("a1kn_red", [1, 1])
        ctmp = sb("a1ctmp", [1, 1])
        ctmp2 = sb("a1ctmp2", [1, 1])

        kps = [ps.enter_context(nc.psum_tensor(f"a1kps{i}", [128, 512], F32))
               for i in range(2)]
        tpb = ps.enter_context(nc.psum_tensor("a1tpb", [128, 1024], BF16))
        kn_ps = ps.enter_context(nc.psum_tensor("a1kn_ps", [1, 512], F32))
        cb_ps = ps.enter_context(nc.psum_tensor("a1cb_ps", [128, 1], F32))

        with nc.Block("a1") as blk:


            def gp0(g):
                for dc in range(NDC):
                    sDWT.dinc(g.dma_start(out=wk_sb[:, dc * 512:(dc + 1) * 512],
                                          in_=wkp[dc * 128:(dc + 1) * 128, :]))
                    sDWT.dinc(g.dma_start(out=wv_sb[:, dc * 512:(dc + 1) * 512],
                                          in_=wv[dc * 128:(dc + 1) * 128, :]))
                g.memset(kn_red[:], 0.0)
                sGP.inc(g.memset(maxk2[:], 0.0))
            blk.gpsimd(gp0)
            tok_wkv = sDWT.v
            tok_gz = sGP.v

            kev_tok = {}
            krope_tok = {}
            kst_tok = {}
            ksq_tok = {}
    
            vev_tok = {}
            vtp_tok = {}
            vts_tok = {}
            vst_tok = {}
            kn_mm_tok = {}
            kn_v_tok = {}
            xn_used = {}
            fm_tok = {}
            fm_tok_v = {}
            tab_tok = {}
            for ti in range(16):
                tok_x = []
                ve_now = sVE.v

                def syx(sy, ti=ti, ve_now=ve_now):
                    if ti >= 2:
                        sy.wait_ge(sVE.h, xn_ve_done[ti - 2])
                    for u in range(4):
                        tok_x.append(sDLN.dinc(sy.dma_start(
                            out=xbufs[(ti % 2) * 4 + u][:],
                            in_=xfull[ti * 512 + u * 128: ti * 512 + (u + 1) * 128, :])))
                    sy.wait_ge(sVE.h, ve_now)
                    sDW.dinc(sy.dma_start(out=ck_sb[ti % 2][:],
                                          in_=cosk[:, ti * 512:(ti + 1) * 512]))
                    tab_tok[ti] = sDW.dinc(sy.dma_start(
                        out=sk_sb[ti % 2][:], in_=sink[:, ti * 512:(ti + 1) * 512]))
                blk.sync(syx)
                if ti == 0:
                    xn_ve_done = {}

                xb = xbufs[(ti % 2) * 4:(ti % 2) * 4 + 4]
                xt = xn_tm[(ti % 2) * 4:(ti % 2) * 4 + 4]
                if ti >= 2:
                    def vgate(v, ti=ti):
                        v.wait_ge(sTE.h, fm_tok[ti - 2])
                    blk.vector(vgate)
                tok_xn = emit_ln(blk, xb, stat4, xt, tok_x)
                xn_ve_done[ti] = tok_xn
                if ti >= 2:
                    def vgate2(v, ti=ti):
                        v.wait_ge(sTE.h, xn_used[ti - 2])
                    blk.vector(vgate2)
                tok_fm = emit_tm2fm(blk, xt, xn_fm[ti % 2], tpb, tok_xn)
                fm_tok[ti] = sTE.v
                fm_tok_v[ti] = tok_fm

                for j in range(HKV):
                    n = ti * HKV + j
                    fb = n % 2

                    def tk(t, ti=ti, j=j, n=n, fb=fb):
                        t.wait_ge(sDWT.h, tok_wkv)
                        t.wait_ge(sVE.h, fm_tok_v[ti])
                        if n - 1 in kev_tok:
                            t.wait_ge(sAC.h, kev_tok[n - 1])
                        for dc in range(NDC):
                            mm = t.matmul(
                                kps[0][:],
                                wk_sb[:, dc * 512 + j * 128: dc * 512 + (j + 1) * 128],
                                xn_fm[ti % 2][:, dc * 512:(dc + 1) * 512],
                                start=(dc == 0), stop=(dc == NDC - 1))
                        sTE.inc(mm)
                    blk.tensor(tk)
                    tok_kmm = sTE.v

                    def sk(sc, j=j, n=n, fb=fb):
                        sc.wait_ge(sTE.h, tok_kmm)
                        if n - 2 in krope_tok:
                            sc.wait_ge(sVE.h, krope_tok[n - 2])
                        kev_tok[n] = sAC.inc(sc.activation(
                            out=kf[fb][:], in_=kps[0][:], func=AF.Identity,
                            bias=bk_sb[:, j:j + 1], scale=1.0))
                    blk.scalar(sk)

                    def vk(v, j=j, ti=ti, n=n, fb=fb):
                        v.wait_ge(sAC.h, kev_tok[n])
                        v.wait_ge(sDW.h, tab_tok[ti])
                        if n - 2 in kst_tok:
                            v.wait_ge(sDST.h, kst_tok[n - 2])
                        q = kf[fb]
                        v.tensor_copy(out=rsw[0:64, :], in_=q[64:128, :])
                        v.tensor_copy(out=rsw[64:128, :], in_=q[0:64, :])
                        v.tensor_tensor(out=rt1[:], in0=q[:],
                                        in1=ck_sb[ti % 2][:], op=ALU.mult)
                        v.tensor_tensor(out=rt2[:], in0=rsw[:],
                                        in1=sk_sb[ti % 2][:], op=ALU.mult)
                        v.tensor_tensor(out=krope[fb][:], in0=rt1[:], in1=rt2[:],
                                        op=ALU.add)
                        if n - 2 in kn_mm_tok:
                            v.wait_ge(sTE.h, kn_mm_tok[n - 2])
                        v.tensor_tensor(out=sq16[fb][:], in0=krope[fb][:],
                                        in1=krope[fb][:], op=ALU.mult)
                        krope_tok[n] = sVE.inc(v.drain())
                    blk.vector(vk)

                    def syk(sy, j=j, ti=ti, n=n, fb=fb):
                        sy.wait_ge(sVE.h, krope_tok[n])
                        kst_tok[n] = sDST.dinc(sy.dma_start(
                            out=kscr[j][:, ti * 512:(ti + 1) * 512], in_=krope[fb][:]))
                    blk.sync(syk)

                    def t2(t, n=n, fb=fb):
                        t.wait_ge(sVE.h, krope_tok[n])
                        if n - 1 in kn_v_tok:
                            t.wait_ge(sVE.h, kn_v_tok[n - 1])
                        kn_mm_tok[n] = sTE.inc(t.matmul(
                            kn_ps[:], ones128b_sb[:], sq16[fb][:],
                            start=True, stop=True))
                    blk.tensor(t2)

                    def v2(v, n=n):
                        v.wait_ge(sTE.h, kn_mm_tok[n])
                        v.wait_ge(sGP.h, tok_gz)
                        v.tensor_reduce(out=kn_red[:], in_=kn_ps[:], axis=AX.X,
                                        op=ALU.max)
                        v.drain()
                        kn_v_tok[n] = sVE.inc(v.tensor_tensor(
                            out=maxk2[:], in0=maxk2[:], in1=kn_red[:], op=ALU.max))
                        v.drain()
                    blk.vector(v2)

                    def tv(t, j=j, ti=ti, n=n):
                        if n - 1 in vev_tok:
                            t.wait_ge(sAC.h, vev_tok[n - 1])
                        for dc in range(NDC):
                            mm = t.matmul(
                                kps[1][:],
                                wv_sb[:, dc * 512 + j * 128: dc * 512 + (j + 1) * 128],
                                xn_fm[ti % 2][:, dc * 512:(dc + 1) * 512],
                                start=(dc == 0), stop=(dc == NDC - 1))
                        sTE.inc(mm)
                        xn_used[ti] = sTE.v
                    blk.tensor(tv)
                    tok_vmm = sTE.v

                    def sv(sc, j=j, n=n, fb=fb):
                        sc.wait_ge(sTE.h, tok_vmm)
                        if n - 2 in vtp_tok:
                            sc.wait_ge(sTE.h, vtp_tok[n - 2])
                        vev_tok[n] = sAC.inc(sc.activation(
                            out=vf16[fb][:], in_=kps[1][:], func=AF.Identity,
                            bias=bv_sb[:, j:j + 1], scale=1.0))
                    blk.scalar(sv)

                    def tv2(t, j=j, n=n, fb=fb):
                        t.wait_ge(sAC.h, vev_tok[n])
                        if n - 1 in vts_tok:
                            t.wait_ge(sVE.h, vts_tok[n - 1])
                        for q in range(4):
                            mm = t.transpose(tpb[:, q * 128:(q + 1) * 128],
                                             vf16[fb][:, q * 128:(q + 1) * 128],
                                             identb_sb[:])
                        vtp_tok[n] = sTE.inc(mm)
                    blk.tensor(tv2)

                    def vv(v, j=j, n=n, fb=fb):
                        v.wait_ge(sTE.h, vtp_tok[n])
                        if n - 2 in vst_tok:
                            v.wait_ge(sDST.h, vst_tok[n - 2])
                        v.tensor_copy(out=vt_stage[fb][:], in_=tpb[:, 0:512])
                        vts_tok[n] = sVE.inc(v.drain())
                    blk.vector(vv)

                    def syv(sy, j=j, ti=ti, n=n, fb=fb):
                        sy.wait_ge(sVE.h, vts_tok[n])
                        vst_tok[n] = sDST.dinc(sy.dma_start(
                            out=vtscr[j][:, ti * 512:(ti + 1) * 512],
                            in_=vt_stage[fb][:]))
                    blk.sync(syv)

            def sc_c(sc):
                sc.wait_ge(sVE.h, sVE.v)
                sAC.inc(sc.sqrt(out=ctmp[:], in_=maxq2[:]))
                sAC.inc(sc.sqrt(out=ctmp2[:], in_=maxk2[:]))
            blk.scalar(sc_c)
            tok_cs = sAC.v

            def v_c(v):
                v.wait_ge(sAC.h, tok_cs)
                v.tensor_tensor(out=ctmp[:], in0=ctmp[:], in1=ctmp2[:], op=ALU.mult)
                v.drain()
                sVE.inc(v.tensor_scalar(out=ctmp[:], in0=ctmp[:], scalar1=-1.0,
                                        scalar2=None, op0=ALU.mult))
                v.drain()
            blk.vector(v_c)
            tok_cn = sVE.v

            def t_c(t):
                t.wait_ge(sVE.h, tok_cn)
                sTE.inc(t.matmul(cb_ps[:], onesrowf_sb[0:1, :], ctmp[:],
                                 start=True, stop=True))
            blk.tensor(t_c)
            tok_cb = sTE.v

            def v_c2(v):
                v.wait_ge(sTE.h, tok_cb)
                sVE.inc(v.tensor_copy(out=cbias[:], in_=cb_ps[:]))
            blk.vector(v_c2)

            def syf(sy):
                sy.wait_ge(sDST.h, sDST.v)
                sy.wait_ge(sVE.h, sVE.v)
            blk.sync(syf)

    with ExitStack() as ps:
        def sb(name, shape, dt=F32):
            return ps.enter_context(nc.sbuf_tensor(name, shape, dt))
        wb = [sb(f"qwb{i}", [128, 2048], BF16) for i in range(2)]
        aw = sb("qaw", [128, 2048], BF16)
        sgn = sb("qsgn", [128, 2048], BF16)
        sc16 = sb("qsc16", [64, 2048], BF16)
        thr = sb("qthr", [128, 2048])
        scbc = sb("qscbc", [128, 2048], BF16)
        mgt = sb("qmgt", [128, 2048], BF16)
        qv = sb("qqv", [128, 2048], BF16)
        wqv = sb("qwqv", [128, 2048], BF16)
        d1t = sb("qd1", [128, 2048], BF16)
        rr = [sb(f"qrr{i}", [128, 2048], BF16) for i in range(2)]

        scl_ps = ps.enter_context(nc.psum_tensor("qscl_ps", [2, 2048], F32))
        bc_ps = ps.enter_context(nc.psum_tensor("qbc_ps", [128, 2048], F32))

        tiles = []
        for dc in range(NDC):
            tiles.append(("gu", dc, 0))
        for hcl in range(8):
            tiles.append(("p", hcl, 0))
        NT = len(tiles)

        tok_ld = [None] * NT
        tok_sums = [None] * NT
        tok_scev = [None] * NT
        tok_bc = [None] * NT
        tok_bcev = [None] * NT
        tok_thr = [None] * NT
        tok_sgn = [None] * NT
        tok_aw = [None] * NT
        tok_q = [None] * NT
        tok_wq = [None] * NT
        tok_r = [None] * NT
        tok_st = [None] * NT

        with nc.Block("q") as blk:
            for ti2, (kind, a, b_) in enumerate(tiles):
                par = ti2 % 2

                def gl(g, ti2=ti2, kind=kind, a=a, b_=b_, par=par):
                    if kind == "gu":
                        src = wgut[a * 128:(a + 1) * 128, :]
                    else:
                        src = wpt[a * 128:(a + 1) * 128, :]
                    if ti2 >= 2:
                        g.wait_ge(sVE.h, tok_r[ti2 - 2])
                    tok_ld[ti2] = sDWT.dinc(g.dma_start(out=wb[par][:], in_=src))
                blk.gpsimd(gl)

                def s1(sc, ti2=ti2, par=par):
                    sc.wait_ge(sDWT.h, tok_ld[ti2])
                    if ti2 > 0:
                        sc.wait_ge(sTE.h, tok_sums[ti2 - 1])
                        sc.wait_ge(sVE.h, tok_q[ti2 - 1])
                    tok_aw[ti2] = sAC.inc(sc.activation(out=aw[:], in_=wb[par][:],
                                                        func=AF.Abs))
                    tok_sgn[ti2] = sAC.inc(sc.sign(out=sgn[:], in_=wb[par][:]))
                blk.scalar(s1)

                def t1(t, ti2=ti2):
                    t.wait_ge(sAC.h, tok_aw[ti2])
                    if ti2 > 0:
                        t.wait_ge(sVE.h, tok_scev[ti2 - 1])
                    for i in range(4):
                        mm = t.matmul(scl_ps[:, i * 512:(i + 1) * 512], ind2_sb[:],
                                      aw[:, i * 512:(i + 1) * 512],
                                      start=True, stop=True)
                    tok_sums[ti2] = sTE.inc(mm)
                blk.tensor(t1)

                def v1(v, ti2=ti2):
                    v.wait_ge(sTE.h, tok_sums[ti2])
                    if ti2 > 0:
                        v.wait_ge(sTE.h, tok_bc[ti2 - 1])
                    v.tensor_copy(out=sc16[0:2, :], in_=scl_ps[:])
                    tok_scev[ti2] = sVE.inc(v.tensor_scalar(
                        out=sc16[0:2, :], in0=sc16[0:2, :], scalar1=1e-8,
                        scalar2=None, op0=ALU.max))
                blk.vector(v1)

                def t2(t, ti2=ti2):
                    t.wait_ge(sVE.h, tok_scev[ti2])
                    if ti2 > 0:
                        t.wait_ge(sAC.h, tok_bcev[ti2 - 1])
                    for i in range(4):
                        mm = t.matmul(bc_ps[:, i * 512:(i + 1) * 512], e2_sb[:],
                                      sc16[0:2, i * 512:(i + 1) * 512],
                                      start=True, stop=True)
                    tok_bc[ti2] = sTE.inc(mm)
                blk.tensor(t2)

                def s2(sc, ti2=ti2):
                    sc.wait_ge(sTE.h, tok_bc[ti2])
                    if ti2 > 0:
                        sc.wait_ge(sVE.h, tok_wq[ti2 - 1])
                    tok_thr[ti2] = sAC.inc(sc.activation(out=thr[:], in_=bc_ps[:],
                                                         func=AF.Copy, scale=QTHRESH))
                    tok_bcev[ti2] = sAC.inc(sc.activation(out=scbc[:], in_=bc_ps[:],
                                                          func=AF.Copy))
                blk.scalar(s2)

                def v2(v, ti2=ti2, par=par):
                    v.wait_ge(sAC.h, tok_bcev[ti2])
                    v.wait_ge(sAC.h, tok_sgn[ti2])
                    if ti2 >= 2:
                        v.wait_ge(sDST.h, tok_st[ti2 - 2])
                    v.tensor_tensor(out=mgt[:], in0=aw[:], in1=thr[:], op=ALU.is_gt)
                    tok_q[ti2] = sVE.inc(v.tensor_tensor(out=qv[:], in0=mgt[:],
                                                         in1=sgn[:], op=ALU.mult))
                    tok_wq[ti2] = sVE.inc(v.tensor_tensor(out=wqv[:], in0=qv[:],
                                                          in1=scbc[:], op=ALU.mult))
                    v.tensor_tensor(out=d1t[:], in0=wqv[:], in1=wb[par][:],
                                    op=ALU.subtract)
                    tok_r[ti2] = sVE.inc(v.tensor_tensor(out=rr[par][:], in0=wb[par][:],
                                                         in1=d1t[:], op=ALU.add))
                blk.vector(v2)

                def sy1(sy, ti2=ti2, kind=kind, a=a, b_=b_, par=par):
                    sy.wait_ge(sVE.h, tok_r[ti2])
                    src = rr[par][:].rearrange("p (i f) -> p i f", i=16)
                    if kind == "gu":
                        dst = wguq_sh[:, :, a, :].transpose([1, 0, 2])
                    else:
                        dst = wpq_sh[:, :, a, :].transpose([1, 0, 2])
                    tok_st[ti2] = sDST.dinc(sy.dma_start(out=dst, in_=src))
                blk.sync(sy1)

            def gcc(g):
                g.wait_ge(sDST.h, sDST.v)
                g.collective_compute(
                    "AllGather", ALU.bypass,
                    replica_groups=[list(range(8))],
                    ins=[wguq_sh.rearrange("a p c f -> a (p c f)").opt()],
                    outs=[wguq.rearrange("a p c f -> a (p c f)").opt()],
                ).then_inc(sems["gp"].h, 1)
                sems["gp"].v += 1
                g.collective_compute(
                    "AllGather", ALU.bypass,
                    replica_groups=[list(range(8))],
                    ins=[wpq_sh.rearrange("d p c f -> (d p c f)").unsqueeze(0).opt()],
                    outs=[wpq.rearrange("s d p c f -> s (d p c f)").opt()],
                ).then_inc(sems["gp"].h, 1)
                sems["gp"].v += 1
                g.wait_ge(sems["gp"].h, sems["gp"].v)
            blk.gpsimd(gcc)

            def syf(sy):
                sy.wait_ge(sDST.h, sDST.v)
                sy.wait_ge(sems["gp"].h, sems["gp"].v)
            blk.sync(syf)

    with ExitStack() as ps:
        def sb(name, shape, dt=F32):
            return ps.enter_context(nc.sbuf_tensor(name, shape, dt))
        kbuf = [sb(f"bk{i}", [128, T], BF16) for i in range(2)]
        vtbuf = [sb(f"bvt{i}", [128, T], BF16) for i in range(2)]
        qbuf = sb("bq", [128, HQ * 512], BF16)
        mask_sb = sb("bmask", [128, 16 * 512], BF16)
        pbuf = sb("bp", [128, 2048], BF16)
        o_all = sb("boall", [128, HQ * 512], BF16)
        den2 = sb("bden2", [64, 8 * 512])
        wo_sb = [sb(f"bwo{i}", [128, NDC * 512], BF16) for i in range(2)]
        sf_ev = sb("bsfev", [128, 512])
        s_tm = sb("bstm", [128, 4 * C])

        qk_ps = ps.enter_context(nc.psum_tensor("bqk", [128, 2048], F32))
        pv_ps = [ps.enter_context(nc.psum_tensor(f"bpv{i}", [128, 512], F32))
                 for i in range(2)]
        den_ps = ps.enter_context(nc.psum_tensor("bdenp", [128, 512], F32))
        tpf_ps = den_ps

        sK = [sems["dk0"], sems["dk1"]]
        sDQ = sems["dq"]
        it_list = [(s, j) for s in range(NSLOT) for j in range(HKV)]
        kv_tok = {}
        q_tok = {}
        it_done = {}

        with nc.Block("b") as blk:
            def pro(sy):
                E0 = 2048
                sK[0].dinc(sy.dma_start(out=kbuf[0][:, 0:E0], in_=kscr[0][:, 0:E0]))
                sK[0].dinc(sy.dma_start(out=vtbuf[0][:, 0:E0], in_=vtscr[0][:, 0:E0]))
                sK[1].dinc(sy.dma_start(out=kbuf[1][:, 0:E0], in_=kscr[1][:, 0:E0]))
                sK[1].dinc(sy.dma_start(out=vtbuf[1][:, 0:E0], in_=vtscr[1][:, 0:E0]))
                sDQ.dinc(sy.dma_start(out=qbuf[:],
                                      in_=qscr[0].rearrange("p g f -> p (g f)")))
                sDQ.dinc(sy.dma_start(
                    out=mask_sb[:].rearrange("p (i f) -> p i f", i=16),
                    in_=masks[0].transpose([1, 0, 2])))
            blk.sync(pro)
            kv_tok[0] = (sK[0], sK[0].v)
            kv_tok[1] = (sK[1], sK[1].v)
            q_tok[0] = sDQ.v

            pair_done_ve = [None]
            wo_done_te = [None]
            prev_s_store = [None]
            norm_done_ve = [None]

            for n, (s, j) in enumerate(it_list):
                if j == 0:
                    soft_barrier(blk)
                E = 512 * (4 * s + 4)
                nsub = E // 128
                par = n % 2
                masked = list(range(max(0, nsub - 16), nsub))

                for p in range(2):
                    exp_base = sAC.v
                    pvd_base = sTE.v
                    msk_base = sVE.v
                    msk_tok = {}
                    mi = 0
                    for i in masked:
                        msk_tok[i] = msk_base + mi + 1
                        mi += 1

                    def tok_qk(i, pvd_base=pvd_base):
                        return pvd_base + (1 if i == 0 else 2 * i)

                    def tok_pvd(i, pvd_base=pvd_base, nsub=nsub):
                        return pvd_base + min(2 * i + 3, 2 * nsub)

                    def tok_exp(i, exp_base=exp_base):
                        return exp_base + i + 1

                    def tb(t, s=s, j=j, p=p, E=E, nsub=nsub, par=par, n=n,
                           exp_base=exp_base, msk_tok=msk_tok):
                        ksem, ktok = kv_tok[n]
                        t.wait_ge(ksem.h, ktok)
                        if p == 0:
                            t.wait_ge(sDQ.h, q_tok[s])
                        if pair_done_ve[0] is not None:
                            t.wait_ge(sVE.h, pair_done_ve[0])
                        def emit_qk(i):
                            ip = i % 2
                            if i >= 2:
                                t.wait_ge(sAC.h, tok_exp(i - 2))
                            t.ldweights(kbuf[par][:, i * 128:(i + 1) * 128])
                            for gg in range(2):
                                h = (2 * p + gg) * 4 + j
                                mm = t.matmul(
                                    qk_ps[:, ip * 1024 + gg * 512:
                                          ip * 1024 + (gg + 1) * 512],
                                    kbuf[par][:, i * 128:(i + 1) * 128],
                                    qbuf[:, h * 512:(h + 1) * 512],
                                    start=True, stop=True)
                            sTE.inc(mm)
                        emit_qk(0)
                        for i in range(nsub):
                            ip = i % 2
                            if i + 1 < nsub:
                                emit_qk(i + 1)
                            if i in msk_tok:
                                t.wait_ge(sVE.h, msk_tok[i])
                            else:
                                t.wait_ge(sAC.h, tok_exp(i))
                            t.ldweights(vtbuf[par][:, i * 128:(i + 1) * 128])
                            for gg in range(2):
                                t.matmul(pv_ps[gg][:],
                                         vtbuf[par][:, i * 128:(i + 1) * 128],
                                         pbuf[:, ip * 1024 + gg * 512:
                                              ip * 1024 + (gg + 1) * 512],
                                         start=(i == 0), stop=(i == nsub - 1))
                            t.ldweights(ones128b_sb[:], tile_position=(0, 0))
                            t.ldweights(ones128b_sb[:], tile_position=(0, 32))
                            for gg in range(2):
                                mm = t.matmul(
                                    den_ps[gg * 32:gg * 32 + 1, :],
                                    ones128b_sb[:],
                                    pbuf[:, ip * 1024 + gg * 512:
                                         ip * 1024 + (gg + 1) * 512],
                                    start=(i == 0), stop=(i == nsub - 1),
                                    tile_position=(0, gg * 32))
                            sTE.inc(mm)
                    blk.tensor(tb)
                    assert sTE.v == pvd_base + 2 * nsub

                    def sb_(sc, nsub=nsub, exp_base=exp_base, pvd_base=pvd_base):
                        for i in range(nsub):
                            ip = i % 2
                            sc.wait_ge(sTE.h, tok_qk(i))
                            if i >= 2:
                                sc.wait_ge(sTE.h, tok_pvd(i - 2))
                            sAC.inc(sc.activation(
                                out=pbuf[:, ip * 1024:(ip + 1) * 1024],
                                in_=qk_ps[:, ip * 1024:(ip + 1) * 1024],
                                func=AF.Exp, bias=cbias[:], scale=1.0))
                    blk.scalar(sb_)
                    assert sAC.v == exp_base + nsub

                    def vb(v, s=s, nsub=nsub, exp_base=exp_base, masked=masked):
                        v.wait_ge(sDQ.h, q_tok[s])
                        for i in masked:
                            ip = i % 2
                            idx = i - (nsub - 16)
                            v.wait_ge(sAC.h, tok_exp(i))
                            v.tensor_tensor(
                                out=pbuf[:, ip * 1024:ip * 1024 + 512],
                                in0=pbuf[:, ip * 1024:ip * 1024 + 512],
                                in1=mask_sb[:, idx * 512:(idx + 1) * 512],
                                op=ALU.mult)
                            sVE.inc(v.tensor_tensor(
                                out=pbuf[:, ip * 1024 + 512:ip * 1024 + 1024],
                                in0=pbuf[:, ip * 1024 + 512:ip * 1024 + 1024],
                                in1=mask_sb[:, idx * 512:(idx + 1) * 512],
                                op=ALU.mult))
                    blk.vector(vb)
                    assert sVE.v == msk_base + len(masked)

                    tok_all_pvd = sTE.v

                    def vep(v, s=s, j=j, p=p, tok_all_pvd=tok_all_pvd):
                        v.wait_ge(sTE.h, tok_all_pvd)
                        if wo_done_te[0] is not None:
                            v.wait_ge(sTE.h, wo_done_te[0])
                        col = (p * 4 + j) * 512
                        v.tensor_copy(out=den2[0:1, col:col + 512],
                                      in_=den_ps[0:1, :])
                        v.tensor_copy(out=den2[32:33, col:col + 512],
                                      in_=den_ps[32:33, :])
                        for gg in range(2):
                            h = (2 * p + gg) * 4 + j
                            mm = v.tensor_copy(out=o_all[:, h * 512:(h + 1) * 512],
                                               in_=pv_ps[gg][:])
                        sVE.inc(mm)
                    blk.vector(vep)
                    pair_done_ve[0] = sVE.v

                it_done[n] = sTE.v

                if n + 2 < len(it_list):
                    s2_, j2_ = it_list[n + 2]
                    E2 = 512 * (4 * s2_ + 4)

                    def syk(sy, s2_=s2_, j2_=j2_, E2=E2, par=par, n=n):
                        sy.wait_ge(sTE.h, it_done[n])
                        sK[par].dinc(sy.dma_start(out=kbuf[par][:, 0:E2],
                                                  in_=kscr[j2_][:, 0:E2]))
                        sK[par].dinc(sy.dma_start(out=vtbuf[par][:, 0:E2],
                                                  in_=vtscr[j2_][:, 0:E2]))
                    blk.sync(syk)
                    kv_tok[n + 2] = (sK[par], sK[par].v)

                if j == HKV - 1:
                    tok_pairs = sVE.v

                    def vnr(v, s=s, tok_pairs=tok_pairs):
                        v.wait_ge(sVE.h, tok_pairs)
                        v.reciprocal(out=den2[0:1, :], in_=den2[0:1, :])
                        v.drain()
                        v.reciprocal(out=den2[32:33, :], in_=den2[32:33, :])
                        sVE.inc(v.drain())
                    blk.vector(vnr)
                    tok_rd = sVE.v

                    nrm_mm_tok = {}
                    nrm_mu_tok = {}
                    for rnd in range(4):
                        def tnr(t, s=s, rnd=rnd):
                            t.wait_ge(sVE.h, tok_rd)
                            if rnd == 0:
                                pass
                            else:
                                t.wait_ge(sVE.h, nrm_mu_tok[rnd - 1])
                            for q4 in range(4):
                                h = rnd * 4 + q4
                                gg = h // 8
                                ggr = (h // 4) % 2
                                pcol = ((h // 8) * 4 + (h % 4)) * 512
                                mm = t.matmul(qk_ps[:, q4 * 512:(q4 + 1) * 512],
                                              onesrowf_sb[ggr * 32:ggr * 32 + 1, :],
                                              den2[ggr * 32:ggr * 32 + 1,
                                                   pcol:pcol + 512],
                                              start=True, stop=True)
                            nrm_mm_tok[rnd] = sTE.inc(mm)
                        blk.tensor(tnr)

                        def vnr2(v, s=s, rnd=rnd):
                            v.wait_ge(sTE.h, nrm_mm_tok[rnd])
                            v.tensor_tensor(
                                out=o_all[:, rnd * 2048:(rnd + 1) * 2048],
                                in0=o_all[:, rnd * 2048:(rnd + 1) * 2048],
                                in1=qk_ps[:], op=ALU.mult)
                            nrm_mu_tok[rnd] = sVE.inc(v.drain())
                        blk.vector(vnr2)
                    tok_oall = sVE.v
                    norm_done_ve[0] = tok_oall
                    if DEBUG:
                        def syo(sy, s=s):
                            sy.wait_ge(sVE.h, tok_oall)
                            sDST.dinc(sy.dma_start(out=oscr[s], in_=o_all[:]))
                        blk.sync(syo)

                    wo_tok = {}
                    wo_used = {}
                    sev_tok = {}
                    tp_tok = {}
                    tr_tok = {}

                    def gw0(g, s=s):
                        if wo_done_te[0] is not None:
                            g.wait_ge(sTE.h, wo_done_te[0])
                        wov = wo.rearrange("(dc p) m -> p dc m", p=128)
                        for blki in range(2):
                            wo_tok[blki] = sDWT.dinc(g.dma_start(
                                out=wo_sb[blki][:].rearrange(
                                    "p (dc f) -> p dc f", dc=NDC),
                                in_=wov[:, :, blki * 512:(blki + 1) * 512]))
                    blk.gpsimd(gw0)

                    for blki in range(4):
                        bpar = blki % 2
                        if blki >= 2:
                            def gw(g, blki=blki, bpar=bpar, s=s):
                                g.wait_ge(sTE.h, wo_used[blki - 2])
                                wov = wo.rearrange("(dc p) m -> p dc m", p=128)
                                wo_tok[blki] = sDWT.dinc(g.dma_start(
                                    out=wo_sb[bpar][:].rearrange(
                                        "p (dc f) -> p dc f", dc=NDC),
                                    in_=wov[:, :, blki * 512:(blki + 1) * 512]))
                            blk.gpsimd(gw)

                        ppieces = [qk_ps[:, 0:512], qk_ps[:, 512:1024],
                                   qk_ps[:, 1024:1536], qk_ps[:, 1536:2048]]
                        mm_base = sTE.v

                        def tw(t, s=s, blki=blki, bpar=bpar, tok_oall=tok_oall):
                            t.wait_ge(sDWT.h, wo_tok[blki])
                            t.wait_ge(sVE.h, tok_oall)
                            for dl in range(4):
                                if blki > 0:
                                    t.wait_ge(sAC.h, sev_tok[(blki - 1, dl)])
                                for dc in range(NDC):
                                    mm = t.matmul(
                                        ppieces[dl],
                                        wo_sb[bpar][:, dc * 512 + dl * 128:
                                                    dc * 512 + (dl + 1) * 128],
                                        o_all[:, dc * 512:(dc + 1) * 512],
                                        start=(dc == 0), stop=(dc == NDC - 1))
                                sTE.inc(mm)
                            wo_used[blki] = sTE.v
                        blk.tensor(tw)
                        assert sTE.v == mm_base + 4

                        tp_base = sTE.v
                        for dl in range(4):
                            tp_tok[(blki, dl)] = tp_base + dl + 1
                        ev_base = sAC.v
                        for dl in range(4):
                            sev_tok[(blki, dl)] = ev_base + dl + 1
                        trv_base = sVE.v
                        for dl in range(4):
                            tr_tok[(blki, dl)] = trv_base + dl + 1

                        def sw(sc, s=s, blki=blki, mm_base=mm_base,
                               ppieces=ppieces):
                            for dl in range(4):
                                dout = blki * 4 + dl
                                sc.wait_ge(sTE.h, mm_base + dl + 1)
                                if (blki, dl) != (0, 0):
                                    prev = (blki, dl - 1) if dl > 0 else (blki - 1, 3)
                                    sc.wait_ge(sTE.h, tp_tok[prev])
                                sAC.inc(sc.activation(
                                    out=sf_ev[:], in_=ppieces[dl], func=AF.Identity,
                                    bias=bo_sb[:, dout:dout + 1], scale=1.0))
                        blk.scalar(sw)
                        assert sAC.v == ev_base + 4

                        def tw2(t, s=s, blki=blki):
                            for dl in range(4):
                                t.wait_ge(sAC.h, sev_tok[(blki, dl)])
                                if (blki, dl) != (0, 0):
                                    prev = (blki, dl - 1) if dl > 0 else (blki - 1, 3)
                                    t.wait_ge(sVE.h, tr_tok[prev])
                                else:
                                    t.wait_ge(sVE.h, pair_done_ve[0])
                                for u in range(4):
                                    mm = t.transpose(tpf_ps[:, u * 128:(u + 1) * 128],
                                                     sf_ev[:, u * 128:(u + 1) * 128],
                                                     identf_sb[:])
                                sTE.inc(mm)
                        blk.tensor(tw2)
                        assert sTE.v == tp_base + 4

                        def vw(v, s=s, blki=blki):
                            stv = s_tm[:].rearrange("p (u f) -> p u f", u=4)
                            for dl in range(4):
                                dout = blki * 4 + dl
                                v.wait_ge(sTE.h, tp_tok[(blki, dl)])
                                if blki == 0 and dl == 0 and prev_s_store[0] is not None:
                                    v.wait_ge(sDST.h, prev_s_store[0])
                                dst = stv[:, :, dout * 128:(dout + 1) * 128]
                                if dl == 3:
                                    v.tensor_copy(
                                        out=dst, in_=tpf_ps[:].rearrange(
                                            "p (u f) -> p u f", u=4))
                                    sVE.inc(v.drain())
                                else:
                                    sVE.inc(v.tensor_copy(
                                        out=dst, in_=tpf_ps[:].rearrange(
                                            "p (u f) -> p u f", u=4)))
                        blk.vector(vw)
                        assert sVE.v == trv_base + 4

                    def sys(sy, s=s):
                        sy.wait_ge(sVE.h, tr_tok[(3, 3)])
                        for u in range(4):
                            sDST.dinc(sy.dma_start(out=sscr[s][u],
                                                   in_=s_tm[:, u * C:(u + 1) * C]))
                    blk.sync(sys)
                    prev_s_store[0] = sDST.v
                    wo_done_te[0] = wo_used[3]

                    if s + 1 < NSLOT:
                        def syq(sy, s=s):
                            sy.wait_ge(sTE.h, it_done[n])
                            sDQ.dinc(sy.dma_start(
                                out=qbuf[:],
                                in_=qscr[s + 1].rearrange("p g f -> p (g f)")))
                            sDQ.dinc(sy.dma_start(
                                out=mask_sb[:].rearrange("p (i f) -> p i f", i=16),
                                in_=masks[s + 1].transpose([1, 0, 2])))
                        blk.sync(syq)
                        q_tok[s + 1] = sDQ.v

            def syf(sy):
                sy.wait_ge(sDST.h, sDST.v)
            blk.sync(syf)

    with ExitStack() as ps:
        def sb(name, shape, dt=F32):
            return ps.enter_context(nc.sbuf_tensor(name, shape, dt))
        xnb = sb("cxnb", [128, NDC * 512], BF16)
        wg_g = [sb(f"cwg{i}", [128, 2048], BF16) for i in range(2)]
        wg_u = [sb(f"cwu{i}", [128, 2048], BF16) for i in range(2)]
        sg = [sb(f"csg{i}", [128, 512]) for i in range(2)]
        hsq = [sb(f"chsq{i}", [128, 512], BF16) for i in range(2)]
        h_all = sb("chall", [128, NHC * 512], BF16)
        wp_sb = [sb(f"cwp{i}", [128, 32 * 128], BF16) for i in range(2)]
        rs1 = sb("crs1", [1, 512])
        rs2 = sb("crs2", [1, 512])
        rdbc = sb("crdbc", [128, 512])
        tff = sb("ctff", [128, 512])
        gfm = [sb(f"cgfm{i}", [128, 512]) for i in range(2)]
        g_tm = sb("cgtm", [128, 4 * C])
        xs = sb("cxs", [128, C])
        ss = sb("css", [128, C])
        ys = [sb(f"cys{i}", [128, C]) for i in range(2)]

        gps = [ps.enter_context(nc.psum_tensor(f"cgps{i}", [128, 512], F32))
               for i in range(2)]
        ups = [ps.enter_context(nc.psum_tensor(f"cups{i}", [128, 512], F32))
               for i in range(2)]
        h2_ps = ps.enter_context(nc.psum_tensor("ch2", [1, 512], F32))
        pps = [ps.enter_context(nc.psum_tensor(f"cpps{i}", [128, 512], F32))
               for i in range(2)]
        tpf2_ps = ps.enter_context(nc.psum_tensor("ctpf2", [128, 512], F32))

        sDG, sDP, sDXS, sDY = sems["dg"], sems["dp"], sems["dxs"], sems["dy"]

        with nc.Block("c") as blk:
            prev_slot_gu_te = [None]
            prev_slot_pj_te = [None]
            prev_slot_cb_ve = [None]
            prev_yst = [None, None]

            for s in range(NSLOT):
                soft_barrier(blk)

                def syxn(sy, s=s):
                    if prev_slot_gu_te[0] is not None:
                        sy.wait_ge(sTE.h, prev_slot_gu_te[0])
                    sDQ_ = sems["dq"]
                    sDQ_.dinc(sy.dma_start(
                        out=xnb[:], in_=xnscr[s].rearrange("p dc f -> p (dc f)")))
                blk.sync(syxn)
                tok_xnb = sems["dq"].v

                gld_tok = {}
                gmm_tok = {}
                umm_tok = {}
                silu_tok = {}
                h_tok = {}
                hsq_tok = {}
                h2_tok = {}

                def syg0(sy, s=s):
                    if prev_slot_gu_te[0] is not None:
                        sy.wait_ge(sTE.h, prev_slot_gu_te[0])
                    guv = wguq.rearrange("fc p dc f -> fc p (dc f)")
                    sDG.dinc(sy.dma_start(out=wg_g[0][:], in_=guv[0]))
                    gld_tok[0] = sDG.dinc(sy.dma_start(out=wg_u[0][:], in_=guv[64]))
                blk.sync(syg0)

                for fc in range(NHC):
                    fpar = fc % 2
                    if fc + 1 < NHC:
                        def syg(sy, s=s, fc=fc):
                            if fc >= 1:
                                sy.wait_ge(sTE.h, umm_tok[fc - 1])
                            guv = wguq.rearrange("fc p dc f -> fc p (dc f)")
                            sDG.dinc(sy.dma_start(out=wg_g[(fc + 1) % 2][:],
                                                  in_=guv[fc + 1]))
                            gld_tok[fc + 1] = sDG.dinc(sy.dma_start(
                                out=wg_u[(fc + 1) % 2][:], in_=guv[64 + fc + 1]))
                        blk.sync(syg)

                    def tg(t, s=s, fc=fc, fpar=fpar):
                        t.wait_ge(sDG.h, gld_tok[fc])
                        t.wait_ge(sems["dq"].h, tok_xnb)
                        if prev_slot_pj_te[0] is not None and fc == 0:
                            pass
                        if fc >= 2:
                            t.wait_ge(sAC.h, silu_tok[fc - 2])
                            t.wait_ge(sVE.h, h_tok[fc - 2])
                        for dc in range(NDC):
                            mm = t.matmul(gps[fpar][:],
                                          wg_g[fpar][:, dc * 128:(dc + 1) * 128],
                                          xnb[:, dc * 512:(dc + 1) * 512],
                                          start=(dc == 0), stop=(dc == NDC - 1))
                        gmm_tok[fc] = sTE.inc(mm)
                        for dc in range(NDC):
                            mm = t.matmul(ups[fpar][:],
                                          wg_u[fpar][:, dc * 128:(dc + 1) * 128],
                                          xnb[:, dc * 512:(dc + 1) * 512],
                                          start=(dc == 0), stop=(dc == NDC - 1))
                        umm_tok[fc] = sTE.inc(mm)
                        if fc >= 2:
                            t.wait_ge(sVE.h, hsq_tok[fc - 2])
                            h2_tok[fc - 2] = sTE.inc(t.matmul(
                                h2_ps[:], ones128b_sb[:], hsq[(fc - 2) % 2][:],
                                start=(fc - 2 == 0), stop=False))
                    blk.tensor(tg)

                    def sg_(sc, s=s, fc=fc, fpar=fpar):
                        sc.wait_ge(sTE.h, gmm_tok[fc])
                        if fc >= 2:
                            sc.wait_ge(sVE.h, h_tok[fc - 2])
                        silu_tok[fc] = sAC.inc(sc.activation(
                            out=sg[fpar][:], in_=gps[fpar][:], func=AF.Silu))
                    blk.scalar(sg_)

                    def vg(v, s=s, fc=fc, fpar=fpar):
                        v.wait_ge(sAC.h, silu_tok[fc])
                        v.wait_ge(sTE.h, umm_tok[fc])
                        if prev_slot_pj_te[0] is not None:
                            v.wait_ge(sTE.h, prev_slot_pj_te[0])
                        h_tok[fc] = sVE.inc(v.tensor_tensor(
                            out=h_all[:, fc * 512:(fc + 1) * 512],
                            in0=sg[fpar][:], in1=ups[fpar][:], op=ALU.mult))
                        if fc >= 1:
                            if fc >= 3:
                                v.wait_ge(sTE.h, h2_tok[fc - 3])
                            hsl = h_all[:, (fc - 1) * 512:fc * 512]
                            hsq_tok[fc - 1] = sVE.inc(v.tensor_tensor(
                                out=hsq[(fc - 1) % 2][:], in0=hsl, in1=hsl,
                                op=ALU.mult))
                    blk.vector(vg)

                def sgt(v, s=s):
                    v.wait_ge(sTE.h, h2_tok[NHC - 3])
                    hsl = h_all[:, (NHC - 1) * 512:NHC * 512]
                    hsq_tok[NHC - 1] = sVE.inc(v.tensor_tensor(
                        out=hsq[(NHC - 1) % 2][:], in0=hsl, in1=hsl, op=ALU.mult))
                blk.vector(sgt)

                def tgt(t, s=s):
                    t.wait_ge(sVE.h, hsq_tok[NHC - 2])
                    h2_tok[NHC - 2] = sTE.inc(t.matmul(
                        h2_ps[:], ones128b_sb[:], hsq[(NHC - 2) % 2][:],
                        start=False, stop=False))
                    t.wait_ge(sVE.h, hsq_tok[NHC - 1])
                    h2_tok[NHC - 1] = sTE.inc(t.matmul(
                        h2_ps[:], ones128b_sb[:], hsq[(NHC - 1) % 2][:],
                        start=False, stop=True))
                blk.tensor(tgt)
                tok_h2 = sTE.v

                def vr(v, s=s):
                    v.wait_ge(sTE.h, tok_h2)
                    sVE.inc(v.tensor_scalar(out=rs1[:], in0=h2_ps[:],
                                            scalar1=1.0 / HF, scalar2=RMS_EPS,
                                            op0=ALU.mult, op1=ALU.add))
                    v.drain()
                blk.vector(vr)
                tok_rs = sVE.v

                def sr(sc, s=s):
                    sc.wait_ge(sVE.h, tok_rs)
                    sAC.inc(sc.sqrt(out=rs2[:], in_=rs1[:]))
                blk.scalar(sr)
                tok_rsq = sAC.v

                def vr2(v, s=s):
                    v.wait_ge(sAC.h, tok_rsq)
                    sVE.inc(v.reciprocal(out=rs1[:], in_=rs2[:]))
                    v.drain()
                blk.vector(vr2)
                tok_rr = sVE.v

                def tr(t, s=s):
                    t.wait_ge(sVE.h, tok_rr)
                    sTE.inc(t.matmul(tpf2_ps[:], onesrowf_sb[0:1, :], rs1[:],
                                     start=True, stop=True))
                blk.tensor(tr)
                tok_rbc = sTE.v

                def vr3(v, s=s):
                    v.wait_ge(sTE.h, tok_rbc)
                    sVE.inc(v.tensor_copy(out=rdbc[:], in_=tpf2_ps[:]))
                blk.vector(vr3)
                tok_rdbc = sVE.v

                pld_tok = {}
                pmm_tok = {}
                half_mm_tok = {}
                tffv_tok = {}
                gsil_tok = {}
                gtp_tok = {}
                gtr_tok = {}
                def wp_src(do, hh):
                    return wpq[hh * 4:(hh + 1) * 4, do].transpose([1, 0, 2, 3])

                def wp_dst(buf):
                    return buf[:].rearrange("p (a b f) -> p a b f", a=4, b=8)

                def syp0(sy, s=s):
                    if prev_slot_pj_te[0] is not None:
                        sy.wait_ge(sTE.h, prev_slot_pj_te[0])
                    sDP.dinc(sy.dma_start(out=wp_dst(wp_sb[0]), in_=wp_src(0, 0)))
                    pld_tok[0] = sDP.v
                    sDP.dinc(sy.dma_start(out=wp_dst(wp_sb[1]), in_=wp_src(0, 1)))
                    pld_tok[1] = sDP.v
                blk.sync(syp0)

                for do in range(NDC):
                    ppar = do % 2
                    for hh in range(2):
                        un = 2 * do + hh
                        buf = wp_sb[un % 2]
                        if un + 2 < 2 * NDC:
                            do2, hh2 = (un + 2) // 2, (un + 2) % 2

                            def syp(sy, s=s, un=un, do2=do2, hh2=hh2):
                                sy.wait_ge(sTE.h, half_mm_tok[un])
                                sDP.dinc(sy.dma_start(
                                    out=wp_dst(wp_sb[un % 2]),
                                    in_=wp_src(do2, hh2)))
                                pld_tok[un + 2] = sDP.v

                        def tp(t, s=s, do=do, hh=hh, un=un, ppar=ppar, buf=buf):
                            t.wait_ge(sDP.h, pld_tok[un])
                            if hh == 0 and do >= 2:
                                t.wait_ge(sVE.h, tffv_tok[do - 2])
                            for hl in range(32):
                                hc = hh * 32 + hl
                                mm = t.matmul(pps[ppar][:],
                                              buf[:, hl * 128:(hl + 1) * 128],
                                              h_all[:, hc * 512:(hc + 1) * 512],
                                              start=(hc == 0), stop=(hc == NHC - 1))
                            half_mm_tok[un] = sTE.inc(mm)
                        blk.tensor(tp)
                        if un + 2 < 2 * NDC:
                            blk.sync(syp)
                    pmm_tok[do] = half_mm_tok[2 * do + 1]

                    def vp(v, s=s, do=do, ppar=ppar):
                        v.wait_ge(sTE.h, pmm_tok[do])
                        v.wait_ge(sVE.h, tok_rdbc)
                        if do >= 1:
                            v.wait_ge(sAC.h, gsil_tok[do - 1])
                        v.tensor_tensor(out=tff[:], in0=pps[ppar][:],
                                        in1=rdbc[:], op=ALU.mult)
                        tffv_tok[do] = sVE.inc(v.drain())
                    blk.vector(vp)

                    def sp(sc, s=s, do=do):
                        sc.wait_ge(sVE.h, tffv_tok[do])
                        if do >= 2:
                            sc.wait_ge(sTE.h, gtp_tok[do - 2])
                        gsil_tok[do] = sAC.inc(sc.activation(
                            out=gfm[do % 2][:], in_=tff[:], func=AF.Silu))
                    blk.scalar(sp)

                    def tp2(t, s=s, do=do):
                        t.wait_ge(sAC.h, gsil_tok[do])
                        if do >= 1:
                            t.wait_ge(sVE.h, gtr_tok[do - 1])
                        else:
                            t.wait_ge(sVE.h, tok_rdbc)
                        if prev_slot_cb_ve[0] is not None:
                            t.wait_ge(sVE.h, prev_slot_cb_ve[0])
                        for u in range(4):
                            mm = t.transpose(tpf2_ps[:, u * 128:(u + 1) * 128],
                                             gfm[do % 2][:, u * 128:(u + 1) * 128],
                                             identf_sb[:])
                        gtp_tok[do] = sTE.inc(mm)
                    blk.tensor(tp2)

                    def vp2(v, s=s, do=do):
                        gv = g_tm[:].rearrange("p (u f) -> p u f", u=4)
                        v.wait_ge(sTE.h, gtp_tok[do])
                        gtr_tok[do] = sVE.inc(v.tensor_copy(
                            out=gv[:, :, do * 128:(do + 1) * 128],
                            in_=tpf2_ps[:].rearrange("p (u f) -> p u f", u=4)))
                    blk.vector(vp2)

                prev_slot_gu_te[0] = umm_tok[NHC - 1]
                prev_slot_pj_te[0] = pmm_tok[NDC - 1]

                cmb_tok = {}
                yst_tok = {}
                for u in range(4):
                    def syc1(sy, s=s, u=u):
                        if u >= 1:
                            sy.wait_ge(sVE.h, cmb_tok[u - 1])
                        elif prev_slot_cb_ve[0] is not None:
                            sy.wait_ge(sVE.h, prev_slot_cb_ve[0])
                        sDXS.dinc(sy.dma_start(
                            out=xs[:], in_=xown[s * 512 + u * 128:
                                              s * 512 + (u + 1) * 128, :]))
                        sDXS.dinc(sy.dma_start(out=ss[:], in_=sscr[s][u]))
                    blk.sync(syc1)
                    tok_ld2 = sDXS.v

                    def vc(v, s=s, u=u, tok_ld2=tok_ld2):
                        v.wait_ge(sDXS.h, tok_ld2)
                        v.wait_ge(sVE.h, gtr_tok[NDC - 1])
                        if u >= 2:
                            v.wait_ge(sDY.h, yst_tok[u - 2])
                        elif prev_yst[u] is not None:
                            v.wait_ge(sDY.h, prev_yst[u])
                        v.tensor_tensor(out=ss[:], in0=ss[:],
                                        in1=g_tm[:, u * C:(u + 1) * C], op=ALU.mult)
                        v.drain()
                        v.tensor_tensor(out=ys[u % 2][:], in0=xs[:], in1=ss[:],
                                        op=ALU.add)
                        cmb_tok[u] = sVE.inc(v.drain())
                    blk.vector(vc)

                    def syy(sy, s=s, u=u):
                        sy.wait_ge(sVE.h, cmb_tok[u])
                        yst_tok[u] = sDY.dinc(sy.dma_start(
                            out=y[s * 512 + u * 128: s * 512 + (u + 1) * 128, :],
                            in_=ys[u % 2][:]))
                    blk.sync(syy)
                prev_slot_cb_ve[0] = cmb_tok[3]
                prev_yst = [yst_tok[2], yst_tok[3]]

            def syf(sy):
                sy.wait_ge(sDY.h, sems["dy"].v)
            blk.sync(syf)

    es.close()
    return nc


_NC_CACHE = None


def kernel(**inputs):
    global _NC_CACHE
    per_core = _host_prep(inputs)
    if _NC_CACHE is None:
        _NC_CACHE = build_nc()
    res = run_bass_kernel_spmd(_NC_CACHE, per_core, core_ids=list(range(8)))
    out = np.zeros((B, T, C), dtype=np.float32)
    for c in range(8):
        b, k = c // 4, c % 4
        yc = np.asarray(res.results[c]["y"])
        for s in range(NSLOT):
            r0 = (4 * s + k) * 512
            out[b, r0:r0 + 512] = yc[s * 512:(s + 1) * 512]
    return out


# version 14
# speedup vs baseline: 1.4939x; 1.0066x over previous
import numpy as np
import ml_dtypes
import concourse.bass as bass
import concourse.mybir as mybir
from concourse.bass_utils import run_bass_kernel_spmd
from contextlib import ExitStack

F32 = mybir.dt.float32
BF16 = mybir.dt.bfloat16
AF = mybir.ActivationFunctionType
ALU = mybir.AluOpType
AX = mybir.AxisListType

B, T, C = 2, 8192, 2048
HQ, HKV, HD = 16, 4, 128
HF = 8192
NDC = C // 128
NHC = HF // 128
NSLOT = 4
LN_EPS = 1e-5
RMS_EPS = float(np.finfo(np.float32).eps)
QTHRESH = 0.5009765625

DEBUG = False


class Sem:
    def __init__(self, nc, es, name):
        self.h = es.enter_context(nc.semaphore(name))
        self.v = 0

    def inc(self, inst, k=1):
        inst.then_inc(self.h, k)
        self.v += k
        return self.v

    def dinc(self, inst):
        return self.inc(inst, 16)



def _rope_perm():
    p = np.empty(128, dtype=np.int64)
    p[:64] = np.arange(64) * 2
    p[64:] = np.arange(64) * 2 + 1
    return p


def _rope_tables():
    i = np.arange(0, HD, 2, dtype=np.float32) / np.float32(HD)
    theta = (np.float64(1.0) / np.float64(10000.0) ** i.astype(np.float64)).astype(np.float32)
    t = np.arange(T, dtype=np.float32)
    ang = (t[:, None] * theta[None, :]).astype(np.float32)
    cos = np.cos(ang.astype(np.float64)).astype(np.float32)
    sin = np.sin(ang.astype(np.float64)).astype(np.float32)
    Ctab = np.ascontiguousarray(np.concatenate([cos.T, cos.T], axis=0))
    Stab = np.ascontiguousarray(np.concatenate([-sin.T, sin.T], axis=0))
    return Ctab, Stab


def _masks_for_core(k):
    out = np.zeros((NSLOT, 16, 128, 512), dtype=np.float32)
    for s in range(NSLOT):
        b = 4 * s + k
        Es = 512 * (4 * s + 4)
        q_abs = 512 * b + np.arange(512)
        for idx in range(16):
            k0 = Es - 2048 + idx * 128
            key_abs = k0 + np.arange(128)
            out[s, idx] = (key_abs[:, None] <= q_abs[None, :]).astype(np.float32)
    return np.ascontiguousarray(out.astype(ml_dtypes.bfloat16))


def _host_prep(inputs):
    x = np.asarray(inputs["L0_attn_x"], dtype=np.float32)
    Wq = np.asarray(inputs["L0_attn_Wq"], dtype=np.float32)
    bq = np.asarray(inputs["L0_attn_bq"], dtype=np.float32)
    Wk = np.asarray(inputs["L0_attn_Wk"], dtype=np.float32)
    bk = np.asarray(inputs["L0_attn_bk"], dtype=np.float32)
    Wv = np.asarray(inputs["L0_attn_Wv"], dtype=np.float32)
    bv = np.asarray(inputs["L0_attn_bv"], dtype=np.float32)
    Wo = np.asarray(inputs["L0_attn_Wo"], dtype=np.float32)
    bo = np.asarray(inputs["L0_attn_bo"], dtype=np.float32)
    Wgu = np.asarray(inputs["L0_ffn_w_gate_up"], dtype=np.float32)
    Wp = np.asarray(inputs["L0_ffn_w_proj"], dtype=np.float32)

    perm = _rope_perm()
    qperm = np.concatenate([h * 128 + perm for h in range(HQ)])
    kperm = np.concatenate([h * 128 + perm for h in range(HKV)])
    Ctab, Stab = _rope_tables()

    shared = {
        "wqp": np.ascontiguousarray(Wq[:, qperm]),
        "wkp": np.ascontiguousarray(Wk[:, kperm]),
        "wv": Wv,
        "wo": Wo,
        "bqt": np.ascontiguousarray(bq[qperm].reshape(HQ, 128).T),
        "bkt": np.ascontiguousarray(bk[kperm].reshape(HKV, 128).T),
        "bvt": np.ascontiguousarray(bv.reshape(HKV, 128).T),
        "bot": np.ascontiguousarray(bo.reshape(NDC, 128).T),

        "cosk": Ctab,
        "sink": Stab,
        "identf": np.eye(128, dtype=np.float32),
        "identb": np.ascontiguousarray(np.eye(128, dtype=np.float32).astype(ml_dtypes.bfloat16)),
        "ind2": np.ascontiguousarray(
            (np.kron(np.eye(2, dtype=np.float32), np.ones((64, 1), np.float32)) / 64.0
             ).astype(ml_dtypes.bfloat16)),
        "e2": np.ascontiguousarray(
            np.kron(np.eye(2, dtype=np.float32), np.ones((1, 64), np.float32)
                    ).astype(ml_dtypes.bfloat16)),
        "ones128b": np.ones((128, 1), dtype=ml_dtypes.bfloat16),
        "onesrowf": np.ones((64, 128), dtype=np.float32),
    }

    WguT = np.ascontiguousarray(Wgu.T)
    WpT = np.ascontiguousarray(Wp.T)
    per_core = []
    for c in range(8):
        b, k = c // 4, c % 4
        own = np.concatenate([np.arange(512) + (4 * s + k) * 512 for s in range(NSLOT)])
        d = dict(shared)
        d["wgut"] = np.ascontiguousarray(WguT[:, c * 2048:(c + 1) * 2048])
        d["wpt"] = np.ascontiguousarray(WpT[c * 1024:(c + 1) * 1024, :])
        d["xfull"] = np.ascontiguousarray(x[b])
        d["xown"] = np.ascontiguousarray(x[b][own])
        d["cosq"] = np.ascontiguousarray((0.25 * Ctab[:, own]).astype(np.float32))
        d["sinq"] = np.ascontiguousarray((0.25 * Stab[:, own]).astype(np.float32))
        d["masks"] = _masks_for_core(k)
        per_core.append(d)
    return per_core



def build_nc():
    nc = bass.Bass("TRN2", target_bir_lowering=False)
    es = ExitStack()

    def din(name, shape, dt=F32):
        return nc.dram_tensor(name, list(shape), dt, kind="ExternalInput").ap()

    xfull = din("xfull", [T, C])
    xown = din("xown", [NSLOT * 512, C])
    wqp = din("wqp", [C, C])
    wkp = din("wkp", [C, 512])
    wv = din("wv", [C, 512])
    wo = din("wo", [C, C])
    bqt = din("bqt", [128, HQ])
    bkt = din("bkt", [128, HKV])
    bvt = din("bvt", [128, HKV])
    bot = din("bot", [128, NDC])
    wgut = din("wgut", [C, 2048])
    wpt = din("wpt", [1024, C])
    cosk = din("cosk", [128, T])
    sink = din("sink", [128, T])
    cosq = din("cosq", [128, NSLOT * 512])
    sinq = din("sinq", [128, NSLOT * 512])
    masks = din("masks", [NSLOT, 16, 128, 512], BF16)
    identf = din("identf", [128, 128])
    identb = din("identb", [128, 128], BF16)
    ind2 = din("ind2", [128, 2], BF16)
    e2 = din("e2", [2, 128], BF16)
    ones128b = din("ones128b", [128, 1], BF16)
    onesrowf = din("onesrowf", [64, 128])

    y = nc.dram_tensor("y", [NSLOT * 512, C], F32, kind="ExternalOutput").ap()

    skind = "ExternalOutput" if DEBUG else "Internal"
    xnscr = nc.dram_tensor("xnscr", [NSLOT, 128, NDC, 512], BF16, kind=skind).ap()
    qscr = nc.dram_tensor("qscr", [NSLOT, 128, HQ, 512], BF16, kind=skind).ap()
    kscr = nc.dram_tensor("kscr", [HKV, 128, T], BF16, kind=skind).ap()
    vtscr = nc.dram_tensor("vtscr", [HKV, 128, T], BF16, kind=skind).ap()
    wguq_sh = nc.dram_tensor("wguq_sh", [16, 128, NDC, 128], BF16, kind="Internal").ap()
    wpq_sh = nc.dram_tensor("wpq_sh", [NDC, 128, 8, 128], BF16, kind="Internal").ap()
    wguq = nc.dram_tensor("wguq", [2 * HF // 128, 128, NDC, 128], BF16,
                          kind="Internal", addr_space="Shared").ap()
    wpq = nc.dram_tensor("wpq", [8, NDC, 128, 8, 128], BF16,
                         kind="Internal", addr_space="Shared").ap()
    sscr = nc.dram_tensor("sscr", [NSLOT, 4, 128, C], F32, kind=skind).ap()
    oscr = nc.dram_tensor("oscr", [NSLOT, 128, HQ * 512], BF16, kind=skind).ap() if DEBUG else None
    dnscr = nc.dram_tensor("dnscr", [NSLOT, HKV, 2, 64, 512], F32, kind=skind).ap() if DEBUG else None

    sems = {n: Sem(nc, es, n) for n in
            ["ve", "ac", "te", "gp", "dln", "dw", "dwt", "dst",
             "dk0", "dk1", "dq", "dg", "dp", "dxs", "dy"]}
    sVE, sAC, sTE, sGP = sems["ve"], sems["ac"], sems["te"], sems["gp"]
    sDLN, sDW, sDWT, sDST = sems["dln"], sems["dw"], sems["dwt"], sems["dst"]

    def gsb(name, shape, dt=F32):
        return es.enter_context(nc.sbuf_tensor(name, shape, dt))
    cbias = gsb("cbias", [128, 1])
    maxq2 = gsb("maxq2", [1, 1])
    maxk2 = gsb("maxk2", [1, 1])
    identb_sb = gsb("identb_sb", [128, 128], BF16)
    identf_sb = gsb("identf_sb", [128, 128])
    ind2_sb = gsb("ind2_sb", [128, 2], BF16)
    e2_sb = gsb("e2_sb", [2, 128], BF16)
    ones128b_sb = gsb("ones128b_sb", [128, 1], BF16)
    onesrowf_sb = gsb("onesrowf_sb", [64, 128])
    bq_sb = gsb("bq_sb", [128, HQ])
    bk_sb = gsb("bk_sb", [128, HKV])
    bv_sb = gsb("bv_sb", [128, HKV])
    bo_sb = gsb("bo_sb", [128, NDC])

    def emit_ln(blk, xbufs, stat4, xn_tm, tok_x):
        sums, ssq, mean, t1, msq, var, sd, rstd, scr = (
            stat4[k] for k in ["sums", "ssq", "mean", "t1", "msq", "var",
                               "sd", "rstd", "scratch"])

        def v1(v):
            for u in range(4):
                v.wait_ge(sDLN.h, tok_x[u])
                v.reduce_sum(out=sums[:, u:u + 1], in_=xbufs[u][:], axis=AX.X)
                v.scalar_tensor_tensor(out=scr[:], in0=xbufs[u][:], scalar=1.0,
                                       in1=xbufs[u][:], op0=ALU.mult, op1=ALU.mult,
                                       accum_out=ssq[:, u:u + 1])
            v.drain()
            v.tensor_scalar(out=mean[:], in0=sums[:], scalar1=1.0 / 2048,
                            scalar2=None, op0=ALU.mult)
            v.tensor_scalar(out=t1[:], in0=ssq[:], scalar1=1.0 / 2048,
                            scalar2=LN_EPS, op0=ALU.mult, op1=ALU.add)
            v.drain()
            v.tensor_tensor(out=msq[:], in0=mean[:], in1=mean[:], op=ALU.mult)
            v.drain()
            sVE.inc(v.tensor_tensor(out=var[:], in0=t1[:], in1=msq[:],
                                    op=ALU.subtract))
        blk.vector(v1)
        tok_var = sVE.v

        def s1(sc):
            sc.wait_ge(sVE.h, tok_var)
            sAC.inc(sc.sqrt(out=sd[:], in_=var[:]))
        blk.scalar(s1)
        tok_sd = sAC.v

        def v2(v):
            v.wait_ge(sAC.h, tok_sd)
            v.reciprocal(out=rstd[:], in_=sd[:])
            v.drain()
            for u in range(4):
                v.tensor_scalar(
                    out=xn_tm[u][:], in0=xbufs[u][:],
                    scalar1=mean[:, u:u + 1], scalar2=rstd[:, u:u + 1],
                    op0=ALU.subtract, op1=ALU.mult)
                sVE.inc(v.drain())
        blk.vector(v2)
        return sVE.v

    def emit_tm2fm(blk, xn_tm, xn_fm, tpb, tok_xn_last):
        tp_toks = []

        def tb(t):
            for u in range(4):
                t.wait_ge(sVE.h, tok_xn_last - 3 + u)
                for r in range(2):
                    if tp_toks:
                        t.wait_ge(sVE.h, ev_base + len(tp_toks))
                    for q in range(8):
                        dc = r * 8 + q
                        mm = t.transpose(tpb[:, q * 128:(q + 1) * 128],
                                         xn_tm[u][:, dc * 128:(dc + 1) * 128],
                                         identb_sb[:])
                    tp_toks.append(sTE.inc(mm))
        ev_base = sVE.v

        def vb(v):
            fmv = xn_fm[:].rearrange("p (dc f) -> p dc f", dc=NDC)
            ti = 0
            for u in range(4):
                for r in range(2):
                    v.wait_ge(sTE.h, tp_toks[ti]); ti += 1
                    dst = fmv[:, r * 8:(r + 1) * 8, u * 128:(u + 1) * 128]
                    v.tensor_copy(out=dst, in_=tpb[:].rearrange(
                        "p (q f) -> p q f", q=8))
                    sVE.inc(v.drain())
        blk.tensor(tb)
        blk.vector(vb)
        return sVE.v

    def soft_barrier(blk):
        a, v, t = sAC.v, sVE.v, sTE.v

        def tb_(e):
            e.wait_ge(sAC.h, a)
            e.wait_ge(sVE.h, v)

        def sb_(e):
            e.wait_ge(sTE.h, t)
            e.wait_ge(sVE.h, v)

        def vb_(e):
            e.wait_ge(sTE.h, t)
            e.wait_ge(sAC.h, a)
        blk.tensor(tb_)
        blk.scalar(sb_)
        blk.vector(vb_)

    with ExitStack() as ps:
        def sb(name, shape, dt=F32):
            return ps.enter_context(nc.sbuf_tensor(name, shape, dt))
        wq_sb = sb("wq_sb", [128, NDC * C], BF16)
        xbufs = [sb(f"a0xb{u}", [128, C]) for u in range(4)]
        stat4 = {k: sb("a0st_" + k, [128, 4]) for k in
                 ["sums", "ssq", "mean", "t1", "msq", "var", "sd", "rstd"]}
        stat4["scratch"] = sb("a0st_scratch", [128, C])
        xn_tm = [sb(f"a0xntm{u}", [128, C], BF16) for u in range(4)]
        xn_fm = sb("a0xnfm", [128, NDC * 512], BF16)
        qf = [sb(f"a0qf{i}", [128, 512]) for i in range(2)]
        rsw = sb("a0rsw", [128, 512])
        rt1 = sb("a0rt1", [128, 512])
        rt2 = sb("a0rt2", [128, 512])
        qstage = sb("a0qstage", [128, HQ * 512], BF16)
        sq16 = sb("a0sq16", [128, 512], BF16)
        qn_red = sb("a0qn_red", [1, 1])
        cq_sb = sb("a0cq", [128, NSLOT * 512])
        sq_sb = sb("a0sq", [128, NSLOT * 512])

        qps = [ps.enter_context(nc.psum_tensor(f"a0qps{i}", [128, 512], F32))
               for i in range(2)]
        tpb = ps.enter_context(nc.psum_tensor("a0tpb", [128, 1024], BF16))
        qn_ps = ps.enter_context(nc.psum_tensor("a0qn_ps", [1, 512], F32))

        with nc.Block("a0") as blk:
            def sy0(sy):
                for dst, src in [(identb_sb, identb), (identf_sb, identf),
                                 (ind2_sb, ind2), (e2_sb, e2),
                                 (ones128b_sb, ones128b), (onesrowf_sb, onesrowf),
                                 (bq_sb, bqt), (bk_sb, bkt), (bv_sb, bvt),
                                 (bo_sb, bot), (cq_sb, cosq), (sq_sb, sinq)]:
                    sDW.dinc(sy.dma_start(out=dst[:], in_=src))
            blk.sync(sy0)
            tok_const = sDW.v

            def gp0(g):
                for dc in range(NDC):
                    sDWT.dinc(g.dma_start(out=wq_sb[:, dc * C:(dc + 1) * C],
                                          in_=wqp[dc * 128:(dc + 1) * 128, :]))
                g.memset(qn_red[:], 0.0)
                sGP.inc(g.memset(maxq2[:], 0.0))
            blk.gpsimd(gp0)
            tok_wq = sDWT.v
            tok_gz = sGP.v

            for s in range(NSLOT):
                soft_barrier(blk)
                tok_x = []
                ve_now = sVE.v

                def syx(sy, s=s, ve_now=ve_now):
                    sy.wait_ge(sVE.h, ve_now)
                    for u in range(4):
                        tok_x.append(sDLN.dinc(sy.dma_start(
                            out=xbufs[u][:],
                            in_=xown[s * 512 + u * 128: s * 512 + (u + 1) * 128, :])))
                blk.sync(syx)

                tok_xn = emit_ln(blk, xbufs, stat4, xn_tm, tok_x)
                tok_fm = emit_tm2fm(blk, xn_tm, xn_fm, tpb, tok_xn)

                def sxn(sy, s=s):
                    sy.wait_ge(sVE.h, tok_fm)
                    sDST.dinc(sy.dma_start(
                        out=xnscr[s],
                        in_=xn_fm[:].rearrange("p (dc f) -> p dc f", dc=NDC)))
                blk.sync(sxn)

                qmm_tok = []
                qev_base = sAC.v
                rope_base = sVE.v

                def tq(t, s=s):
                    t.wait_ge(sDWT.h, tok_wq)
                    t.wait_ge(sVE.h, tok_fm)
                    for g in range(HQ):
                        if g >= 2:
                            t.wait_ge(sAC.h, qev_base + (g - 2) + 1)
                        for dc in range(NDC):
                            mm = t.matmul(
                                qps[g % 2][:],
                                wq_sb[:, dc * C + g * 128: dc * C + (g + 1) * 128],
                                xn_fm[:, dc * 512:(dc + 1) * 512],
                                start=(dc == 0), stop=(dc == NDC - 1))
                        qmm_tok.append(sTE.inc(mm))
                blk.tensor(tq)

                def sq(sc, s=s):
                    sc.wait_ge(sDW.h, tok_const)
                    for g in range(HQ):
                        sc.wait_ge(sTE.h, qmm_tok[g])
                        if g >= 2:
                            sc.wait_ge(sVE.h, rope_base + (g - 2) + 1)
                        sAC.inc(sc.activation(out=qf[g % 2][:], in_=qps[g % 2][:],
                                              func=AF.Identity, bias=bq_sb[:, g:g + 1],
                                              scale=1.0))
                blk.scalar(sq)

                def vq(v, s=s):
                    v.wait_ge(sDW.h, tok_const)
                    for g in range(HQ):
                        v.wait_ge(sAC.h, qev_base + g + 1)
                        q = qf[g % 2]
                        v.tensor_copy(out=rsw[0:64, :], in_=q[64:128, :])
                        v.tensor_copy(out=rsw[64:128, :], in_=q[0:64, :])
                        v.tensor_tensor(out=rt1[:], in0=q[:],
                                        in1=cq_sb[:, s * 512:(s + 1) * 512], op=ALU.mult)
                        v.tensor_tensor(out=rt2[:], in0=rsw[:],
                                        in1=sq_sb[:, s * 512:(s + 1) * 512], op=ALU.mult)
                        sVE.inc(v.tensor_tensor(
                            out=qstage[:, g * 512:(g + 1) * 512],
                            in0=rt1[:], in1=rt2[:], op=ALU.add))
                    v.drain()
                blk.vector(vq)
                tok_rope_last = sVE.v

                for g in range(HQ):
                    def s2(v, g=g, s=s):
                        v.wait_ge(sVE.h, tok_rope_last - (HQ - 1) + g)
                        if g >= 1:
                            v.wait_ge(sTE.h, qn_mm_tok[g - 1])
                        qsl = qstage[:, g * 512:(g + 1) * 512]
                        sVE.inc(v.tensor_tensor(out=sq16[:], in0=qsl, in1=qsl,
                                                op=ALU.mult))
                    blk.vector(s2)
                    tok_sq16 = sVE.v

                    def t2(t, g=g):
                        t.wait_ge(sDW.h, tok_const)
                        t.wait_ge(sVE.h, tok_sq16)
                        qn_mm_tok[g] = sTE.inc(t.matmul(
                            qn_ps[:], ones128b_sb[:], sq16[:], start=True, stop=True))
                    if g == 0:
                        qn_mm_tok = {}
                    blk.tensor(t2)

                    def v2(v, g=g):
                        v.wait_ge(sTE.h, qn_mm_tok[g])
                        v.wait_ge(sGP.h, tok_gz)
                        v.tensor_reduce(out=qn_red[:], in_=qn_ps[:], axis=AX.X,
                                        op=ALU.max)
                        v.drain()
                        sVE.inc(v.tensor_tensor(out=maxq2[:], in0=maxq2[:],
                                                in1=qn_red[:], op=ALU.max))
                        v.drain()
                    blk.vector(v2)

                def syq(sy, s=s):
                    sy.wait_ge(sVE.h, sVE.v)
                    sDST.dinc(sy.dma_start(
                        out=qscr[s],
                        in_=qstage[:].rearrange("p (g f) -> p g f", g=HQ)))
                blk.sync(syq)

            def syf(sy):
                sy.wait_ge(sDST.h, sDST.v)
            blk.sync(syf)

    with ExitStack() as ps:
        def sb(name, shape, dt=F32):
            return ps.enter_context(nc.sbuf_tensor(name, shape, dt))
        wk_sb = sb("wk_sb", [128, NDC * 512], BF16)
        wv_sb = sb("wv_sb", [128, NDC * 512], BF16)
        ck_sb = [sb(f"ck_sb{i}", [128, 512]) for i in range(2)]
        sk_sb = [sb(f"sk_sb{i}", [128, 512]) for i in range(2)]
        xbufs = [sb(f"a1xb{u}", [128, C]) for u in range(8)]
        stat4 = {k: sb("a1st_" + k, [128, 4]) for k in
                 ["sums", "ssq", "mean", "t1", "msq", "var", "sd", "rstd"]}
        stat4["scratch"] = sb("a1st_scratch", [128, C])
        xn_tm = [sb(f"a1xntm{u}", [128, C], BF16) for u in range(8)]
        xn_fm = [sb(f"a1xnfm{i}", [128, NDC * 512], BF16) for i in range(2)]
        kf = [sb(f"a1kf{i}", [128, 512]) for i in range(2)]
        rsw = sb("a1rsw", [128, 512])
        rt1 = sb("a1rt1", [128, 512])
        rt2 = sb("a1rt2", [128, 512])
        krope = [sb(f"a1krope{i}", [128, 512], BF16) for i in range(2)]
        vf16 = [sb(f"a1vf16{i}", [128, 512], BF16) for i in range(2)]
        vt_stage = [sb(f"a1vt{i}", [128, 512], BF16) for i in range(2)]
        sq16 = [sb(f"a1sq16{i}", [128, 512], BF16) for i in range(2)]
        kn_red = sb("a1kn_red", [1, 1])
        ctmp = sb("a1ctmp", [1, 1])
        ctmp2 = sb("a1ctmp2", [1, 1])

        kps = [ps.enter_context(nc.psum_tensor(f"a1kps{i}", [128, 512], F32))
               for i in range(2)]
        tpb = ps.enter_context(nc.psum_tensor("a1tpb", [128, 1024], BF16))
        kn_ps = ps.enter_context(nc.psum_tensor("a1kn_ps", [1, 512], F32))
        cb_ps = ps.enter_context(nc.psum_tensor("a1cb_ps", [128, 1], F32))

        with nc.Block("a1") as blk:


            def gp0(g):
                for dc in range(NDC):
                    sDWT.dinc(g.dma_start(out=wk_sb[:, dc * 512:(dc + 1) * 512],
                                          in_=wkp[dc * 128:(dc + 1) * 128, :]))
                    sDWT.dinc(g.dma_start(out=wv_sb[:, dc * 512:(dc + 1) * 512],
                                          in_=wv[dc * 128:(dc + 1) * 128, :]))
                g.memset(kn_red[:], 0.0)
                sGP.inc(g.memset(maxk2[:], 0.0))
            blk.gpsimd(gp0)
            tok_wkv = sDWT.v
            tok_gz = sGP.v

            kev_tok = {}
            krope_tok = {}
            kst_tok = {}
            ksq_tok = {}
    
            vev_tok = {}
            vtp_tok = {}
            vts_tok = {}
            vst_tok = {}
            kn_mm_tok = {}
            kn_v_tok = {}
            xn_used = {}
            fm_tok = {}
            fm_tok_v = {}
            tab_tok = {}
            for ti in range(16):
                tok_x = []
                ve_now = sVE.v

                def syx(sy, ti=ti, ve_now=ve_now):
                    if ti >= 2:
                        sy.wait_ge(sVE.h, xn_ve_done[ti - 2])
                    for u in range(4):
                        tok_x.append(sDLN.dinc(sy.dma_start(
                            out=xbufs[(ti % 2) * 4 + u][:],
                            in_=xfull[ti * 512 + u * 128: ti * 512 + (u + 1) * 128, :])))
                    sy.wait_ge(sVE.h, ve_now)
                    sDW.dinc(sy.dma_start(out=ck_sb[ti % 2][:],
                                          in_=cosk[:, ti * 512:(ti + 1) * 512]))
                    tab_tok[ti] = sDW.dinc(sy.dma_start(
                        out=sk_sb[ti % 2][:], in_=sink[:, ti * 512:(ti + 1) * 512]))
                blk.sync(syx)
                if ti == 0:
                    xn_ve_done = {}

                xb = xbufs[(ti % 2) * 4:(ti % 2) * 4 + 4]
                xt = xn_tm[(ti % 2) * 4:(ti % 2) * 4 + 4]
                if ti >= 2:
                    def vgate(v, ti=ti):
                        v.wait_ge(sTE.h, fm_tok[ti - 2])
                    blk.vector(vgate)
                tok_xn = emit_ln(blk, xb, stat4, xt, tok_x)
                xn_ve_done[ti] = tok_xn
                if ti >= 2:
                    def vgate2(v, ti=ti):
                        v.wait_ge(sTE.h, xn_used[ti - 2])
                    blk.vector(vgate2)
                tok_fm = emit_tm2fm(blk, xt, xn_fm[ti % 2], tpb, tok_xn)
                fm_tok[ti] = sTE.v
                fm_tok_v[ti] = tok_fm

                for j in range(HKV):
                    n = ti * HKV + j
                    fb = n % 2

                    def tk(t, ti=ti, j=j, n=n, fb=fb):
                        t.wait_ge(sDWT.h, tok_wkv)
                        t.wait_ge(sVE.h, fm_tok_v[ti])
                        if n - 1 in kev_tok:
                            t.wait_ge(sAC.h, kev_tok[n - 1])
                        for dc in range(NDC):
                            mm = t.matmul(
                                kps[0][:],
                                wk_sb[:, dc * 512 + j * 128: dc * 512 + (j + 1) * 128],
                                xn_fm[ti % 2][:, dc * 512:(dc + 1) * 512],
                                start=(dc == 0), stop=(dc == NDC - 1))
                        sTE.inc(mm)
                    blk.tensor(tk)
                    tok_kmm = sTE.v

                    def sk(sc, j=j, n=n, fb=fb):
                        sc.wait_ge(sTE.h, tok_kmm)
                        if n - 2 in krope_tok:
                            sc.wait_ge(sVE.h, krope_tok[n - 2])
                        kev_tok[n] = sAC.inc(sc.activation(
                            out=kf[fb][:], in_=kps[0][:], func=AF.Identity,
                            bias=bk_sb[:, j:j + 1], scale=1.0))
                    blk.scalar(sk)

                    def vk(v, j=j, ti=ti, n=n, fb=fb):
                        v.wait_ge(sAC.h, kev_tok[n])
                        v.wait_ge(sDW.h, tab_tok[ti])
                        if n - 2 in kst_tok:
                            v.wait_ge(sDST.h, kst_tok[n - 2])
                        q = kf[fb]
                        v.tensor_copy(out=rsw[0:64, :], in_=q[64:128, :])
                        v.tensor_copy(out=rsw[64:128, :], in_=q[0:64, :])
                        v.tensor_tensor(out=rt1[:], in0=q[:],
                                        in1=ck_sb[ti % 2][:], op=ALU.mult)
                        v.tensor_tensor(out=rt2[:], in0=rsw[:],
                                        in1=sk_sb[ti % 2][:], op=ALU.mult)
                        v.tensor_tensor(out=krope[fb][:], in0=rt1[:], in1=rt2[:],
                                        op=ALU.add)
                        if n - 2 in kn_mm_tok:
                            v.wait_ge(sTE.h, kn_mm_tok[n - 2])
                        v.tensor_tensor(out=sq16[fb][:], in0=krope[fb][:],
                                        in1=krope[fb][:], op=ALU.mult)
                        krope_tok[n] = sVE.inc(v.drain())
                    blk.vector(vk)

                    def syk(sy, j=j, ti=ti, n=n, fb=fb):
                        sy.wait_ge(sVE.h, krope_tok[n])
                        kst_tok[n] = sDST.dinc(sy.dma_start(
                            out=kscr[j][:, ti * 512:(ti + 1) * 512], in_=krope[fb][:]))
                    blk.sync(syk)

                    def t2(t, n=n, fb=fb):
                        t.wait_ge(sVE.h, krope_tok[n])
                        if n - 1 in kn_v_tok:
                            t.wait_ge(sVE.h, kn_v_tok[n - 1])
                        kn_mm_tok[n] = sTE.inc(t.matmul(
                            kn_ps[:], ones128b_sb[:], sq16[fb][:],
                            start=True, stop=True))
                    blk.tensor(t2)

                    def v2(v, n=n):
                        v.wait_ge(sTE.h, kn_mm_tok[n])
                        v.wait_ge(sGP.h, tok_gz)
                        v.tensor_reduce(out=kn_red[:], in_=kn_ps[:], axis=AX.X,
                                        op=ALU.max)
                        v.drain()
                        kn_v_tok[n] = sVE.inc(v.tensor_tensor(
                            out=maxk2[:], in0=maxk2[:], in1=kn_red[:], op=ALU.max))
                        v.drain()
                    blk.vector(v2)

                    def tv(t, j=j, ti=ti, n=n):
                        if n - 1 in vev_tok:
                            t.wait_ge(sAC.h, vev_tok[n - 1])
                        for dc in range(NDC):
                            mm = t.matmul(
                                kps[1][:],
                                wv_sb[:, dc * 512 + j * 128: dc * 512 + (j + 1) * 128],
                                xn_fm[ti % 2][:, dc * 512:(dc + 1) * 512],
                                start=(dc == 0), stop=(dc == NDC - 1))
                        sTE.inc(mm)
                        xn_used[ti] = sTE.v
                    blk.tensor(tv)
                    tok_vmm = sTE.v

                    def sv(sc, j=j, n=n, fb=fb):
                        sc.wait_ge(sTE.h, tok_vmm)
                        if n - 2 in vtp_tok:
                            sc.wait_ge(sTE.h, vtp_tok[n - 2])
                        vev_tok[n] = sAC.inc(sc.activation(
                            out=vf16[fb][:], in_=kps[1][:], func=AF.Identity,
                            bias=bv_sb[:, j:j + 1], scale=1.0))
                    blk.scalar(sv)

                    def tv2(t, j=j, n=n, fb=fb):
                        t.wait_ge(sAC.h, vev_tok[n])
                        if n - 1 in vts_tok:
                            t.wait_ge(sVE.h, vts_tok[n - 1])
                        for q in range(4):
                            mm = t.transpose(tpb[:, q * 128:(q + 1) * 128],
                                             vf16[fb][:, q * 128:(q + 1) * 128],
                                             identb_sb[:])
                        vtp_tok[n] = sTE.inc(mm)
                    blk.tensor(tv2)

                    def vv(v, j=j, n=n, fb=fb):
                        v.wait_ge(sTE.h, vtp_tok[n])
                        if n - 2 in vst_tok:
                            v.wait_ge(sDST.h, vst_tok[n - 2])
                        v.tensor_copy(out=vt_stage[fb][:], in_=tpb[:, 0:512])
                        vts_tok[n] = sVE.inc(v.drain())
                    blk.vector(vv)

                    def syv(sy, j=j, ti=ti, n=n, fb=fb):
                        sy.wait_ge(sVE.h, vts_tok[n])
                        vst_tok[n] = sDST.dinc(sy.dma_start(
                            out=vtscr[j][:, ti * 512:(ti + 1) * 512],
                            in_=vt_stage[fb][:]))
                    blk.sync(syv)

            def sc_c(sc):
                sc.wait_ge(sVE.h, sVE.v)
                sAC.inc(sc.sqrt(out=ctmp[:], in_=maxq2[:]))
                sAC.inc(sc.sqrt(out=ctmp2[:], in_=maxk2[:]))
            blk.scalar(sc_c)
            tok_cs = sAC.v

            def v_c(v):
                v.wait_ge(sAC.h, tok_cs)
                v.tensor_tensor(out=ctmp[:], in0=ctmp[:], in1=ctmp2[:], op=ALU.mult)
                v.drain()
                sVE.inc(v.tensor_scalar(out=ctmp[:], in0=ctmp[:], scalar1=-1.0,
                                        scalar2=None, op0=ALU.mult))
                v.drain()
            blk.vector(v_c)
            tok_cn = sVE.v

            def t_c(t):
                t.wait_ge(sVE.h, tok_cn)
                sTE.inc(t.matmul(cb_ps[:], onesrowf_sb[0:1, :], ctmp[:],
                                 start=True, stop=True))
            blk.tensor(t_c)
            tok_cb = sTE.v

            def v_c2(v):
                v.wait_ge(sTE.h, tok_cb)
                sVE.inc(v.tensor_copy(out=cbias[:], in_=cb_ps[:]))
            blk.vector(v_c2)

            def syf(sy):
                sy.wait_ge(sDST.h, sDST.v)
                sy.wait_ge(sVE.h, sVE.v)
            blk.sync(syf)

    with ExitStack() as ps:
        def sb(name, shape, dt=F32):
            return ps.enter_context(nc.sbuf_tensor(name, shape, dt))
        wb = [sb(f"qwb{i}", [128, 2048], BF16) for i in range(2)]
        aw = sb("qaw", [128, 2048], BF16)
        sgn = sb("qsgn", [128, 2048], BF16)
        sc16 = sb("qsc16", [64, 2048], BF16)
        thr = sb("qthr", [128, 2048])
        scbc = sb("qscbc", [128, 2048], BF16)
        mgt = sb("qmgt", [128, 2048], BF16)
        qv = sb("qqv", [128, 2048], BF16)
        wqv = sb("qwqv", [128, 2048], BF16)
        d1t = sb("qd1", [128, 2048], BF16)
        rr = [sb(f"qrr{i}", [128, 2048], BF16) for i in range(2)]

        scl_ps = ps.enter_context(nc.psum_tensor("qscl_ps", [2, 2048], F32))
        bc_ps = ps.enter_context(nc.psum_tensor("qbc_ps", [128, 2048], F32))

        tiles = []
        for dc in range(NDC):
            tiles.append(("gu", dc, 0))
        for hcl in range(8):
            tiles.append(("p", hcl, 0))
        NT = len(tiles)

        tok_ld = [None] * NT
        tok_sums = [None] * NT
        tok_scev = [None] * NT
        tok_bc = [None] * NT
        tok_bcev = [None] * NT
        tok_thr = [None] * NT
        tok_sgn = [None] * NT
        tok_aw = [None] * NT
        tok_q = [None] * NT
        tok_wq = [None] * NT
        tok_r = [None] * NT
        tok_st = [None] * NT

        with nc.Block("q") as blk:
            for ti2, (kind, a, b_) in enumerate(tiles):
                par = ti2 % 2

                def gl(g, ti2=ti2, kind=kind, a=a, b_=b_, par=par):
                    if kind == "gu":
                        src = wgut[a * 128:(a + 1) * 128, :]
                    else:
                        src = wpt[a * 128:(a + 1) * 128, :]
                    if ti2 >= 2:
                        g.wait_ge(sVE.h, tok_r[ti2 - 2])
                    tok_ld[ti2] = sDWT.dinc(g.dma_start(out=wb[par][:], in_=src))
                blk.gpsimd(gl)

                def s1(sc, ti2=ti2, par=par):
                    sc.wait_ge(sDWT.h, tok_ld[ti2])
                    if ti2 > 0:
                        sc.wait_ge(sTE.h, tok_sums[ti2 - 1])
                        sc.wait_ge(sVE.h, tok_q[ti2 - 1])
                    tok_aw[ti2] = sAC.inc(sc.activation(out=aw[:], in_=wb[par][:],
                                                        func=AF.Abs))
                    tok_sgn[ti2] = sAC.inc(sc.sign(out=sgn[:], in_=wb[par][:]))
                blk.scalar(s1)

                def t1(t, ti2=ti2):
                    t.wait_ge(sAC.h, tok_aw[ti2])
                    if ti2 > 0:
                        t.wait_ge(sVE.h, tok_scev[ti2 - 1])
                    for i in range(4):
                        mm = t.matmul(scl_ps[:, i * 512:(i + 1) * 512], ind2_sb[:],
                                      aw[:, i * 512:(i + 1) * 512],
                                      start=True, stop=True)
                    tok_sums[ti2] = sTE.inc(mm)
                blk.tensor(t1)

                def v1(v, ti2=ti2):
                    v.wait_ge(sTE.h, tok_sums[ti2])
                    if ti2 > 0:
                        v.wait_ge(sTE.h, tok_bc[ti2 - 1])
                    v.tensor_copy(out=sc16[0:2, :], in_=scl_ps[:])
                    tok_scev[ti2] = sVE.inc(v.tensor_scalar(
                        out=sc16[0:2, :], in0=sc16[0:2, :], scalar1=1e-8,
                        scalar2=None, op0=ALU.max))
                blk.vector(v1)

                def t2(t, ti2=ti2):
                    t.wait_ge(sVE.h, tok_scev[ti2])
                    if ti2 > 0:
                        t.wait_ge(sAC.h, tok_bcev[ti2 - 1])
                    for i in range(4):
                        mm = t.matmul(bc_ps[:, i * 512:(i + 1) * 512], e2_sb[:],
                                      sc16[0:2, i * 512:(i + 1) * 512],
                                      start=True, stop=True)
                    tok_bc[ti2] = sTE.inc(mm)
                blk.tensor(t2)

                def s2(sc, ti2=ti2):
                    sc.wait_ge(sTE.h, tok_bc[ti2])
                    if ti2 > 0:
                        sc.wait_ge(sVE.h, tok_wq[ti2 - 1])
                    tok_thr[ti2] = sAC.inc(sc.activation(out=thr[:], in_=bc_ps[:],
                                                         func=AF.Copy, scale=QTHRESH))
                    tok_bcev[ti2] = sAC.inc(sc.activation(out=scbc[:], in_=bc_ps[:],
                                                          func=AF.Copy))
                blk.scalar(s2)

                def v2(v, ti2=ti2, par=par):
                    v.wait_ge(sAC.h, tok_bcev[ti2])
                    v.wait_ge(sAC.h, tok_sgn[ti2])
                    if ti2 >= 2:
                        v.wait_ge(sDST.h, tok_st[ti2 - 2])
                    v.tensor_tensor(out=mgt[:], in0=aw[:], in1=thr[:], op=ALU.is_gt)
                    tok_q[ti2] = sVE.inc(v.tensor_tensor(out=qv[:], in0=mgt[:],
                                                         in1=sgn[:], op=ALU.mult))
                    tok_wq[ti2] = sVE.inc(v.tensor_tensor(out=wqv[:], in0=qv[:],
                                                          in1=scbc[:], op=ALU.mult))
                    v.tensor_tensor(out=d1t[:], in0=wqv[:], in1=wb[par][:],
                                    op=ALU.subtract)
                    tok_r[ti2] = sVE.inc(v.tensor_tensor(out=rr[par][:], in0=wb[par][:],
                                                         in1=d1t[:], op=ALU.add))
                blk.vector(v2)

                def sy1(sy, ti2=ti2, kind=kind, a=a, b_=b_, par=par):
                    sy.wait_ge(sVE.h, tok_r[ti2])
                    src = rr[par][:].rearrange("p (i f) -> p i f", i=16)
                    if kind == "gu":
                        dst = wguq_sh[:, :, a, :].transpose([1, 0, 2])
                    else:
                        dst = wpq_sh[:, :, a, :].transpose([1, 0, 2])
                    tok_st[ti2] = sDST.dinc(sy.dma_start(out=dst, in_=src))
                blk.sync(sy1)

            def gcc(g):
                g.wait_ge(sDST.h, sDST.v)
                g.collective_compute(
                    "AllGather", ALU.bypass,
                    replica_groups=[list(range(8))],
                    ins=[wguq_sh.rearrange("a p c f -> a (p c f)").opt()],
                    outs=[wguq.rearrange("a p c f -> a (p c f)").opt()],
                ).then_inc(sems["gp"].h, 1)
                sems["gp"].v += 1
                g.collective_compute(
                    "AllGather", ALU.bypass,
                    replica_groups=[list(range(8))],
                    ins=[wpq_sh.rearrange("d p c f -> (d p c f)").unsqueeze(0).opt()],
                    outs=[wpq.rearrange("s d p c f -> s (d p c f)").opt()],
                ).then_inc(sems["gp"].h, 1)
                sems["gp"].v += 1
                g.wait_ge(sems["gp"].h, sems["gp"].v)
            blk.gpsimd(gcc)

            def syf(sy):
                sy.wait_ge(sDST.h, sDST.v)
                sy.wait_ge(sems["gp"].h, sems["gp"].v)
            blk.sync(syf)

    with ExitStack() as ps:
        def sb(name, shape, dt=F32):
            return ps.enter_context(nc.sbuf_tensor(name, shape, dt))
        kbuf = [sb(f"bk{i}", [128, T], BF16) for i in range(2)]
        vtbuf = [sb(f"bvt{i}", [128, T], BF16) for i in range(2)]
        qbuf = sb("bq", [128, HQ * 512], BF16)
        mask_sb = sb("bmask", [128, 16 * 512], BF16)
        pbuf = sb("bp", [128, 2048], BF16)
        o_all = sb("boall", [128, HQ * 512], BF16)
        den2 = sb("bden2", [64, 8 * 512])
        wo_sb = [sb(f"bwo{i}", [128, NDC * 512], BF16) for i in range(2)]
        sf_ev = sb("bsfev", [128, 512])
        s_tm = sb("bstm", [128, 4 * C])

        qk_ps = ps.enter_context(nc.psum_tensor("bqk", [128, 2048], F32))
        pv_ps = [ps.enter_context(nc.psum_tensor(f"bpv{i}", [128, 512], F32))
                 for i in range(2)]
        den_ps = ps.enter_context(nc.psum_tensor("bdenp", [128, 512], F32))
        tpf_ps = den_ps

        sK = [sems["dk0"], sems["dk1"]]
        sDQ = sems["dq"]
        it_list = [(s, j) for s in range(NSLOT) for j in range(HKV)]
        kv_tok = {}
        q_tok = {}
        it_done = {}

        with nc.Block("b") as blk:
            def pro(sy):
                E0 = 2048
                sK[0].dinc(sy.dma_start(out=kbuf[0][:, 0:E0], in_=kscr[0][:, 0:E0]))
                sK[0].dinc(sy.dma_start(out=vtbuf[0][:, 0:E0], in_=vtscr[0][:, 0:E0]))
                sK[1].dinc(sy.dma_start(out=kbuf[1][:, 0:E0], in_=kscr[1][:, 0:E0]))
                sK[1].dinc(sy.dma_start(out=vtbuf[1][:, 0:E0], in_=vtscr[1][:, 0:E0]))
                sDQ.dinc(sy.dma_start(out=qbuf[:],
                                      in_=qscr[0].rearrange("p g f -> p (g f)")))
                sDQ.dinc(sy.dma_start(
                    out=mask_sb[:].rearrange("p (i f) -> p i f", i=16),
                    in_=masks[0].transpose([1, 0, 2])))
            blk.sync(pro)
            kv_tok[0] = (sK[0], sK[0].v)
            kv_tok[1] = (sK[1], sK[1].v)
            q_tok[0] = sDQ.v

            pair_done_ve = [None]
            wo_done_te = [None]
            prev_s_store = [None]
            norm_done_ve = [None]

            for n, (s, j) in enumerate(it_list):
                if j == 0:
                    soft_barrier(blk)
                E = 512 * (4 * s + 4)
                nsub = E // 128
                par = n % 2
                masked = list(range(max(0, nsub - 16), nsub))

                for p in range(2):
                    exp_base = sAC.v
                    pvd_base = sTE.v
                    msk_base = sVE.v
                    msk_tok = {}
                    mi = 0
                    for i in masked:
                        msk_tok[i] = msk_base + mi + 1
                        mi += 1

                    def tok_qk(i, pvd_base=pvd_base):
                        return pvd_base + (1 if i == 0 else 2 * i)

                    def tok_pvd(i, pvd_base=pvd_base, nsub=nsub):
                        return pvd_base + min(2 * i + 3, 2 * nsub)

                    def tok_exp(i, exp_base=exp_base):
                        return exp_base + i + 1

                    def tb(t, s=s, j=j, p=p, E=E, nsub=nsub, par=par, n=n,
                           exp_base=exp_base, pvd_base=pvd_base, msk_tok=msk_tok):
                        ksem, ktok = kv_tok[n]
                        t.wait_ge(ksem.h, ktok)
                        if p == 0:
                            t.wait_ge(sDQ.h, q_tok[s])
                        def emit_qk(i):
                            ip = i % 2
                            if i >= 2:
                                t.wait_ge(sAC.h, tok_exp(i - 2))
                            elif exp_base + i - 1 >= 1:
                                t.wait_ge(sAC.h, exp_base + i - 1)
                            t.ldweights(kbuf[par][:, i * 128:(i + 1) * 128])
                            for gg in range(2):
                                h = (2 * p + gg) * 4 + j
                                mm = t.matmul(
                                    qk_ps[:, ip * 1024 + gg * 512:
                                          ip * 1024 + (gg + 1) * 512],
                                    kbuf[par][:, i * 128:(i + 1) * 128],
                                    qbuf[:, h * 512:(h + 1) * 512],
                                    start=True, stop=True)
                            sTE.inc(mm)
                        emit_qk(0)
                        for i in range(nsub):
                            ip = i % 2
                            if i + 1 < nsub:
                                emit_qk(i + 1)
                            if i == 0 and pair_done_ve[0] is not None:
                                t.wait_ge(sVE.h, pair_done_ve[0])
                            if i in msk_tok:
                                t.wait_ge(sVE.h, msk_tok[i])
                            else:
                                t.wait_ge(sAC.h, tok_exp(i))
                            t.ldweights(vtbuf[par][:, i * 128:(i + 1) * 128])
                            for gg in range(2):
                                t.matmul(pv_ps[gg][:],
                                         vtbuf[par][:, i * 128:(i + 1) * 128],
                                         pbuf[:, ip * 1024 + gg * 512:
                                              ip * 1024 + (gg + 1) * 512],
                                         start=(i == 0), stop=(i == nsub - 1))
                            t.ldweights(ones128b_sb[:], tile_position=(0, 0))
                            t.ldweights(ones128b_sb[:], tile_position=(0, 32))
                            for gg in range(2):
                                mm = t.matmul(
                                    den_ps[gg * 32:gg * 32 + 1, :],
                                    ones128b_sb[:],
                                    pbuf[:, ip * 1024 + gg * 512:
                                         ip * 1024 + (gg + 1) * 512],
                                    start=(i == 0), stop=(i == nsub - 1),
                                    tile_position=(0, gg * 32))
                            sTE.inc(mm)
                    blk.tensor(tb)
                    assert sTE.v == pvd_base + 2 * nsub

                    def sb_(sc, nsub=nsub, exp_base=exp_base, pvd_base=pvd_base):
                        for i in range(nsub):
                            ip = i % 2
                            sc.wait_ge(sTE.h, tok_qk(i))
                            if i >= 2:
                                sc.wait_ge(sTE.h, tok_pvd(i - 2))
                            elif pvd_base + i - 1 >= 1:
                                sc.wait_ge(sTE.h, pvd_base + i - 1)
                            sAC.inc(sc.activation(
                                out=pbuf[:, ip * 1024:(ip + 1) * 1024],
                                in_=qk_ps[:, ip * 1024:(ip + 1) * 1024],
                                func=AF.Exp, bias=cbias[:], scale=1.0))
                    blk.scalar(sb_)
                    assert sAC.v == exp_base + nsub

                    def vb(v, s=s, nsub=nsub, exp_base=exp_base, masked=masked):
                        v.wait_ge(sDQ.h, q_tok[s])
                        for i in masked:
                            ip = i % 2
                            idx = i - (nsub - 16)
                            v.wait_ge(sAC.h, tok_exp(i))
                            v.tensor_tensor(
                                out=pbuf[:, ip * 1024:ip * 1024 + 512],
                                in0=pbuf[:, ip * 1024:ip * 1024 + 512],
                                in1=mask_sb[:, idx * 512:(idx + 1) * 512],
                                op=ALU.mult)
                            sVE.inc(v.tensor_tensor(
                                out=pbuf[:, ip * 1024 + 512:ip * 1024 + 1024],
                                in0=pbuf[:, ip * 1024 + 512:ip * 1024 + 1024],
                                in1=mask_sb[:, idx * 512:(idx + 1) * 512],
                                op=ALU.mult))
                    blk.vector(vb)
                    assert sVE.v == msk_base + len(masked)

                    tok_all_pvd = sTE.v

                    def vep(v, s=s, j=j, p=p, tok_all_pvd=tok_all_pvd):
                        v.wait_ge(sTE.h, tok_all_pvd)
                        if wo_done_te[0] is not None:
                            v.wait_ge(sTE.h, wo_done_te[0])
                        col = (p * 4 + j) * 512
                        v.tensor_copy(out=den2[0:1, col:col + 512],
                                      in_=den_ps[0:1, :])
                        v.tensor_copy(out=den2[32:33, col:col + 512],
                                      in_=den_ps[32:33, :])
                        for gg in range(2):
                            h = (2 * p + gg) * 4 + j
                            mm = v.tensor_copy(out=o_all[:, h * 512:(h + 1) * 512],
                                               in_=pv_ps[gg][:])
                        sVE.inc(mm)
                    blk.vector(vep)
                    pair_done_ve[0] = sVE.v

                it_done[n] = sTE.v

                if n + 2 < len(it_list):
                    s2_, j2_ = it_list[n + 2]
                    E2 = 512 * (4 * s2_ + 4)

                    def syk(sy, s2_=s2_, j2_=j2_, E2=E2, par=par, n=n):
                        sy.wait_ge(sTE.h, it_done[n])
                        sK[par].dinc(sy.dma_start(out=kbuf[par][:, 0:E2],
                                                  in_=kscr[j2_][:, 0:E2]))
                        sK[par].dinc(sy.dma_start(out=vtbuf[par][:, 0:E2],
                                                  in_=vtscr[j2_][:, 0:E2]))
                    blk.sync(syk)
                    kv_tok[n + 2] = (sK[par], sK[par].v)

                if j == HKV - 1:
                    tok_pairs = sVE.v

                    def vnr(v, s=s, tok_pairs=tok_pairs):
                        v.wait_ge(sVE.h, tok_pairs)
                        v.reciprocal(out=den2[0:1, :], in_=den2[0:1, :])
                        v.drain()
                        v.reciprocal(out=den2[32:33, :], in_=den2[32:33, :])
                        sVE.inc(v.drain())
                    blk.vector(vnr)
                    tok_rd = sVE.v

                    nrm_mm_tok = {}
                    nrm_mu_tok = {}
                    for rnd in range(4):
                        def tnr(t, s=s, rnd=rnd):
                            t.wait_ge(sVE.h, tok_rd)
                            if rnd == 0:
                                pass
                            else:
                                t.wait_ge(sVE.h, nrm_mu_tok[rnd - 1])
                            for q4 in range(4):
                                h = rnd * 4 + q4
                                gg = h // 8
                                ggr = (h // 4) % 2
                                pcol = ((h // 8) * 4 + (h % 4)) * 512
                                mm = t.matmul(qk_ps[:, q4 * 512:(q4 + 1) * 512],
                                              onesrowf_sb[ggr * 32:ggr * 32 + 1, :],
                                              den2[ggr * 32:ggr * 32 + 1,
                                                   pcol:pcol + 512],
                                              start=True, stop=True)
                            nrm_mm_tok[rnd] = sTE.inc(mm)
                        blk.tensor(tnr)

                        def vnr2(v, s=s, rnd=rnd):
                            v.wait_ge(sTE.h, nrm_mm_tok[rnd])
                            v.tensor_tensor(
                                out=o_all[:, rnd * 2048:(rnd + 1) * 2048],
                                in0=o_all[:, rnd * 2048:(rnd + 1) * 2048],
                                in1=qk_ps[:], op=ALU.mult)
                            nrm_mu_tok[rnd] = sVE.inc(v.drain())
                        blk.vector(vnr2)
                    tok_oall = sVE.v
                    norm_done_ve[0] = tok_oall
                    if DEBUG:
                        def syo(sy, s=s):
                            sy.wait_ge(sVE.h, tok_oall)
                            sDST.dinc(sy.dma_start(out=oscr[s], in_=o_all[:]))
                        blk.sync(syo)

                    wo_tok = {}
                    wo_used = {}
                    sev_tok = {}
                    tp_tok = {}
                    tr_tok = {}

                    def gw0(g, s=s):
                        if wo_done_te[0] is not None:
                            g.wait_ge(sTE.h, wo_done_te[0])
                        wov = wo.rearrange("(dc p) m -> p dc m", p=128)
                        for blki in range(2):
                            wo_tok[blki] = sDWT.dinc(g.dma_start(
                                out=wo_sb[blki][:].rearrange(
                                    "p (dc f) -> p dc f", dc=NDC),
                                in_=wov[:, :, blki * 512:(blki + 1) * 512]))
                    blk.gpsimd(gw0)

                    for blki in range(4):
                        bpar = blki % 2
                        if blki >= 2:
                            def gw(g, blki=blki, bpar=bpar, s=s):
                                g.wait_ge(sTE.h, wo_used[blki - 2])
                                wov = wo.rearrange("(dc p) m -> p dc m", p=128)
                                wo_tok[blki] = sDWT.dinc(g.dma_start(
                                    out=wo_sb[bpar][:].rearrange(
                                        "p (dc f) -> p dc f", dc=NDC),
                                    in_=wov[:, :, blki * 512:(blki + 1) * 512]))
                            blk.gpsimd(gw)

                        ppieces = [qk_ps[:, 0:512], qk_ps[:, 512:1024],
                                   qk_ps[:, 1024:1536], qk_ps[:, 1536:2048]]
                        mm_base = sTE.v

                        def tw(t, s=s, blki=blki, bpar=bpar, tok_oall=tok_oall):
                            t.wait_ge(sDWT.h, wo_tok[blki])
                            t.wait_ge(sVE.h, tok_oall)
                            for dl in range(4):
                                if blki > 0:
                                    t.wait_ge(sAC.h, sev_tok[(blki - 1, dl)])
                                for dc in range(NDC):
                                    mm = t.matmul(
                                        ppieces[dl],
                                        wo_sb[bpar][:, dc * 512 + dl * 128:
                                                    dc * 512 + (dl + 1) * 128],
                                        o_all[:, dc * 512:(dc + 1) * 512],
                                        start=(dc == 0), stop=(dc == NDC - 1))
                                sTE.inc(mm)
                            wo_used[blki] = sTE.v
                        blk.tensor(tw)
                        assert sTE.v == mm_base + 4

                        tp_base = sTE.v
                        for dl in range(4):
                            tp_tok[(blki, dl)] = tp_base + dl + 1
                        ev_base = sAC.v
                        for dl in range(4):
                            sev_tok[(blki, dl)] = ev_base + dl + 1
                        trv_base = sVE.v
                        for dl in range(4):
                            tr_tok[(blki, dl)] = trv_base + dl + 1

                        def sw(sc, s=s, blki=blki, mm_base=mm_base,
                               ppieces=ppieces):
                            for dl in range(4):
                                dout = blki * 4 + dl
                                sc.wait_ge(sTE.h, mm_base + dl + 1)
                                if (blki, dl) != (0, 0):
                                    prev = (blki, dl - 1) if dl > 0 else (blki - 1, 3)
                                    sc.wait_ge(sTE.h, tp_tok[prev])
                                sAC.inc(sc.activation(
                                    out=sf_ev[:], in_=ppieces[dl], func=AF.Identity,
                                    bias=bo_sb[:, dout:dout + 1], scale=1.0))
                        blk.scalar(sw)
                        assert sAC.v == ev_base + 4

                        def tw2(t, s=s, blki=blki):
                            for dl in range(4):
                                t.wait_ge(sAC.h, sev_tok[(blki, dl)])
                                if (blki, dl) != (0, 0):
                                    prev = (blki, dl - 1) if dl > 0 else (blki - 1, 3)
                                    t.wait_ge(sVE.h, tr_tok[prev])
                                else:
                                    t.wait_ge(sVE.h, pair_done_ve[0])
                                for u in range(4):
                                    mm = t.transpose(tpf_ps[:, u * 128:(u + 1) * 128],
                                                     sf_ev[:, u * 128:(u + 1) * 128],
                                                     identf_sb[:])
                                sTE.inc(mm)
                        blk.tensor(tw2)
                        assert sTE.v == tp_base + 4

                        def vw(v, s=s, blki=blki):
                            stv = s_tm[:].rearrange("p (u f) -> p u f", u=4)
                            for dl in range(4):
                                dout = blki * 4 + dl
                                v.wait_ge(sTE.h, tp_tok[(blki, dl)])
                                if blki == 0 and dl == 0 and prev_s_store[0] is not None:
                                    v.wait_ge(sDST.h, prev_s_store[0])
                                dst = stv[:, :, dout * 128:(dout + 1) * 128]
                                if dl == 3:
                                    v.tensor_copy(
                                        out=dst, in_=tpf_ps[:].rearrange(
                                            "p (u f) -> p u f", u=4))
                                    sVE.inc(v.drain())
                                else:
                                    sVE.inc(v.tensor_copy(
                                        out=dst, in_=tpf_ps[:].rearrange(
                                            "p (u f) -> p u f", u=4)))
                        blk.vector(vw)
                        assert sVE.v == trv_base + 4

                    def sys(sy, s=s):
                        sy.wait_ge(sVE.h, tr_tok[(3, 3)])
                        for u in range(4):
                            sDST.dinc(sy.dma_start(out=sscr[s][u],
                                                   in_=s_tm[:, u * C:(u + 1) * C]))
                    blk.sync(sys)
                    prev_s_store[0] = sDST.v
                    wo_done_te[0] = wo_used[3]

                    if s + 1 < NSLOT:
                        def syq(sy, s=s):
                            sy.wait_ge(sTE.h, it_done[n])
                            sDQ.dinc(sy.dma_start(
                                out=qbuf[:],
                                in_=qscr[s + 1].rearrange("p g f -> p (g f)")))
                            sDQ.dinc(sy.dma_start(
                                out=mask_sb[:].rearrange("p (i f) -> p i f", i=16),
                                in_=masks[s + 1].transpose([1, 0, 2])))
                        blk.sync(syq)
                        q_tok[s + 1] = sDQ.v

            def syf(sy):
                sy.wait_ge(sDST.h, sDST.v)
            blk.sync(syf)

    with ExitStack() as ps:
        def sb(name, shape, dt=F32):
            return ps.enter_context(nc.sbuf_tensor(name, shape, dt))
        xnb = sb("cxnb", [128, NDC * 512], BF16)
        wg_g = [sb(f"cwg{i}", [128, 2048], BF16) for i in range(2)]
        wg_u = [sb(f"cwu{i}", [128, 2048], BF16) for i in range(2)]
        sg = [sb(f"csg{i}", [128, 512]) for i in range(2)]
        hsq = [sb(f"chsq{i}", [128, 512], BF16) for i in range(2)]
        h_all = sb("chall", [128, NHC * 512], BF16)
        wp_sb = [sb(f"cwp{i}", [128, 32 * 128], BF16) for i in range(2)]
        rs1 = sb("crs1", [1, 512])
        rs2 = sb("crs2", [1, 512])
        rdbc = sb("crdbc", [128, 512])
        tff = sb("ctff", [128, 512])
        gfm = [sb(f"cgfm{i}", [128, 512]) for i in range(2)]
        g_tm = sb("cgtm", [128, 4 * C])
        xs = sb("cxs", [128, C])
        ss = sb("css", [128, C])
        ys = [sb(f"cys{i}", [128, C]) for i in range(2)]

        gps = [ps.enter_context(nc.psum_tensor(f"cgps{i}", [128, 512], F32))
               for i in range(2)]
        ups = [ps.enter_context(nc.psum_tensor(f"cups{i}", [128, 512], F32))
               for i in range(2)]
        h2_ps = ps.enter_context(nc.psum_tensor("ch2", [1, 512], F32))
        pps = [ps.enter_context(nc.psum_tensor(f"cpps{i}", [128, 512], F32))
               for i in range(2)]
        tpf2_ps = ps.enter_context(nc.psum_tensor("ctpf2", [128, 512], F32))

        sDG, sDP, sDXS, sDY = sems["dg"], sems["dp"], sems["dxs"], sems["dy"]

        with nc.Block("c") as blk:
            prev_slot_gu_te = [None]
            prev_slot_pj_te = [None]
            prev_slot_cb_ve = [None]
            prev_yst = [None, None]

            for s in range(NSLOT):
                soft_barrier(blk)

                def syxn(sy, s=s):
                    if prev_slot_gu_te[0] is not None:
                        sy.wait_ge(sTE.h, prev_slot_gu_te[0])
                    sDQ_ = sems["dq"]
                    sDQ_.dinc(sy.dma_start(
                        out=xnb[:], in_=xnscr[s].rearrange("p dc f -> p (dc f)")))
                blk.sync(syxn)
                tok_xnb = sems["dq"].v

                gld_tok = {}
                gmm_tok = {}
                umm_tok = {}
                silu_tok = {}
                h_tok = {}
                hsq_tok = {}
                h2_tok = {}

                def syg0(sy, s=s):
                    if prev_slot_gu_te[0] is not None:
                        sy.wait_ge(sTE.h, prev_slot_gu_te[0])
                    guv = wguq.rearrange("fc p dc f -> fc p (dc f)")
                    sDG.dinc(sy.dma_start(out=wg_g[0][:], in_=guv[0]))
                    gld_tok[0] = sDG.dinc(sy.dma_start(out=wg_u[0][:], in_=guv[64]))
                blk.sync(syg0)

                for fc in range(NHC):
                    fpar = fc % 2
                    if fc + 1 < NHC:
                        def syg(sy, s=s, fc=fc):
                            if fc >= 1:
                                sy.wait_ge(sTE.h, umm_tok[fc - 1])
                            guv = wguq.rearrange("fc p dc f -> fc p (dc f)")
                            sDG.dinc(sy.dma_start(out=wg_g[(fc + 1) % 2][:],
                                                  in_=guv[fc + 1]))
                            gld_tok[fc + 1] = sDG.dinc(sy.dma_start(
                                out=wg_u[(fc + 1) % 2][:], in_=guv[64 + fc + 1]))
                        blk.sync(syg)

                    def tg(t, s=s, fc=fc, fpar=fpar):
                        t.wait_ge(sDG.h, gld_tok[fc])
                        t.wait_ge(sems["dq"].h, tok_xnb)
                        if prev_slot_pj_te[0] is not None and fc == 0:
                            pass
                        if fc >= 2:
                            t.wait_ge(sAC.h, silu_tok[fc - 2])
                            t.wait_ge(sVE.h, h_tok[fc - 2])
                        for dc in range(NDC):
                            mm = t.matmul(gps[fpar][:],
                                          wg_g[fpar][:, dc * 128:(dc + 1) * 128],
                                          xnb[:, dc * 512:(dc + 1) * 512],
                                          start=(dc == 0), stop=(dc == NDC - 1))
                        gmm_tok[fc] = sTE.inc(mm)
                        for dc in range(NDC):
                            mm = t.matmul(ups[fpar][:],
                                          wg_u[fpar][:, dc * 128:(dc + 1) * 128],
                                          xnb[:, dc * 512:(dc + 1) * 512],
                                          start=(dc == 0), stop=(dc == NDC - 1))
                        umm_tok[fc] = sTE.inc(mm)
                        if fc >= 2:
                            t.wait_ge(sVE.h, hsq_tok[fc - 2])
                            h2_tok[fc - 2] = sTE.inc(t.matmul(
                                h2_ps[:], ones128b_sb[:], hsq[(fc - 2) % 2][:],
                                start=(fc - 2 == 0), stop=False))
                    blk.tensor(tg)

                    def sg_(sc, s=s, fc=fc, fpar=fpar):
                        sc.wait_ge(sTE.h, gmm_tok[fc])
                        if fc >= 2:
                            sc.wait_ge(sVE.h, h_tok[fc - 2])
                        silu_tok[fc] = sAC.inc(sc.activation(
                            out=sg[fpar][:], in_=gps[fpar][:], func=AF.Silu))
                    blk.scalar(sg_)

                    def vg(v, s=s, fc=fc, fpar=fpar):
                        v.wait_ge(sAC.h, silu_tok[fc])
                        v.wait_ge(sTE.h, umm_tok[fc])
                        if prev_slot_pj_te[0] is not None:
                            v.wait_ge(sTE.h, prev_slot_pj_te[0])
                        h_tok[fc] = sVE.inc(v.tensor_tensor(
                            out=h_all[:, fc * 512:(fc + 1) * 512],
                            in0=sg[fpar][:], in1=ups[fpar][:], op=ALU.mult))
                        if fc >= 1:
                            if fc >= 3:
                                v.wait_ge(sTE.h, h2_tok[fc - 3])
                            hsl = h_all[:, (fc - 1) * 512:fc * 512]
                            hsq_tok[fc - 1] = sVE.inc(v.tensor_tensor(
                                out=hsq[(fc - 1) % 2][:], in0=hsl, in1=hsl,
                                op=ALU.mult))
                    blk.vector(vg)

                def sgt(v, s=s):
                    v.wait_ge(sTE.h, h2_tok[NHC - 3])
                    hsl = h_all[:, (NHC - 1) * 512:NHC * 512]
                    hsq_tok[NHC - 1] = sVE.inc(v.tensor_tensor(
                        out=hsq[(NHC - 1) % 2][:], in0=hsl, in1=hsl, op=ALU.mult))
                blk.vector(sgt)

                def tgt(t, s=s):
                    t.wait_ge(sVE.h, hsq_tok[NHC - 2])
                    h2_tok[NHC - 2] = sTE.inc(t.matmul(
                        h2_ps[:], ones128b_sb[:], hsq[(NHC - 2) % 2][:],
                        start=False, stop=False))
                    t.wait_ge(sVE.h, hsq_tok[NHC - 1])
                    h2_tok[NHC - 1] = sTE.inc(t.matmul(
                        h2_ps[:], ones128b_sb[:], hsq[(NHC - 1) % 2][:],
                        start=False, stop=True))
                blk.tensor(tgt)
                tok_h2 = sTE.v

                def vr(v, s=s):
                    v.wait_ge(sTE.h, tok_h2)
                    sVE.inc(v.tensor_scalar(out=rs1[:], in0=h2_ps[:],
                                            scalar1=1.0 / HF, scalar2=RMS_EPS,
                                            op0=ALU.mult, op1=ALU.add))
                    v.drain()
                blk.vector(vr)
                tok_rs = sVE.v

                def sr(sc, s=s):
                    sc.wait_ge(sVE.h, tok_rs)
                    sAC.inc(sc.sqrt(out=rs2[:], in_=rs1[:]))
                blk.scalar(sr)
                tok_rsq = sAC.v

                def vr2(v, s=s):
                    v.wait_ge(sAC.h, tok_rsq)
                    sVE.inc(v.reciprocal(out=rs1[:], in_=rs2[:]))
                    v.drain()
                blk.vector(vr2)
                tok_rr = sVE.v

                def tr(t, s=s):
                    t.wait_ge(sVE.h, tok_rr)
                    sTE.inc(t.matmul(tpf2_ps[:], onesrowf_sb[0:1, :], rs1[:],
                                     start=True, stop=True))
                blk.tensor(tr)
                tok_rbc = sTE.v

                def vr3(v, s=s):
                    v.wait_ge(sTE.h, tok_rbc)
                    sVE.inc(v.tensor_copy(out=rdbc[:], in_=tpf2_ps[:]))
                blk.vector(vr3)
                tok_rdbc = sVE.v

                pld_tok = {}
                pmm_tok = {}
                half_mm_tok = {}
                tffv_tok = {}
                gsil_tok = {}
                gtp_tok = {}
                gtr_tok = {}
                def wp_src(do, hh):
                    return wpq[hh * 4:(hh + 1) * 4, do].transpose([1, 0, 2, 3])

                def wp_dst(buf):
                    return buf[:].rearrange("p (a b f) -> p a b f", a=4, b=8)

                def syp0(sy, s=s):
                    if prev_slot_pj_te[0] is not None:
                        sy.wait_ge(sTE.h, prev_slot_pj_te[0])
                    sDP.dinc(sy.dma_start(out=wp_dst(wp_sb[0]), in_=wp_src(0, 0)))
                    pld_tok[0] = sDP.v
                    sDP.dinc(sy.dma_start(out=wp_dst(wp_sb[1]), in_=wp_src(0, 1)))
                    pld_tok[1] = sDP.v
                blk.sync(syp0)

                for do in range(NDC):
                    ppar = do % 2
                    for hh in range(2):
                        un = 2 * do + hh
                        buf = wp_sb[un % 2]
                        if un + 2 < 2 * NDC:
                            do2, hh2 = (un + 2) // 2, (un + 2) % 2

                            def syp(sy, s=s, un=un, do2=do2, hh2=hh2):
                                sy.wait_ge(sTE.h, half_mm_tok[un])
                                sDP.dinc(sy.dma_start(
                                    out=wp_dst(wp_sb[un % 2]),
                                    in_=wp_src(do2, hh2)))
                                pld_tok[un + 2] = sDP.v

                        def tp(t, s=s, do=do, hh=hh, un=un, ppar=ppar, buf=buf):
                            t.wait_ge(sDP.h, pld_tok[un])
                            if hh == 0 and do >= 2:
                                t.wait_ge(sVE.h, tffv_tok[do - 2])
                            for hl in range(32):
                                hc = hh * 32 + hl
                                mm = t.matmul(pps[ppar][:],
                                              buf[:, hl * 128:(hl + 1) * 128],
                                              h_all[:, hc * 512:(hc + 1) * 512],
                                              start=(hc == 0), stop=(hc == NHC - 1))
                            half_mm_tok[un] = sTE.inc(mm)
                        blk.tensor(tp)
                        if un + 2 < 2 * NDC:
                            blk.sync(syp)
                    pmm_tok[do] = half_mm_tok[2 * do + 1]

                    def vp(v, s=s, do=do, ppar=ppar):
                        v.wait_ge(sTE.h, pmm_tok[do])
                        v.wait_ge(sVE.h, tok_rdbc)
                        if do >= 1:
                            v.wait_ge(sAC.h, gsil_tok[do - 1])
                        v.tensor_tensor(out=tff[:], in0=pps[ppar][:],
                                        in1=rdbc[:], op=ALU.mult)
                        tffv_tok[do] = sVE.inc(v.drain())
                    blk.vector(vp)

                    def sp(sc, s=s, do=do):
                        sc.wait_ge(sVE.h, tffv_tok[do])
                        if do >= 2:
                            sc.wait_ge(sTE.h, gtp_tok[do - 2])
                        gsil_tok[do] = sAC.inc(sc.activation(
                            out=gfm[do % 2][:], in_=tff[:], func=AF.Silu))
                    blk.scalar(sp)

                    def tp2(t, s=s, do=do):
                        t.wait_ge(sAC.h, gsil_tok[do])
                        if do >= 1:
                            t.wait_ge(sVE.h, gtr_tok[do - 1])
                        else:
                            t.wait_ge(sVE.h, tok_rdbc)
                        if prev_slot_cb_ve[0] is not None:
                            t.wait_ge(sVE.h, prev_slot_cb_ve[0])
                        for u in range(4):
                            mm = t.transpose(tpf2_ps[:, u * 128:(u + 1) * 128],
                                             gfm[do % 2][:, u * 128:(u + 1) * 128],
                                             identf_sb[:])
                        gtp_tok[do] = sTE.inc(mm)
                    blk.tensor(tp2)

                    def vp2(v, s=s, do=do):
                        gv = g_tm[:].rearrange("p (u f) -> p u f", u=4)
                        v.wait_ge(sTE.h, gtp_tok[do])
                        gtr_tok[do] = sVE.inc(v.tensor_copy(
                            out=gv[:, :, do * 128:(do + 1) * 128],
                            in_=tpf2_ps[:].rearrange("p (u f) -> p u f", u=4)))
                    blk.vector(vp2)

                prev_slot_gu_te[0] = umm_tok[NHC - 1]
                prev_slot_pj_te[0] = pmm_tok[NDC - 1]

                cmb_tok = {}
                yst_tok = {}
                for u in range(4):
                    def syc1(sy, s=s, u=u):
                        if u >= 1:
                            sy.wait_ge(sVE.h, cmb_tok[u - 1])
                        elif prev_slot_cb_ve[0] is not None:
                            sy.wait_ge(sVE.h, prev_slot_cb_ve[0])
                        sDXS.dinc(sy.dma_start(
                            out=xs[:], in_=xown[s * 512 + u * 128:
                                              s * 512 + (u + 1) * 128, :]))
                        sDXS.dinc(sy.dma_start(out=ss[:], in_=sscr[s][u]))
                    blk.sync(syc1)
                    tok_ld2 = sDXS.v

                    def vc(v, s=s, u=u, tok_ld2=tok_ld2):
                        v.wait_ge(sDXS.h, tok_ld2)
                        v.wait_ge(sVE.h, gtr_tok[NDC - 1])
                        if u >= 2:
                            v.wait_ge(sDY.h, yst_tok[u - 2])
                        elif prev_yst[u] is not None:
                            v.wait_ge(sDY.h, prev_yst[u])
                        v.tensor_tensor(out=ss[:], in0=ss[:],
                                        in1=g_tm[:, u * C:(u + 1) * C], op=ALU.mult)
                        v.drain()
                        v.tensor_tensor(out=ys[u % 2][:], in0=xs[:], in1=ss[:],
                                        op=ALU.add)
                        cmb_tok[u] = sVE.inc(v.drain())
                    blk.vector(vc)

                    def syy(sy, s=s, u=u):
                        sy.wait_ge(sVE.h, cmb_tok[u])
                        yst_tok[u] = sDY.dinc(sy.dma_start(
                            out=y[s * 512 + u * 128: s * 512 + (u + 1) * 128, :],
                            in_=ys[u % 2][:]))
                    blk.sync(syy)
                prev_slot_cb_ve[0] = cmb_tok[3]
                prev_yst = [yst_tok[2], yst_tok[3]]

            def syf(sy):
                sy.wait_ge(sDY.h, sems["dy"].v)
            blk.sync(syf)

    es.close()
    return nc


_NC_CACHE = None


def kernel(**inputs):
    global _NC_CACHE
    per_core = _host_prep(inputs)
    if _NC_CACHE is None:
        _NC_CACHE = build_nc()
    res = run_bass_kernel_spmd(_NC_CACHE, per_core, core_ids=list(range(8)))
    out = np.zeros((B, T, C), dtype=np.float32)
    for c in range(8):
        b, k = c // 4, c % 4
        yc = np.asarray(res.results[c]["y"])
        for s in range(NSLOT):
            r0 = (4 * s + k) * 512
            out[b, r0:r0 + 512] = yc[s * 512:(s + 1) * 512]
    return out


# version 16 (final)
# speedup vs baseline: 1.7136x; 1.1471x over previous
import numpy as np
import ml_dtypes
import concourse.bass as bass
import concourse.mybir as mybir
from concourse.bass_utils import run_bass_kernel_spmd
from contextlib import ExitStack

F32 = mybir.dt.float32
BF16 = mybir.dt.bfloat16
AF = mybir.ActivationFunctionType
ALU = mybir.AluOpType
AX = mybir.AxisListType

B, T, C = 2, 8192, 2048
HQ, HKV, HD = 16, 4, 128
HF = 8192
NDC = C // 128
NHC = HF // 128
NSLOT = 4
LN_EPS = 1e-5
RMS_EPS = float(np.finfo(np.float32).eps)
QTHRESH = 0.5009765625

DEBUG = False


class Sem:
    def __init__(self, nc, es, name):
        self.h = es.enter_context(nc.semaphore(name))
        self.v = 0

    def inc(self, inst, k=1):
        inst.then_inc(self.h, k)
        self.v += k
        return self.v

    def dinc(self, inst):
        return self.inc(inst, 16)



def _rope_perm():
    p = np.empty(128, dtype=np.int64)
    p[:64] = np.arange(64) * 2
    p[64:] = np.arange(64) * 2 + 1
    return p


def _rope_tables():
    i = np.arange(0, HD, 2, dtype=np.float32) / np.float32(HD)
    theta = (np.float64(1.0) / np.float64(10000.0) ** i.astype(np.float64)).astype(np.float32)
    t = np.arange(T, dtype=np.float32)
    ang = (t[:, None] * theta[None, :]).astype(np.float32)
    cos = np.cos(ang.astype(np.float64)).astype(np.float32)
    sin = np.sin(ang.astype(np.float64)).astype(np.float32)
    Ctab = np.ascontiguousarray(np.concatenate([cos.T, cos.T], axis=0))
    Stab = np.ascontiguousarray(np.concatenate([-sin.T, sin.T], axis=0))
    return Ctab, Stab


def _masks_for_core(k):
    out = np.zeros((NSLOT, 16, 128, 512), dtype=np.float32)
    for s in range(NSLOT):
        b = 4 * s + k
        Es = 512 * (4 * s + 4)
        q_abs = 512 * b + np.arange(512)
        for idx in range(16):
            k0 = Es - 2048 + idx * 128
            key_abs = k0 + np.arange(128)
            out[s, idx] = (key_abs[:, None] <= q_abs[None, :]).astype(np.float32)
    return np.ascontiguousarray(out.astype(ml_dtypes.bfloat16))


def _host_prep(inputs):
    x = np.asarray(inputs["L0_attn_x"], dtype=np.float32)
    Wq = np.asarray(inputs["L0_attn_Wq"], dtype=np.float32)
    bq = np.asarray(inputs["L0_attn_bq"], dtype=np.float32)
    Wk = np.asarray(inputs["L0_attn_Wk"], dtype=np.float32)
    bk = np.asarray(inputs["L0_attn_bk"], dtype=np.float32)
    Wv = np.asarray(inputs["L0_attn_Wv"], dtype=np.float32)
    bv = np.asarray(inputs["L0_attn_bv"], dtype=np.float32)
    Wo = np.asarray(inputs["L0_attn_Wo"], dtype=np.float32)
    bo = np.asarray(inputs["L0_attn_bo"], dtype=np.float32)
    Wgu = np.asarray(inputs["L0_ffn_w_gate_up"], dtype=np.float32)
    Wp = np.asarray(inputs["L0_ffn_w_proj"], dtype=np.float32)

    perm = _rope_perm()
    qperm = np.concatenate([h * 128 + perm for h in range(HQ)])
    kperm = np.concatenate([h * 128 + perm for h in range(HKV)])
    Ctab, Stab = _rope_tables()

    shared = {
        "wqp": np.ascontiguousarray(Wq[:, qperm]),
        "wkp": np.ascontiguousarray(Wk[:, kperm]),
        "wv": Wv,
        "wo": Wo,
        "bqt": np.ascontiguousarray(bq[qperm].reshape(HQ, 128).T),
        "bkt": np.ascontiguousarray(bk[kperm].reshape(HKV, 128).T),
        "bvt": np.ascontiguousarray(bv.reshape(HKV, 128).T),
        "bot": np.ascontiguousarray(bo.reshape(NDC, 128).T),

        "cosk": Ctab,
        "sink": Stab,
        "identf": np.eye(128, dtype=np.float32),
        "identb": np.ascontiguousarray(np.eye(128, dtype=np.float32).astype(ml_dtypes.bfloat16)),
        "ind2": np.ascontiguousarray(
            (np.kron(np.eye(2, dtype=np.float32), np.ones((64, 1), np.float32)) / 64.0
             ).astype(ml_dtypes.bfloat16)),
        "e2": np.ascontiguousarray(
            np.kron(np.eye(2, dtype=np.float32), np.ones((1, 64), np.float32)
                    ).astype(ml_dtypes.bfloat16)),
        "ones128b": np.ones((128, 1), dtype=ml_dtypes.bfloat16),
        "onesrowf": np.ones((64, 128), dtype=np.float32),
    }

    WguT = np.ascontiguousarray(Wgu.T)
    WpT = np.ascontiguousarray(Wp.T)
    per_core = []
    for c in range(8):
        b, k = c // 4, c % 4
        own = np.concatenate([np.arange(512) + (4 * s + k) * 512 for s in range(NSLOT)])
        d = dict(shared)
        d["wgut"] = np.ascontiguousarray(WguT[:, c * 2048:(c + 1) * 2048])
        d["wpt"] = np.ascontiguousarray(WpT[c * 1024:(c + 1) * 1024, :])
        d["xfull"] = np.ascontiguousarray(x[b])
        d["xown"] = np.ascontiguousarray(x[b][own])
        d["cosq"] = np.ascontiguousarray((0.25 * Ctab[:, own]).astype(np.float32))
        d["sinq"] = np.ascontiguousarray((0.25 * Stab[:, own]).astype(np.float32))
        d["cosko"] = np.ascontiguousarray(Ctab[:, own])
        d["sinko"] = np.ascontiguousarray(Stab[:, own])
        d["masks"] = _masks_for_core(k)
        per_core.append(d)
    return per_core



def build_nc():
    nc = bass.Bass("TRN2", target_bir_lowering=False)
    es = ExitStack()

    def din(name, shape, dt=F32):
        return nc.dram_tensor(name, list(shape), dt, kind="ExternalInput").ap()

    xfull = din("xfull", [T, C])
    xown = din("xown", [NSLOT * 512, C])
    wqp = din("wqp", [C, C])
    wkp = din("wkp", [C, 512])
    wv = din("wv", [C, 512])
    wo = din("wo", [C, C])
    bqt = din("bqt", [128, HQ])
    bkt = din("bkt", [128, HKV])
    bvt = din("bvt", [128, HKV])
    bot = din("bot", [128, NDC])
    wgut = din("wgut", [C, 2048])
    wpt = din("wpt", [1024, C])
    cosk = din("cosk", [128, T])
    sink = din("sink", [128, T])
    cosq = din("cosq", [128, NSLOT * 512])
    sinq = din("sinq", [128, NSLOT * 512])
    cosko = din("cosko", [128, NSLOT * 512])
    sinko = din("sinko", [128, NSLOT * 512])
    masks = din("masks", [NSLOT, 16, 128, 512], BF16)
    identf = din("identf", [128, 128])
    identb = din("identb", [128, 128], BF16)
    ind2 = din("ind2", [128, 2], BF16)
    e2 = din("e2", [2, 128], BF16)
    ones128b = din("ones128b", [128, 1], BF16)
    onesrowf = din("onesrowf", [64, 128])

    y = nc.dram_tensor("y", [NSLOT * 512, C], F32, kind="ExternalOutput").ap()

    skind = "ExternalOutput" if DEBUG else "Internal"
    xnscr = nc.dram_tensor("xnscr", [NSLOT, 128, NDC, 512], BF16, kind=skind).ap()
    qscr = nc.dram_tensor("qscr", [NSLOT, 128, HQ, 512], BF16, kind=skind).ap()
    kshard = nc.dram_tensor("kshard", [HKV, 128, NSLOT * 512], BF16,
                            kind="Internal").ap()
    vtshard = nc.dram_tensor("vtshard", [HKV, 128, NSLOT * 512], BF16,
                             kind="Internal").ap()
    kscr = nc.dram_tensor("kscr", [4, HKV, 128, NSLOT * 512], BF16,
                          kind="Internal").ap()
    vtscr = nc.dram_tensor("vtscr", [4, HKV, 128, NSLOT * 512], BF16,
                           kind="Internal").ap()
    mkbounce = nc.dram_tensor("mkbounce", [1, 1], F32, kind="Internal").ap()
    mkout = nc.dram_tensor("mkout", [1, 1], F32, kind="Internal").ap()

    def kv_load(sy, sem, dstbuf, scr, j, E):
        tok = None
        v3 = scr[:, j].rearrange("r p (s f) -> p s r f", f=512)
        for sg in range(E // 2048):
            dst3 = dstbuf[:, sg * 2048:(sg + 1) * 2048].rearrange(
                "p (r f) -> p r f", f=512)
            tok = sem.dinc(sy.dma_start(out=dst3, in_=v3[:, sg]))
        return tok
    wguq_sh = nc.dram_tensor("wguq_sh", [16, 128, NDC, 128], BF16, kind="Internal").ap()
    wpq_sh = nc.dram_tensor("wpq_sh", [NDC, 128, 8, 128], BF16, kind="Internal").ap()
    wguq = nc.dram_tensor("wguq", [2 * HF // 128, 128, NDC, 128], BF16,
                          kind="Internal", addr_space="Shared").ap()
    wpq = nc.dram_tensor("wpq", [8, NDC, 128, 8, 128], BF16,
                         kind="Internal", addr_space="Shared").ap()
    sscr = nc.dram_tensor("sscr", [NSLOT, 4, 128, C], F32, kind=skind).ap()
    oscr = nc.dram_tensor("oscr", [NSLOT, 128, HQ * 512], BF16, kind=skind).ap() if DEBUG else None
    dnscr = nc.dram_tensor("dnscr", [NSLOT, HKV, 2, 64, 512], F32, kind=skind).ap() if DEBUG else None

    sems = {n: Sem(nc, es, n) for n in
            ["ve", "ac", "te", "gp", "dln", "dw", "dwt", "dst",
             "dk0", "dk1", "dq", "dg", "dp", "dxs", "dy"]}
    sVE, sAC, sTE, sGP = sems["ve"], sems["ac"], sems["te"], sems["gp"]
    sDLN, sDW, sDWT, sDST = sems["dln"], sems["dw"], sems["dwt"], sems["dst"]

    def gsb(name, shape, dt=F32):
        return es.enter_context(nc.sbuf_tensor(name, shape, dt))
    cbias = gsb("cbias", [128, 1])
    maxq2 = gsb("maxq2", [1, 1])
    maxk2 = gsb("maxk2", [1, 1])
    identb_sb = gsb("identb_sb", [128, 128], BF16)
    identf_sb = gsb("identf_sb", [128, 128])
    ind2_sb = gsb("ind2_sb", [128, 2], BF16)
    e2_sb = gsb("e2_sb", [2, 128], BF16)
    ones128b_sb = gsb("ones128b_sb", [128, 1], BF16)
    onesrowf_sb = gsb("onesrowf_sb", [64, 128])
    bq_sb = gsb("bq_sb", [128, HQ])
    bk_sb = gsb("bk_sb", [128, HKV])
    bv_sb = gsb("bv_sb", [128, HKV])
    bo_sb = gsb("bo_sb", [128, NDC])

    def emit_ln(blk, xbufs, stat4, xn_tm, tok_x):
        sums, ssq, mean, t1, msq, var, sd, rstd, scr = (
            stat4[k] for k in ["sums", "ssq", "mean", "t1", "msq", "var",
                               "sd", "rstd", "scratch"])

        def v1(v):
            for u in range(4):
                v.wait_ge(sDLN.h, tok_x[u])
                v.reduce_sum(out=sums[:, u:u + 1], in_=xbufs[u][:], axis=AX.X)
                v.scalar_tensor_tensor(out=scr[:], in0=xbufs[u][:], scalar=1.0,
                                       in1=xbufs[u][:], op0=ALU.mult, op1=ALU.mult,
                                       accum_out=ssq[:, u:u + 1])
            v.drain()
            v.tensor_scalar(out=mean[:], in0=sums[:], scalar1=1.0 / 2048,
                            scalar2=None, op0=ALU.mult)
            v.tensor_scalar(out=t1[:], in0=ssq[:], scalar1=1.0 / 2048,
                            scalar2=LN_EPS, op0=ALU.mult, op1=ALU.add)
            v.drain()
            v.tensor_tensor(out=msq[:], in0=mean[:], in1=mean[:], op=ALU.mult)
            v.drain()
            sVE.inc(v.tensor_tensor(out=var[:], in0=t1[:], in1=msq[:],
                                    op=ALU.subtract))
        blk.vector(v1)
        tok_var = sVE.v

        def s1(sc):
            sc.wait_ge(sVE.h, tok_var)
            sAC.inc(sc.sqrt(out=sd[:], in_=var[:]))
        blk.scalar(s1)
        tok_sd = sAC.v

        def v2(v):
            v.wait_ge(sAC.h, tok_sd)
            v.reciprocal(out=rstd[:], in_=sd[:])
            v.drain()
            for u in range(4):
                v.tensor_scalar(
                    out=xn_tm[u][:], in0=xbufs[u][:],
                    scalar1=mean[:, u:u + 1], scalar2=rstd[:, u:u + 1],
                    op0=ALU.subtract, op1=ALU.mult)
                sVE.inc(v.drain())
        blk.vector(v2)
        return sVE.v

    def emit_tm2fm(blk, xn_tm, xn_fm, tpb, tok_xn_last):
        tp_toks = []

        def tb(t):
            for u in range(4):
                t.wait_ge(sVE.h, tok_xn_last - 3 + u)
                for r in range(2):
                    if tp_toks:
                        t.wait_ge(sVE.h, ev_base + len(tp_toks))
                    for q in range(8):
                        dc = r * 8 + q
                        mm = t.transpose(tpb[:, q * 128:(q + 1) * 128],
                                         xn_tm[u][:, dc * 128:(dc + 1) * 128],
                                         identb_sb[:])
                    tp_toks.append(sTE.inc(mm))
        ev_base = sVE.v

        def vb(v):
            fmv = xn_fm[:].rearrange("p (dc f) -> p dc f", dc=NDC)
            ti = 0
            for u in range(4):
                for r in range(2):
                    v.wait_ge(sTE.h, tp_toks[ti]); ti += 1
                    dst = fmv[:, r * 8:(r + 1) * 8, u * 128:(u + 1) * 128]
                    v.tensor_copy(out=dst, in_=tpb[:].rearrange(
                        "p (q f) -> p q f", q=8))
                    sVE.inc(v.drain())
        blk.tensor(tb)
        blk.vector(vb)
        return sVE.v

    def soft_barrier(blk):
        a, v, t = sAC.v, sVE.v, sTE.v

        def tb_(e):
            e.wait_ge(sAC.h, a)
            e.wait_ge(sVE.h, v)

        def sb_(e):
            e.wait_ge(sTE.h, t)
            e.wait_ge(sVE.h, v)

        def vb_(e):
            e.wait_ge(sTE.h, t)
            e.wait_ge(sAC.h, a)
        blk.tensor(tb_)
        blk.scalar(sb_)
        blk.vector(vb_)

    with ExitStack() as ps:
        def sb(name, shape, dt=F32):
            return ps.enter_context(nc.sbuf_tensor(name, shape, dt))
        wq_sb = sb("wq_sb", [128, NDC * C], BF16)
        xbufs = [sb(f"a0xb{u}", [128, C]) for u in range(4)]
        stat4 = {k: sb("a0st_" + k, [128, 4]) for k in
                 ["sums", "ssq", "mean", "t1", "msq", "var", "sd", "rstd"]}
        stat4["scratch"] = sb("a0st_scratch", [128, C])
        xn_tm = [sb(f"a0xntm{u}", [128, C], BF16) for u in range(4)]
        xn_fm = sb("a0xnfm", [128, NDC * 512], BF16)
        qf = [sb(f"a0qf{i}", [128, 512]) for i in range(2)]
        rsw = sb("a0rsw", [128, 512])
        rt1 = sb("a0rt1", [128, 512])
        rt2 = sb("a0rt2", [128, 512])
        qstage = sb("a0qstage", [128, HQ * 512], BF16)
        sq16 = sb("a0sq16", [128, 512], BF16)
        qn_red = sb("a0qn_red", [1, 1])
        cq_sb = sb("a0cq", [128, NSLOT * 512])
        sq_sb = sb("a0sq", [128, NSLOT * 512])

        qps = [ps.enter_context(nc.psum_tensor(f"a0qps{i}", [128, 512], F32))
               for i in range(2)]
        tpb = ps.enter_context(nc.psum_tensor("a0tpb", [128, 1024], BF16))
        qn_ps = ps.enter_context(nc.psum_tensor("a0qn_ps", [1, 512], F32))

        with nc.Block("a0") as blk:
            def sy0(sy):
                for dst, src in [(identb_sb, identb), (identf_sb, identf),
                                 (ind2_sb, ind2), (e2_sb, e2),
                                 (ones128b_sb, ones128b), (onesrowf_sb, onesrowf),
                                 (bq_sb, bqt), (bk_sb, bkt), (bv_sb, bvt),
                                 (bo_sb, bot), (cq_sb, cosq), (sq_sb, sinq)]:
                    sDW.dinc(sy.dma_start(out=dst[:], in_=src))
            blk.sync(sy0)
            tok_const = sDW.v

            def gp0(g):
                for dc in range(NDC):
                    sDWT.dinc(g.dma_start(out=wq_sb[:, dc * C:(dc + 1) * C],
                                          in_=wqp[dc * 128:(dc + 1) * 128, :]))
                g.memset(qn_red[:], 0.0)
                sGP.inc(g.memset(maxq2[:], 0.0))
            blk.gpsimd(gp0)
            tok_wq = sDWT.v
            tok_gz = sGP.v

            for s in range(NSLOT):
                soft_barrier(blk)
                tok_x = []
                ve_now = sVE.v

                def syx(sy, s=s, ve_now=ve_now):
                    sy.wait_ge(sVE.h, ve_now)
                    for u in range(4):
                        tok_x.append(sDLN.dinc(sy.dma_start(
                            out=xbufs[u][:],
                            in_=xown[s * 512 + u * 128: s * 512 + (u + 1) * 128, :])))
                blk.sync(syx)

                tok_xn = emit_ln(blk, xbufs, stat4, xn_tm, tok_x)
                tok_fm = emit_tm2fm(blk, xn_tm, xn_fm, tpb, tok_xn)

                def sxn(sy, s=s):
                    sy.wait_ge(sVE.h, tok_fm)
                    sDST.dinc(sy.dma_start(
                        out=xnscr[s],
                        in_=xn_fm[:].rearrange("p (dc f) -> p dc f", dc=NDC)))
                blk.sync(sxn)

                qmm_tok = []
                qev_base = sAC.v
                rope_base = sVE.v

                def tq(t, s=s):
                    t.wait_ge(sDWT.h, tok_wq)
                    t.wait_ge(sVE.h, tok_fm)
                    for g in range(HQ):
                        if g >= 2:
                            t.wait_ge(sAC.h, qev_base + (g - 2) + 1)
                        for dc in range(NDC):
                            mm = t.matmul(
                                qps[g % 2][:],
                                wq_sb[:, dc * C + g * 128: dc * C + (g + 1) * 128],
                                xn_fm[:, dc * 512:(dc + 1) * 512],
                                start=(dc == 0), stop=(dc == NDC - 1))
                        qmm_tok.append(sTE.inc(mm))
                blk.tensor(tq)

                def sq(sc, s=s):
                    sc.wait_ge(sDW.h, tok_const)
                    for g in range(HQ):
                        sc.wait_ge(sTE.h, qmm_tok[g])
                        if g >= 2:
                            sc.wait_ge(sVE.h, rope_base + (g - 2) + 1)
                        sAC.inc(sc.activation(out=qf[g % 2][:], in_=qps[g % 2][:],
                                              func=AF.Identity, bias=bq_sb[:, g:g + 1],
                                              scale=1.0))
                blk.scalar(sq)

                def vq(v, s=s):
                    v.wait_ge(sDW.h, tok_const)
                    for g in range(HQ):
                        v.wait_ge(sAC.h, qev_base + g + 1)
                        q = qf[g % 2]
                        v.tensor_copy(out=rsw[0:64, :], in_=q[64:128, :])
                        v.tensor_copy(out=rsw[64:128, :], in_=q[0:64, :])
                        v.tensor_tensor(out=rt1[:], in0=q[:],
                                        in1=cq_sb[:, s * 512:(s + 1) * 512], op=ALU.mult)
                        v.tensor_tensor(out=rt2[:], in0=rsw[:],
                                        in1=sq_sb[:, s * 512:(s + 1) * 512], op=ALU.mult)
                        sVE.inc(v.tensor_tensor(
                            out=qstage[:, g * 512:(g + 1) * 512],
                            in0=rt1[:], in1=rt2[:], op=ALU.add))
                    v.drain()
                blk.vector(vq)
                tok_rope_last = sVE.v

                for g in range(HQ):
                    def s2(v, g=g, s=s):
                        v.wait_ge(sVE.h, tok_rope_last - (HQ - 1) + g)
                        if g >= 1:
                            v.wait_ge(sTE.h, qn_mm_tok[g - 1])
                        qsl = qstage[:, g * 512:(g + 1) * 512]
                        sVE.inc(v.tensor_tensor(out=sq16[:], in0=qsl, in1=qsl,
                                                op=ALU.mult))
                    blk.vector(s2)
                    tok_sq16 = sVE.v

                    def t2(t, g=g):
                        t.wait_ge(sDW.h, tok_const)
                        t.wait_ge(sVE.h, tok_sq16)
                        qn_mm_tok[g] = sTE.inc(t.matmul(
                            qn_ps[:], ones128b_sb[:], sq16[:], start=True, stop=True))
                    if g == 0:
                        qn_mm_tok = {}
                    blk.tensor(t2)

                    def v2(v, g=g):
                        v.wait_ge(sTE.h, qn_mm_tok[g])
                        v.wait_ge(sGP.h, tok_gz)
                        v.tensor_reduce(out=qn_red[:], in_=qn_ps[:], axis=AX.X,
                                        op=ALU.max)
                        v.drain()
                        sVE.inc(v.tensor_tensor(out=maxq2[:], in0=maxq2[:],
                                                in1=qn_red[:], op=ALU.max))
                        v.drain()
                    blk.vector(v2)

                def syq(sy, s=s):
                    sy.wait_ge(sVE.h, sVE.v)
                    sDST.dinc(sy.dma_start(
                        out=qscr[s],
                        in_=qstage[:].rearrange("p (g f) -> p g f", g=HQ)))
                blk.sync(syq)

            def syf(sy):
                sy.wait_ge(sDST.h, sDST.v)
            blk.sync(syf)

    with ExitStack() as ps:
        def sb(name, shape, dt=F32):
            return ps.enter_context(nc.sbuf_tensor(name, shape, dt))
        wk_sb = sb("wk_sb", [128, NDC * 512], BF16)
        wv_sb = sb("wv_sb", [128, NDC * 512], BF16)
        cko_sb = sb("cko_sb", [128, NSLOT * 512])
        sko_sb = sb("sko_sb", [128, NSLOT * 512])
        xnb1 = [sb(f"a1xnb{i}", [128, NDC * 512], BF16) for i in range(2)]
        kf = [sb(f"a1kf{i}", [128, 512]) for i in range(2)]
        rsw = sb("a1rsw", [128, 512])
        rt1 = sb("a1rt1", [128, 512])
        rt2 = sb("a1rt2", [128, 512])
        krope = [sb(f"a1krope{i}", [128, 512], BF16) for i in range(2)]
        vf16 = [sb(f"a1vf16{i}", [128, 512], BF16) for i in range(2)]
        vt_stage = [sb(f"a1vt{i}", [128, 512], BF16) for i in range(2)]
        sq16 = [sb(f"a1sq16{i}", [128, 512], BF16) for i in range(2)]
        kn_red = sb("a1kn_red", [1, 1])

        ps1 = ExitStack()
        kps = [ps1.enter_context(nc.psum_tensor(f"a1kps{i}", [128, 512], F32))
               for i in range(2)]
        tpb = ps1.enter_context(nc.psum_tensor("a1tpb", [128, 1024], BF16))
        kn_ps = ps1.enter_context(nc.psum_tensor("a1kn_ps", [1, 512], F32))

        blk = ps.enter_context(nc.Block("a1"))
        def gp0(g):
            for dc in range(NDC):
                sDWT.dinc(g.dma_start(out=wk_sb[:, dc * 512:(dc + 1) * 512],
                                      in_=wkp[dc * 128:(dc + 1) * 128, :]))
                sDWT.dinc(g.dma_start(out=wv_sb[:, dc * 512:(dc + 1) * 512],
                                      in_=wv[dc * 128:(dc + 1) * 128, :]))
            g.memset(kn_red[:], 0.0)
            sGP.inc(g.memset(maxk2[:], 0.0))
        blk.gpsimd(gp0)
        tok_wkv = sDWT.v
        tok_gz = sGP.v

        def sy0(sy):
            sDW.dinc(sy.dma_start(out=cko_sb[:], in_=cosko))
            sDW.dinc(sy.dma_start(out=sko_sb[:], in_=sinko))
        blk.sync(sy0)
        tok_tab = sDW.v

        xn_tok = {}
        kev_tok = {}
        krope_tok = {}
        kst_tok = {}
        vev_tok = {}
        vtp_tok = {}
        vts_tok = {}
        vst_tok = {}
        kn_mm_tok = {}
        kn_v_tok = {}
        xn_used = {}

        for s in range(NSLOT):
            def syx(sy, s=s):
                if s >= 2:
                    sy.wait_ge(sTE.h, xn_used[s - 2])
                xn_tok[s] = sDLN.dinc(sy.dma_start(
                    out=xnb1[s % 2][:],
                    in_=xnscr[s].rearrange("p dc f -> p (dc f)")))
            blk.sync(syx)

            for j in range(HKV):
                n = s * HKV + j
                fb = n % 2

                def tk(t, s=s, j=j, n=n):
                    t.wait_ge(sDWT.h, tok_wkv)
                    t.wait_ge(sDLN.h, xn_tok[s])
                    if n - 1 in kev_tok:
                        t.wait_ge(sAC.h, kev_tok[n - 1])
                    for dc in range(NDC):
                        mm = t.matmul(
                            kps[0][:],
                            wk_sb[:, dc * 512 + j * 128: dc * 512 + (j + 1) * 128],
                            xnb1[s % 2][:, dc * 512:(dc + 1) * 512],
                            start=(dc == 0), stop=(dc == NDC - 1))
                    sTE.inc(mm)
                blk.tensor(tk)
                tok_kmm = sTE.v

                def sk(sc, j=j, n=n, fb=fb):
                    sc.wait_ge(sTE.h, tok_kmm)
                    if n - 2 in krope_tok:
                        sc.wait_ge(sVE.h, krope_tok[n - 2])
                    kev_tok[n] = sAC.inc(sc.activation(
                        out=kf[fb][:], in_=kps[0][:], func=AF.Identity,
                        bias=bk_sb[:, j:j + 1], scale=1.0))
                blk.scalar(sk)

                def vk(v, j=j, s=s, n=n, fb=fb):
                    v.wait_ge(sAC.h, kev_tok[n])
                    v.wait_ge(sDW.h, tok_tab)
                    if n - 2 in kst_tok:
                        v.wait_ge(sDST.h, kst_tok[n - 2])
                    q = kf[fb]
                    v.tensor_copy(out=rsw[0:64, :], in_=q[64:128, :])
                    v.tensor_copy(out=rsw[64:128, :], in_=q[0:64, :])
                    v.tensor_tensor(out=rt1[:], in0=q[:],
                                    in1=cko_sb[:, s * 512:(s + 1) * 512],
                                    op=ALU.mult)
                    v.tensor_tensor(out=rt2[:], in0=rsw[:],
                                    in1=sko_sb[:, s * 512:(s + 1) * 512],
                                    op=ALU.mult)
                    v.tensor_tensor(out=krope[fb][:], in0=rt1[:], in1=rt2[:],
                                    op=ALU.add)
                    if n - 2 in kn_mm_tok:
                        v.wait_ge(sTE.h, kn_mm_tok[n - 2])
                    v.tensor_tensor(out=sq16[fb][:], in0=krope[fb][:],
                                    in1=krope[fb][:], op=ALU.mult)
                    krope_tok[n] = sVE.inc(v.drain())
                blk.vector(vk)

                def syk(sy, j=j, s=s, n=n, fb=fb):
                    sy.wait_ge(sVE.h, krope_tok[n])
                    kst_tok[n] = sDST.dinc(sy.dma_start(
                        out=kshard[j][:, s * 512:(s + 1) * 512],
                        in_=krope[fb][:]))
                blk.sync(syk)

                def t2(t, n=n, fb=fb):
                    t.wait_ge(sVE.h, krope_tok[n])
                    if n - 1 in kn_v_tok:
                        t.wait_ge(sVE.h, kn_v_tok[n - 1])
                    kn_mm_tok[n] = sTE.inc(t.matmul(
                        kn_ps[:], ones128b_sb[:], sq16[fb][:],
                        start=True, stop=True))
                blk.tensor(t2)

                def v2(v, n=n):
                    v.wait_ge(sTE.h, kn_mm_tok[n])
                    v.wait_ge(sGP.h, tok_gz)
                    v.tensor_reduce(out=kn_red[:], in_=kn_ps[:], axis=AX.X,
                                    op=ALU.max)
                    v.drain()
                    kn_v_tok[n] = sVE.inc(v.tensor_tensor(
                        out=maxk2[:], in0=maxk2[:], in1=kn_red[:], op=ALU.max))
                    v.drain()
                blk.vector(v2)

                def tv(t, j=j, s=s, n=n):
                    if n - 1 in vev_tok:
                        t.wait_ge(sAC.h, vev_tok[n - 1])
                    for dc in range(NDC):
                        mm = t.matmul(
                            kps[1][:],
                            wv_sb[:, dc * 512 + j * 128: dc * 512 + (j + 1) * 128],
                            xnb1[s % 2][:, dc * 512:(dc + 1) * 512],
                            start=(dc == 0), stop=(dc == NDC - 1))
                    sTE.inc(mm)
                    xn_used[s] = sTE.v
                blk.tensor(tv)
                tok_vmm = sTE.v

                def sv(sc, j=j, n=n, fb=fb):
                    sc.wait_ge(sTE.h, tok_vmm)
                    if n - 2 in vtp_tok:
                        sc.wait_ge(sTE.h, vtp_tok[n - 2])
                    vev_tok[n] = sAC.inc(sc.activation(
                        out=vf16[fb][:], in_=kps[1][:], func=AF.Identity,
                        bias=bv_sb[:, j:j + 1], scale=1.0))
                blk.scalar(sv)

                def tv2(t, j=j, n=n, fb=fb):
                    t.wait_ge(sAC.h, vev_tok[n])
                    if n - 1 in vts_tok:
                        t.wait_ge(sVE.h, vts_tok[n - 1])
                    for q in range(4):
                        mm = t.transpose(tpb[:, q * 128:(q + 1) * 128],
                                         vf16[fb][:, q * 128:(q + 1) * 128],
                                         identb_sb[:])
                    vtp_tok[n] = sTE.inc(mm)
                blk.tensor(tv2)

                def vv(v, j=j, n=n, fb=fb):
                    v.wait_ge(sTE.h, vtp_tok[n])
                    if n - 2 in vst_tok:
                        v.wait_ge(sDST.h, vst_tok[n - 2])
                    v.tensor_copy(out=vt_stage[fb][:], in_=tpb[:, 0:512])
                    vts_tok[n] = sVE.inc(v.drain())
                blk.vector(vv)

                def syv(sy, j=j, s=s, n=n, fb=fb):
                    sy.wait_ge(sVE.h, vts_tok[n])
                    vst_tok[n] = sDST.dinc(sy.dma_start(
                        out=vtshard[j][:, s * 512:(s + 1) * 512],
                        in_=vt_stage[fb][:]))
                blk.sync(syv)

        def syb(sy):
            sy.wait_ge(sVE.h, sVE.v)
            sDST.dinc(sy.dma_start(out=mkbounce, in_=maxk2[:]))
        blk.sync(syb)

        ps1.close()

        def sb(name, shape, dt=F32):
            return ps.enter_context(nc.sbuf_tensor(name, shape, dt))
        wb = [sb(f"qwb{i}", [128, 2048], BF16) for i in range(2)]
        aw = sb("qaw", [128, 2048], BF16)
        sgn = sb("qsgn", [128, 2048], BF16)
        sc16 = sb("qsc16", [64, 2048], BF16)
        thr = sb("qthr", [128, 2048])
        scbc = sb("qscbc", [128, 2048], BF16)
        mgt = sb("qmgt", [128, 2048], BF16)
        qv = sb("qqv", [128, 2048], BF16)
        wqv = sb("qwqv", [128, 2048], BF16)
        d1t = sb("qd1", [128, 2048], BF16)
        rr = [sb(f"qrr{i}", [128, 2048], BF16) for i in range(2)]
        qctmp = sb("qctmp", [1, 1])
        qctmp2 = sb("qctmp2", [1, 1])

        scl_ps = ps.enter_context(nc.psum_tensor("qscl_ps", [2, 2048], F32))
        bc_ps = ps.enter_context(nc.psum_tensor("qbc_ps", [128, 2048], F32))
        qcb_ps = bc_ps[:, 0:1]
        _g_ac, _g_ve, _g_te = sAC.v, sVE.v, sTE.v

        def _qgate_t(t):
            t.wait_ge(sAC.h, _g_ac)
            t.wait_ge(sVE.h, _g_ve)

        def _qgate_v(v):
            v.wait_ge(sAC.h, _g_ac)
            v.wait_ge(sTE.h, _g_te)
        blk.tensor(_qgate_t)
        blk.vector(_qgate_v)

        tiles = []
        for dc in range(NDC):
            tiles.append(("gu", dc, 0))
        for hcl in range(8):
            tiles.append(("p", hcl, 0))
        NT = len(tiles)

        tok_ld = [None] * NT
        tok_sums = [None] * NT
        tok_scev = [None] * NT
        tok_bc = [None] * NT
        tok_bcev = [None] * NT
        tok_thr = [None] * NT
        tok_sgn = [None] * NT
        tok_aw = [None] * NT
        tok_q = [None] * NT
        tok_wq = [None] * NT
        tok_r = [None] * NT
        tok_st = [None] * NT

        if True:
            for ti2, (kind, a, b_) in enumerate(tiles):
                par = ti2 % 2

                def gl(g, ti2=ti2, kind=kind, a=a, b_=b_, par=par):
                    if kind == "gu":
                        src = wgut[a * 128:(a + 1) * 128, :]
                    else:
                        src = wpt[a * 128:(a + 1) * 128, :]
                    if ti2 >= 2:
                        g.wait_ge(sVE.h, tok_r[ti2 - 2])
                    tok_ld[ti2] = sDWT.dinc(g.dma_start(out=wb[par][:], in_=src))
                blk.gpsimd(gl)

                def s1(sc, ti2=ti2, par=par):
                    sc.wait_ge(sDWT.h, tok_ld[ti2])
                    if ti2 > 0:
                        sc.wait_ge(sTE.h, tok_sums[ti2 - 1])
                        sc.wait_ge(sVE.h, tok_q[ti2 - 1])
                    tok_aw[ti2] = sAC.inc(sc.activation(out=aw[:], in_=wb[par][:],
                                                        func=AF.Abs))
                    tok_sgn[ti2] = sAC.inc(sc.sign(out=sgn[:], in_=wb[par][:]))
                blk.scalar(s1)

                def t1(t, ti2=ti2):
                    t.wait_ge(sAC.h, tok_aw[ti2])
                    if ti2 > 0:
                        t.wait_ge(sVE.h, tok_scev[ti2 - 1])
                    for i in range(4):
                        mm = t.matmul(scl_ps[:, i * 512:(i + 1) * 512], ind2_sb[:],
                                      aw[:, i * 512:(i + 1) * 512],
                                      start=True, stop=True)
                    tok_sums[ti2] = sTE.inc(mm)
                blk.tensor(t1)

                def v1(v, ti2=ti2):
                    v.wait_ge(sTE.h, tok_sums[ti2])
                    if ti2 > 0:
                        v.wait_ge(sTE.h, tok_bc[ti2 - 1])
                    v.tensor_copy(out=sc16[0:2, :], in_=scl_ps[:])
                    tok_scev[ti2] = sVE.inc(v.tensor_scalar(
                        out=sc16[0:2, :], in0=sc16[0:2, :], scalar1=1e-8,
                        scalar2=None, op0=ALU.max))
                blk.vector(v1)

                def t2(t, ti2=ti2):
                    t.wait_ge(sVE.h, tok_scev[ti2])
                    if ti2 > 0:
                        t.wait_ge(sAC.h, tok_bcev[ti2 - 1])
                    for i in range(4):
                        mm = t.matmul(bc_ps[:, i * 512:(i + 1) * 512], e2_sb[:],
                                      sc16[0:2, i * 512:(i + 1) * 512],
                                      start=True, stop=True)
                    tok_bc[ti2] = sTE.inc(mm)
                blk.tensor(t2)

                def s2(sc, ti2=ti2):
                    sc.wait_ge(sTE.h, tok_bc[ti2])
                    if ti2 > 0:
                        sc.wait_ge(sVE.h, tok_wq[ti2 - 1])
                    tok_thr[ti2] = sAC.inc(sc.activation(out=thr[:], in_=bc_ps[:],
                                                         func=AF.Copy, scale=QTHRESH))
                    tok_bcev[ti2] = sAC.inc(sc.activation(out=scbc[:], in_=bc_ps[:],
                                                          func=AF.Copy))
                blk.scalar(s2)

                def v2(v, ti2=ti2, par=par):
                    v.wait_ge(sAC.h, tok_bcev[ti2])
                    v.wait_ge(sAC.h, tok_sgn[ti2])
                    if ti2 >= 2:
                        v.wait_ge(sDST.h, tok_st[ti2 - 2])
                    v.tensor_tensor(out=mgt[:], in0=aw[:], in1=thr[:], op=ALU.is_gt)
                    tok_q[ti2] = sVE.inc(v.tensor_tensor(out=qv[:], in0=mgt[:],
                                                         in1=sgn[:], op=ALU.mult))
                    tok_wq[ti2] = sVE.inc(v.tensor_tensor(out=wqv[:], in0=qv[:],
                                                          in1=scbc[:], op=ALU.mult))
                    v.tensor_tensor(out=d1t[:], in0=wqv[:], in1=wb[par][:],
                                    op=ALU.subtract)
                    tok_r[ti2] = sVE.inc(v.tensor_tensor(out=rr[par][:], in0=wb[par][:],
                                                         in1=d1t[:], op=ALU.add))
                blk.vector(v2)

                def sy1(sy, ti2=ti2, kind=kind, a=a, b_=b_, par=par):
                    sy.wait_ge(sVE.h, tok_r[ti2])
                    src = rr[par][:].rearrange("p (i f) -> p i f", i=16)
                    if kind == "gu":
                        dst = wguq_sh[:, :, a, :].transpose([1, 0, 2])
                    else:
                        dst = wpq_sh[:, :, a, :].transpose([1, 0, 2])
                    tok_st[ti2] = sDST.dinc(sy.dma_start(out=dst, in_=src))
                blk.sync(sy1)

            def gcc(g):
                g.wait_ge(sDST.h, sDST.v)
                grp4 = [[0, 1, 2, 3], [4, 5, 6, 7]]
                g.collective_compute(
                    "AllGather", ALU.bypass, replica_groups=grp4,
                    ins=[kshard.rearrange("j p f -> (j p f)").unsqueeze(0).opt()],
                    outs=[kscr.rearrange("r j p f -> r (j p f)").opt()],
                ).then_inc(sGP.h, 1)
                sGP.v += 1
                g.collective_compute(
                    "AllGather", ALU.bypass, replica_groups=grp4,
                    ins=[vtshard.rearrange("j p f -> (j p f)").unsqueeze(0).opt()],
                    outs=[vtscr.rearrange("r j p f -> r (j p f)").opt()],
                ).then_inc(sGP.h, 1)
                sGP.v += 1
                g.collective_compute(
                    "AllReduce", ALU.max, replica_groups=[list(range(8))],
                    ins=[mkbounce.opt()], outs=[mkout.opt()],
                ).then_inc(sGP.h, 1)
                sGP.v += 1
                g.collective_compute(
                    "AllGather", ALU.bypass,
                    replica_groups=[list(range(8))],
                    ins=[wguq_sh.rearrange("a p c f -> a (p c f)").opt()],
                    outs=[wguq.rearrange("a p c f -> a (p c f)").opt()],
                ).then_inc(sGP.h, 1)
                sGP.v += 1
                g.collective_compute(
                    "AllGather", ALU.bypass,
                    replica_groups=[list(range(8))],
                    ins=[wpq_sh.rearrange("d p c f -> (d p c f)").unsqueeze(0).opt()],
                    outs=[wpq.rearrange("s d p c f -> s (d p c f)").opt()],
                ).then_inc(sGP.h, 1)
                sGP.v += 1
                g.wait_ge(sGP.h, sGP.v)
            blk.gpsimd(gcc)
            tok_cc = sGP.v

            def syb2(sy):
                sy.wait_ge(sGP.h, tok_cc)
                sDW.dinc(sy.dma_start(out=maxk2[:], in_=mkout))
            blk.sync(syb2)
            tok_mk = sDW.v

            def sc_c(sc):
                sc.wait_ge(sDW.h, tok_mk)
                sAC.inc(sc.sqrt(out=qctmp[:], in_=maxq2[:]))
                sAC.inc(sc.sqrt(out=qctmp2[:], in_=maxk2[:]))
            blk.scalar(sc_c)
            tok_cs = sAC.v

            def v_c(v):
                v.wait_ge(sAC.h, tok_cs)
                v.tensor_tensor(out=qctmp[:], in0=qctmp[:], in1=qctmp2[:],
                                op=ALU.mult)
                v.drain()
                v.tensor_scalar(out=qctmp[:], in0=qctmp[:], scalar1=-1.0,
                                scalar2=None, op0=ALU.mult)
                sVE.inc(v.drain())
            blk.vector(v_c)
            tok_cn = sVE.v

            def t_c(t):
                t.wait_ge(sVE.h, tok_cn)
                sTE.inc(t.matmul(qcb_ps, onesrowf_sb[0:1, :], qctmp[:],
                                 start=True, stop=True))
            blk.tensor(t_c)
            tok_cb = sTE.v

            def v_c2(v):
                v.wait_ge(sTE.h, tok_cb)
                sVE.inc(v.tensor_copy(out=cbias[:], in_=qcb_ps))
            blk.vector(v_c2)

            def syf(sy):
                sy.wait_ge(sDST.h, sDST.v)
                sy.wait_ge(sGP.h, sGP.v)
                sy.wait_ge(sVE.h, sVE.v)
            blk.sync(syf)

    with ExitStack() as ps:
        def sb(name, shape, dt=F32):
            return ps.enter_context(nc.sbuf_tensor(name, shape, dt))
        kbuf = [sb(f"bk{i}", [128, T], BF16) for i in range(2)]
        vtbuf = [sb(f"bvt{i}", [128, T], BF16) for i in range(2)]
        qbuf = sb("bq", [128, HQ * 512], BF16)
        mask_sb = sb("bmask", [128, 16 * 512], BF16)
        pbuf = sb("bp", [128, 2048], BF16)
        o_all = sb("boall", [128, HQ * 512], BF16)
        den2 = sb("bden2", [64, 8 * 512])
        wo_sb = [sb(f"bwo{i}", [128, NDC * 512], BF16) for i in range(2)]
        sf_ev = sb("bsfev", [128, 512])
        s_tm = sb("bstm", [128, 4 * C])

        qk_ps = ps.enter_context(nc.psum_tensor("bqk", [128, 2048], F32))
        pv_ps = [ps.enter_context(nc.psum_tensor(f"bpv{i}", [128, 512], F32))
                 for i in range(2)]
        den_ps = ps.enter_context(nc.psum_tensor("bdenp", [128, 512], F32))
        tpf_ps = den_ps

        sK = [sems["dk0"], sems["dk1"]]
        sDQ = sems["dq"]
        it_list = [(s, j) for s in range(NSLOT) for j in range(HKV)]
        kv_tok = {}
        q_tok = {}
        it_done = {}

        with nc.Block("b") as blk:
            def pro(sy):
                E0 = 2048
                kv_load(sy, sK[0], kbuf[0], kscr, 0, E0)
                kv_load(sy, sK[0], vtbuf[0], vtscr, 0, E0)
                kv_load(sy, sK[1], kbuf[1], kscr, 1, E0)
                kv_load(sy, sK[1], vtbuf[1], vtscr, 1, E0)
                sDQ.dinc(sy.dma_start(out=qbuf[:],
                                      in_=qscr[0].rearrange("p g f -> p (g f)")))
                sDQ.dinc(sy.dma_start(
                    out=mask_sb[:].rearrange("p (i f) -> p i f", i=16),
                    in_=masks[0].transpose([1, 0, 2])))
            blk.sync(pro)
            kv_tok[0] = (sK[0], sK[0].v)
            kv_tok[1] = (sK[1], sK[1].v)
            q_tok[0] = sDQ.v

            pair_done_ve = [None]
            wo_done_te = [None]
            prev_s_store = [None]
            norm_done_ve = [None]

            for n, (s, j) in enumerate(it_list):
                if j == 0:
                    soft_barrier(blk)
                E = 512 * (4 * s + 4)
                nsub = E // 128
                par = n % 2
                masked = list(range(max(0, nsub - 16), nsub))

                for p in range(2):
                    exp_base = sAC.v
                    pvd_base = sTE.v
                    msk_base = sVE.v
                    msk_tok = {}
                    mi = 0
                    for i in masked:
                        msk_tok[i] = msk_base + mi + 1
                        mi += 1

                    def tok_qk(i, pvd_base=pvd_base):
                        return pvd_base + (1 if i == 0 else 2 * i)

                    def tok_pvd(i, pvd_base=pvd_base, nsub=nsub):
                        return pvd_base + min(2 * i + 3, 2 * nsub)

                    def tok_exp(i, exp_base=exp_base):
                        return exp_base + i + 1

                    def tb(t, s=s, j=j, p=p, E=E, nsub=nsub, par=par, n=n,
                           exp_base=exp_base, pvd_base=pvd_base, msk_tok=msk_tok):
                        ksem, ktok = kv_tok[n]
                        t.wait_ge(ksem.h, ktok)
                        if p == 0:
                            t.wait_ge(sDQ.h, q_tok[s])
                        def emit_qk(i):
                            ip = i % 2
                            if i >= 2:
                                t.wait_ge(sAC.h, tok_exp(i - 2))
                            elif exp_base + i - 1 >= 1:
                                t.wait_ge(sAC.h, exp_base + i - 1)
                            t.ldweights(kbuf[par][:, i * 128:(i + 1) * 128])
                            for gg in range(2):
                                h = (2 * p + gg) * 4 + j
                                mm = t.matmul(
                                    qk_ps[:, ip * 1024 + gg * 512:
                                          ip * 1024 + (gg + 1) * 512],
                                    kbuf[par][:, i * 128:(i + 1) * 128],
                                    qbuf[:, h * 512:(h + 1) * 512],
                                    start=True, stop=True)
                            sTE.inc(mm)
                        emit_qk(0)
                        for i in range(nsub):
                            ip = i % 2
                            if i + 1 < nsub:
                                emit_qk(i + 1)
                            if i == 0 and pair_done_ve[0] is not None:
                                t.wait_ge(sVE.h, pair_done_ve[0])
                            if i in msk_tok:
                                t.wait_ge(sVE.h, msk_tok[i])
                            else:
                                t.wait_ge(sAC.h, tok_exp(i))
                            t.ldweights(vtbuf[par][:, i * 128:(i + 1) * 128])
                            for gg in range(2):
                                t.matmul(pv_ps[gg][:],
                                         vtbuf[par][:, i * 128:(i + 1) * 128],
                                         pbuf[:, ip * 1024 + gg * 512:
                                              ip * 1024 + (gg + 1) * 512],
                                         start=(i == 0), stop=(i == nsub - 1))
                            t.ldweights(ones128b_sb[:], tile_position=(0, 0))
                            t.ldweights(ones128b_sb[:], tile_position=(0, 32))
                            for gg in range(2):
                                mm = t.matmul(
                                    den_ps[gg * 32:gg * 32 + 1, :],
                                    ones128b_sb[:],
                                    pbuf[:, ip * 1024 + gg * 512:
                                         ip * 1024 + (gg + 1) * 512],
                                    start=(i == 0), stop=(i == nsub - 1),
                                    tile_position=(0, gg * 32))
                            sTE.inc(mm)
                    blk.tensor(tb)
                    assert sTE.v == pvd_base + 2 * nsub

                    def sb_(sc, nsub=nsub, exp_base=exp_base, pvd_base=pvd_base):
                        for i in range(nsub):
                            ip = i % 2
                            sc.wait_ge(sTE.h, tok_qk(i))
                            if i >= 2:
                                sc.wait_ge(sTE.h, tok_pvd(i - 2))
                            elif pvd_base + i - 1 >= 1:
                                sc.wait_ge(sTE.h, pvd_base + i - 1)
                            sAC.inc(sc.activation(
                                out=pbuf[:, ip * 1024:(ip + 1) * 1024],
                                in_=qk_ps[:, ip * 1024:(ip + 1) * 1024],
                                func=AF.Exp, bias=cbias[:], scale=1.0))
                    blk.scalar(sb_)
                    assert sAC.v == exp_base + nsub

                    def vb(v, s=s, nsub=nsub, exp_base=exp_base, masked=masked):
                        v.wait_ge(sDQ.h, q_tok[s])
                        for i in masked:
                            ip = i % 2
                            idx = i - (nsub - 16)
                            v.wait_ge(sAC.h, tok_exp(i))
                            v.tensor_tensor(
                                out=pbuf[:, ip * 1024:ip * 1024 + 512],
                                in0=pbuf[:, ip * 1024:ip * 1024 + 512],
                                in1=mask_sb[:, idx * 512:(idx + 1) * 512],
                                op=ALU.mult)
                            sVE.inc(v.tensor_tensor(
                                out=pbuf[:, ip * 1024 + 512:ip * 1024 + 1024],
                                in0=pbuf[:, ip * 1024 + 512:ip * 1024 + 1024],
                                in1=mask_sb[:, idx * 512:(idx + 1) * 512],
                                op=ALU.mult))
                    blk.vector(vb)
                    assert sVE.v == msk_base + len(masked)

                    tok_all_pvd = sTE.v

                    def vep(v, s=s, j=j, p=p, tok_all_pvd=tok_all_pvd):
                        v.wait_ge(sTE.h, tok_all_pvd)
                        if wo_done_te[0] is not None:
                            v.wait_ge(sTE.h, wo_done_te[0])
                        col = (p * 4 + j) * 512
                        v.tensor_copy(out=den2[0:1, col:col + 512],
                                      in_=den_ps[0:1, :])
                        v.tensor_copy(out=den2[32:33, col:col + 512],
                                      in_=den_ps[32:33, :])
                        for gg in range(2):
                            h = (2 * p + gg) * 4 + j
                            mm = v.tensor_copy(out=o_all[:, h * 512:(h + 1) * 512],
                                               in_=pv_ps[gg][:])
                        sVE.inc(mm)
                    blk.vector(vep)
                    pair_done_ve[0] = sVE.v

                it_done[n] = sTE.v

                if n + 2 < len(it_list):
                    s2_, j2_ = it_list[n + 2]
                    E2 = 512 * (4 * s2_ + 4)

                    def syk(sy, s2_=s2_, j2_=j2_, E2=E2, par=par, n=n):
                        sy.wait_ge(sTE.h, it_done[n])
                        kv_load(sy, sK[par], kbuf[par], kscr, j2_, E2)
                        kv_load(sy, sK[par], vtbuf[par], vtscr, j2_, E2)
                    blk.sync(syk)
                    kv_tok[n + 2] = (sK[par], sK[par].v)

                if j == HKV - 1:
                    tok_pairs = sVE.v

                    def vnr(v, s=s, tok_pairs=tok_pairs):
                        v.wait_ge(sVE.h, tok_pairs)
                        v.reciprocal(out=den2[0:1, :], in_=den2[0:1, :])
                        v.drain()
                        v.reciprocal(out=den2[32:33, :], in_=den2[32:33, :])
                        sVE.inc(v.drain())
                    blk.vector(vnr)
                    tok_rd = sVE.v

                    nrm_mm_tok = {}
                    nrm_mu_tok = {}
                    for rnd in range(4):
                        def tnr(t, s=s, rnd=rnd):
                            t.wait_ge(sVE.h, tok_rd)
                            if rnd == 0:
                                pass
                            else:
                                t.wait_ge(sVE.h, nrm_mu_tok[rnd - 1])
                            for q4 in range(4):
                                h = rnd * 4 + q4
                                gg = h // 8
                                ggr = (h // 4) % 2
                                pcol = ((h // 8) * 4 + (h % 4)) * 512
                                mm = t.matmul(qk_ps[:, q4 * 512:(q4 + 1) * 512],
                                              onesrowf_sb[ggr * 32:ggr * 32 + 1, :],
                                              den2[ggr * 32:ggr * 32 + 1,
                                                   pcol:pcol + 512],
                                              start=True, stop=True)
                            nrm_mm_tok[rnd] = sTE.inc(mm)
                        blk.tensor(tnr)

                        def vnr2(v, s=s, rnd=rnd):
                            v.wait_ge(sTE.h, nrm_mm_tok[rnd])
                            v.tensor_tensor(
                                out=o_all[:, rnd * 2048:(rnd + 1) * 2048],
                                in0=o_all[:, rnd * 2048:(rnd + 1) * 2048],
                                in1=qk_ps[:], op=ALU.mult)
                            nrm_mu_tok[rnd] = sVE.inc(v.drain())
                        blk.vector(vnr2)
                    tok_oall = sVE.v
                    norm_done_ve[0] = tok_oall
                    if DEBUG:
                        def syo(sy, s=s):
                            sy.wait_ge(sVE.h, tok_oall)
                            sDST.dinc(sy.dma_start(out=oscr[s], in_=o_all[:]))
                        blk.sync(syo)

                    wo_tok = {}
                    wo_used = {}
                    sev_tok = {}
                    tp_tok = {}
                    tr_tok = {}

                    def gw0(g, s=s):
                        if wo_done_te[0] is not None:
                            g.wait_ge(sTE.h, wo_done_te[0])
                        wov = wo.rearrange("(dc p) m -> p dc m", p=128)
                        for blki in range(2):
                            wo_tok[blki] = sDWT.dinc(g.dma_start(
                                out=wo_sb[blki][:].rearrange(
                                    "p (dc f) -> p dc f", dc=NDC),
                                in_=wov[:, :, blki * 512:(blki + 1) * 512]))
                    blk.gpsimd(gw0)

                    for blki in range(4):
                        bpar = blki % 2
                        if blki >= 2:
                            def gw(g, blki=blki, bpar=bpar, s=s):
                                g.wait_ge(sTE.h, wo_used[blki - 2])
                                wov = wo.rearrange("(dc p) m -> p dc m", p=128)
                                wo_tok[blki] = sDWT.dinc(g.dma_start(
                                    out=wo_sb[bpar][:].rearrange(
                                        "p (dc f) -> p dc f", dc=NDC),
                                    in_=wov[:, :, blki * 512:(blki + 1) * 512]))
                            blk.gpsimd(gw)

                        ppieces = [qk_ps[:, 0:512], qk_ps[:, 512:1024],
                                   qk_ps[:, 1024:1536], qk_ps[:, 1536:2048]]
                        mm_base = sTE.v

                        def tw(t, s=s, blki=blki, bpar=bpar, tok_oall=tok_oall):
                            t.wait_ge(sDWT.h, wo_tok[blki])
                            t.wait_ge(sVE.h, tok_oall)
                            for dl in range(4):
                                if blki > 0:
                                    t.wait_ge(sAC.h, sev_tok[(blki - 1, dl)])
                                for dc in range(NDC):
                                    mm = t.matmul(
                                        ppieces[dl],
                                        wo_sb[bpar][:, dc * 512 + dl * 128:
                                                    dc * 512 + (dl + 1) * 128],
                                        o_all[:, dc * 512:(dc + 1) * 512],
                                        start=(dc == 0), stop=(dc == NDC - 1))
                                sTE.inc(mm)
                            wo_used[blki] = sTE.v
                        blk.tensor(tw)
                        assert sTE.v == mm_base + 4

                        tp_base = sTE.v
                        for dl in range(4):
                            tp_tok[(blki, dl)] = tp_base + dl + 1
                        ev_base = sAC.v
                        for dl in range(4):
                            sev_tok[(blki, dl)] = ev_base + dl + 1
                        trv_base = sVE.v
                        for dl in range(4):
                            tr_tok[(blki, dl)] = trv_base + dl + 1

                        def sw(sc, s=s, blki=blki, mm_base=mm_base,
                               ppieces=ppieces):
                            for dl in range(4):
                                dout = blki * 4 + dl
                                sc.wait_ge(sTE.h, mm_base + dl + 1)
                                if (blki, dl) != (0, 0):
                                    prev = (blki, dl - 1) if dl > 0 else (blki - 1, 3)
                                    sc.wait_ge(sTE.h, tp_tok[prev])
                                sAC.inc(sc.activation(
                                    out=sf_ev[:], in_=ppieces[dl], func=AF.Identity,
                                    bias=bo_sb[:, dout:dout + 1], scale=1.0))
                        blk.scalar(sw)
                        assert sAC.v == ev_base + 4

                        def tw2(t, s=s, blki=blki):
                            for dl in range(4):
                                t.wait_ge(sAC.h, sev_tok[(blki, dl)])
                                if (blki, dl) != (0, 0):
                                    prev = (blki, dl - 1) if dl > 0 else (blki - 1, 3)
                                    t.wait_ge(sVE.h, tr_tok[prev])
                                else:
                                    t.wait_ge(sVE.h, pair_done_ve[0])
                                for u in range(4):
                                    mm = t.transpose(tpf_ps[:, u * 128:(u + 1) * 128],
                                                     sf_ev[:, u * 128:(u + 1) * 128],
                                                     identf_sb[:])
                                sTE.inc(mm)
                        blk.tensor(tw2)
                        assert sTE.v == tp_base + 4

                        def vw(v, s=s, blki=blki):
                            stv = s_tm[:].rearrange("p (u f) -> p u f", u=4)
                            for dl in range(4):
                                dout = blki * 4 + dl
                                v.wait_ge(sTE.h, tp_tok[(blki, dl)])
                                if blki == 0 and dl == 0 and prev_s_store[0] is not None:
                                    v.wait_ge(sDST.h, prev_s_store[0])
                                dst = stv[:, :, dout * 128:(dout + 1) * 128]
                                if dl == 3:
                                    v.tensor_copy(
                                        out=dst, in_=tpf_ps[:].rearrange(
                                            "p (u f) -> p u f", u=4))
                                    sVE.inc(v.drain())
                                else:
                                    sVE.inc(v.tensor_copy(
                                        out=dst, in_=tpf_ps[:].rearrange(
                                            "p (u f) -> p u f", u=4)))
                        blk.vector(vw)
                        assert sVE.v == trv_base + 4

                    def sys(sy, s=s):
                        sy.wait_ge(sVE.h, tr_tok[(3, 3)])
                        for u in range(4):
                            sDST.dinc(sy.dma_start(out=sscr[s][u],
                                                   in_=s_tm[:, u * C:(u + 1) * C]))
                    blk.sync(sys)
                    prev_s_store[0] = sDST.v
                    wo_done_te[0] = wo_used[3]

                    if s + 1 < NSLOT:
                        def syq(sy, s=s):
                            sy.wait_ge(sTE.h, it_done[n])
                            sDQ.dinc(sy.dma_start(
                                out=qbuf[:],
                                in_=qscr[s + 1].rearrange("p g f -> p (g f)")))
                            sDQ.dinc(sy.dma_start(
                                out=mask_sb[:].rearrange("p (i f) -> p i f", i=16),
                                in_=masks[s + 1].transpose([1, 0, 2])))
                        blk.sync(syq)
                        q_tok[s + 1] = sDQ.v

            def syf(sy):
                sy.wait_ge(sDST.h, sDST.v)
            blk.sync(syf)

    with ExitStack() as ps:
        def sb(name, shape, dt=F32):
            return ps.enter_context(nc.sbuf_tensor(name, shape, dt))
        xnb = sb("cxnb", [128, NDC * 512], BF16)
        wg_g = [sb(f"cwg{i}", [128, 2048], BF16) for i in range(2)]
        wg_u = [sb(f"cwu{i}", [128, 2048], BF16) for i in range(2)]
        sg = [sb(f"csg{i}", [128, 512]) for i in range(2)]
        hsq = [sb(f"chsq{i}", [128, 512], BF16) for i in range(2)]
        h_all = sb("chall", [128, NHC * 512], BF16)
        wp_sb = [sb(f"cwp{i}", [128, 32 * 128], BF16) for i in range(2)]
        rs1 = sb("crs1", [1, 512])
        rs2 = sb("crs2", [1, 512])
        rdbc = sb("crdbc", [128, 512])
        tff = sb("ctff", [128, 512])
        gfm = [sb(f"cgfm{i}", [128, 512]) for i in range(2)]
        g_tm = sb("cgtm", [128, 4 * C])
        xs = sb("cxs", [128, C])
        ss = sb("css", [128, C])
        ys = [sb(f"cys{i}", [128, C]) for i in range(2)]

        gps = [ps.enter_context(nc.psum_tensor(f"cgps{i}", [128, 512], F32))
               for i in range(2)]
        ups = [ps.enter_context(nc.psum_tensor(f"cups{i}", [128, 512], F32))
               for i in range(2)]
        h2_ps = ps.enter_context(nc.psum_tensor("ch2", [1, 512], F32))
        pps = [ps.enter_context(nc.psum_tensor(f"cpps{i}", [128, 512], F32))
               for i in range(2)]
        tpf2_ps = ps.enter_context(nc.psum_tensor("ctpf2", [128, 512], F32))

        sDG, sDP, sDXS, sDY = sems["dg"], sems["dp"], sems["dxs"], sems["dy"]

        with nc.Block("c") as blk:
            prev_slot_gu_te = [None]
            prev_slot_pj_te = [None]
            prev_slot_cb_ve = [None]
            prev_yst = [None, None]

            for s in range(NSLOT):
                soft_barrier(blk)

                def syxn(sy, s=s):
                    if prev_slot_gu_te[0] is not None:
                        sy.wait_ge(sTE.h, prev_slot_gu_te[0])
                    sDQ_ = sems["dq"]
                    sDQ_.dinc(sy.dma_start(
                        out=xnb[:], in_=xnscr[s].rearrange("p dc f -> p (dc f)")))
                blk.sync(syxn)
                tok_xnb = sems["dq"].v

                gld_tok = {}
                gmm_tok = {}
                umm_tok = {}
                silu_tok = {}
                h_tok = {}
                hsq_tok = {}
                h2_tok = {}

                def syg0(sy, s=s):
                    if prev_slot_gu_te[0] is not None:
                        sy.wait_ge(sTE.h, prev_slot_gu_te[0])
                    guv = wguq.rearrange("fc p dc f -> fc p (dc f)")
                    sDG.dinc(sy.dma_start(out=wg_g[0][:], in_=guv[0]))
                    gld_tok[0] = sDG.dinc(sy.dma_start(out=wg_u[0][:], in_=guv[64]))
                blk.sync(syg0)

                for fc in range(NHC):
                    fpar = fc % 2
                    if fc + 1 < NHC:
                        def syg(sy, s=s, fc=fc):
                            if fc >= 1:
                                sy.wait_ge(sTE.h, umm_tok[fc - 1])
                            guv = wguq.rearrange("fc p dc f -> fc p (dc f)")
                            sDG.dinc(sy.dma_start(out=wg_g[(fc + 1) % 2][:],
                                                  in_=guv[fc + 1]))
                            gld_tok[fc + 1] = sDG.dinc(sy.dma_start(
                                out=wg_u[(fc + 1) % 2][:], in_=guv[64 + fc + 1]))
                        blk.sync(syg)

                    def tg(t, s=s, fc=fc, fpar=fpar):
                        t.wait_ge(sDG.h, gld_tok[fc])
                        t.wait_ge(sems["dq"].h, tok_xnb)
                        if prev_slot_pj_te[0] is not None and fc == 0:
                            pass
                        if fc >= 2:
                            t.wait_ge(sAC.h, silu_tok[fc - 2])
                            t.wait_ge(sVE.h, h_tok[fc - 2])
                        for dc in range(NDC):
                            mm = t.matmul(gps[fpar][:],
                                          wg_g[fpar][:, dc * 128:(dc + 1) * 128],
                                          xnb[:, dc * 512:(dc + 1) * 512],
                                          start=(dc == 0), stop=(dc == NDC - 1))
                        gmm_tok[fc] = sTE.inc(mm)
                        for dc in range(NDC):
                            mm = t.matmul(ups[fpar][:],
                                          wg_u[fpar][:, dc * 128:(dc + 1) * 128],
                                          xnb[:, dc * 512:(dc + 1) * 512],
                                          start=(dc == 0), stop=(dc == NDC - 1))
                        umm_tok[fc] = sTE.inc(mm)
                        if fc >= 2:
                            t.wait_ge(sVE.h, hsq_tok[fc - 2])
                            h2_tok[fc - 2] = sTE.inc(t.matmul(
                                h2_ps[:], ones128b_sb[:], hsq[(fc - 2) % 2][:],
                                start=(fc - 2 == 0), stop=False))
                    blk.tensor(tg)

                    def sg_(sc, s=s, fc=fc, fpar=fpar):
                        sc.wait_ge(sTE.h, gmm_tok[fc])
                        if fc >= 2:
                            sc.wait_ge(sVE.h, h_tok[fc - 2])
                        silu_tok[fc] = sAC.inc(sc.activation(
                            out=sg[fpar][:], in_=gps[fpar][:], func=AF.Silu))
                    blk.scalar(sg_)

                    def vg(v, s=s, fc=fc, fpar=fpar):
                        v.wait_ge(sAC.h, silu_tok[fc])
                        v.wait_ge(sTE.h, umm_tok[fc])
                        if prev_slot_pj_te[0] is not None:
                            v.wait_ge(sTE.h, prev_slot_pj_te[0])
                        h_tok[fc] = sVE.inc(v.tensor_tensor(
                            out=h_all[:, fc * 512:(fc + 1) * 512],
                            in0=sg[fpar][:], in1=ups[fpar][:], op=ALU.mult))
                        if fc >= 1:
                            if fc >= 3:
                                v.wait_ge(sTE.h, h2_tok[fc - 3])
                            hsl = h_all[:, (fc - 1) * 512:fc * 512]
                            hsq_tok[fc - 1] = sVE.inc(v.tensor_tensor(
                                out=hsq[(fc - 1) % 2][:], in0=hsl, in1=hsl,
                                op=ALU.mult))
                    blk.vector(vg)

                def sgt(v, s=s):
                    v.wait_ge(sTE.h, h2_tok[NHC - 3])
                    hsl = h_all[:, (NHC - 1) * 512:NHC * 512]
                    hsq_tok[NHC - 1] = sVE.inc(v.tensor_tensor(
                        out=hsq[(NHC - 1) % 2][:], in0=hsl, in1=hsl, op=ALU.mult))
                blk.vector(sgt)

                def tgt(t, s=s):
                    t.wait_ge(sVE.h, hsq_tok[NHC - 2])
                    h2_tok[NHC - 2] = sTE.inc(t.matmul(
                        h2_ps[:], ones128b_sb[:], hsq[(NHC - 2) % 2][:],
                        start=False, stop=False))
                    t.wait_ge(sVE.h, hsq_tok[NHC - 1])
                    h2_tok[NHC - 1] = sTE.inc(t.matmul(
                        h2_ps[:], ones128b_sb[:], hsq[(NHC - 1) % 2][:],
                        start=False, stop=True))
                blk.tensor(tgt)
                tok_h2 = sTE.v

                def vr(v, s=s):
                    v.wait_ge(sTE.h, tok_h2)
                    sVE.inc(v.tensor_scalar(out=rs1[:], in0=h2_ps[:],
                                            scalar1=1.0 / HF, scalar2=RMS_EPS,
                                            op0=ALU.mult, op1=ALU.add))
                    v.drain()
                blk.vector(vr)
                tok_rs = sVE.v

                def sr(sc, s=s):
                    sc.wait_ge(sVE.h, tok_rs)
                    sAC.inc(sc.sqrt(out=rs2[:], in_=rs1[:]))
                blk.scalar(sr)
                tok_rsq = sAC.v

                def vr2(v, s=s):
                    v.wait_ge(sAC.h, tok_rsq)
                    sVE.inc(v.reciprocal(out=rs1[:], in_=rs2[:]))
                    v.drain()
                blk.vector(vr2)
                tok_rr = sVE.v

                def tr(t, s=s):
                    t.wait_ge(sVE.h, tok_rr)
                    sTE.inc(t.matmul(tpf2_ps[:], onesrowf_sb[0:1, :], rs1[:],
                                     start=True, stop=True))
                blk.tensor(tr)
                tok_rbc = sTE.v

                def vr3(v, s=s):
                    v.wait_ge(sTE.h, tok_rbc)
                    sVE.inc(v.tensor_copy(out=rdbc[:], in_=tpf2_ps[:]))
                blk.vector(vr3)
                tok_rdbc = sVE.v

                pld_tok = {}
                pmm_tok = {}
                half_mm_tok = {}
                tffv_tok = {}
                gsil_tok = {}
                gtp_tok = {}
                gtr_tok = {}
                def wp_src(do, hh):
                    return wpq[hh * 4:(hh + 1) * 4, do].transpose([1, 0, 2, 3])

                def wp_dst(buf):
                    return buf[:].rearrange("p (a b f) -> p a b f", a=4, b=8)

                def syp0(sy, s=s):
                    if prev_slot_pj_te[0] is not None:
                        sy.wait_ge(sTE.h, prev_slot_pj_te[0])
                    sDP.dinc(sy.dma_start(out=wp_dst(wp_sb[0]), in_=wp_src(0, 0)))
                    pld_tok[0] = sDP.v
                    sDP.dinc(sy.dma_start(out=wp_dst(wp_sb[1]), in_=wp_src(0, 1)))
                    pld_tok[1] = sDP.v
                blk.sync(syp0)

                for do in range(NDC):
                    ppar = do % 2
                    for hh in range(2):
                        un = 2 * do + hh
                        buf = wp_sb[un % 2]
                        if un + 2 < 2 * NDC:
                            do2, hh2 = (un + 2) // 2, (un + 2) % 2

                            def syp(sy, s=s, un=un, do2=do2, hh2=hh2):
                                sy.wait_ge(sTE.h, half_mm_tok[un])
                                sDP.dinc(sy.dma_start(
                                    out=wp_dst(wp_sb[un % 2]),
                                    in_=wp_src(do2, hh2)))
                                pld_tok[un + 2] = sDP.v

                        def tp(t, s=s, do=do, hh=hh, un=un, ppar=ppar, buf=buf):
                            t.wait_ge(sDP.h, pld_tok[un])
                            if hh == 0 and do >= 2:
                                t.wait_ge(sVE.h, tffv_tok[do - 2])
                            for hl in range(32):
                                hc = hh * 32 + hl
                                mm = t.matmul(pps[ppar][:],
                                              buf[:, hl * 128:(hl + 1) * 128],
                                              h_all[:, hc * 512:(hc + 1) * 512],
                                              start=(hc == 0), stop=(hc == NHC - 1))
                            half_mm_tok[un] = sTE.inc(mm)
                        blk.tensor(tp)
                        if un + 2 < 2 * NDC:
                            blk.sync(syp)
                    pmm_tok[do] = half_mm_tok[2 * do + 1]

                    def vp(v, s=s, do=do, ppar=ppar):
                        v.wait_ge(sTE.h, pmm_tok[do])
                        v.wait_ge(sVE.h, tok_rdbc)
                        if do >= 1:
                            v.wait_ge(sAC.h, gsil_tok[do - 1])
                        v.tensor_tensor(out=tff[:], in0=pps[ppar][:],
                                        in1=rdbc[:], op=ALU.mult)
                        tffv_tok[do] = sVE.inc(v.drain())
                    blk.vector(vp)

                    def sp(sc, s=s, do=do):
                        sc.wait_ge(sVE.h, tffv_tok[do])
                        if do >= 2:
                            sc.wait_ge(sTE.h, gtp_tok[do - 2])
                        gsil_tok[do] = sAC.inc(sc.activation(
                            out=gfm[do % 2][:], in_=tff[:], func=AF.Silu))
                    blk.scalar(sp)

                    def tp2(t, s=s, do=do):
                        t.wait_ge(sAC.h, gsil_tok[do])
                        if do >= 1:
                            t.wait_ge(sVE.h, gtr_tok[do - 1])
                        else:
                            t.wait_ge(sVE.h, tok_rdbc)
                        if prev_slot_cb_ve[0] is not None:
                            t.wait_ge(sVE.h, prev_slot_cb_ve[0])
                        for u in range(4):
                            mm = t.transpose(tpf2_ps[:, u * 128:(u + 1) * 128],
                                             gfm[do % 2][:, u * 128:(u + 1) * 128],
                                             identf_sb[:])
                        gtp_tok[do] = sTE.inc(mm)
                    blk.tensor(tp2)

                    def vp2(v, s=s, do=do):
                        gv = g_tm[:].rearrange("p (u f) -> p u f", u=4)
                        v.wait_ge(sTE.h, gtp_tok[do])
                        gtr_tok[do] = sVE.inc(v.tensor_copy(
                            out=gv[:, :, do * 128:(do + 1) * 128],
                            in_=tpf2_ps[:].rearrange("p (u f) -> p u f", u=4)))
                    blk.vector(vp2)

                prev_slot_gu_te[0] = umm_tok[NHC - 1]
                prev_slot_pj_te[0] = pmm_tok[NDC - 1]

                cmb_tok = {}
                yst_tok = {}
                for u in range(4):
                    def syc1(sy, s=s, u=u):
                        if u >= 1:
                            sy.wait_ge(sVE.h, cmb_tok[u - 1])
                        elif prev_slot_cb_ve[0] is not None:
                            sy.wait_ge(sVE.h, prev_slot_cb_ve[0])
                        sDXS.dinc(sy.dma_start(
                            out=xs[:], in_=xown[s * 512 + u * 128:
                                              s * 512 + (u + 1) * 128, :]))
                        sDXS.dinc(sy.dma_start(out=ss[:], in_=sscr[s][u]))
                    blk.sync(syc1)
                    tok_ld2 = sDXS.v

                    def vc(v, s=s, u=u, tok_ld2=tok_ld2):
                        v.wait_ge(sDXS.h, tok_ld2)
                        v.wait_ge(sVE.h, gtr_tok[NDC - 1])
                        if u >= 2:
                            v.wait_ge(sDY.h, yst_tok[u - 2])
                        elif prev_yst[u] is not None:
                            v.wait_ge(sDY.h, prev_yst[u])
                        v.tensor_tensor(out=ss[:], in0=ss[:],
                                        in1=g_tm[:, u * C:(u + 1) * C], op=ALU.mult)
                        v.drain()
                        v.tensor_tensor(out=ys[u % 2][:], in0=xs[:], in1=ss[:],
                                        op=ALU.add)
                        cmb_tok[u] = sVE.inc(v.drain())
                    blk.vector(vc)

                    def syy(sy, s=s, u=u):
                        sy.wait_ge(sVE.h, cmb_tok[u])
                        yst_tok[u] = sDY.dinc(sy.dma_start(
                            out=y[s * 512 + u * 128: s * 512 + (u + 1) * 128, :],
                            in_=ys[u % 2][:]))
                    blk.sync(syy)
                prev_slot_cb_ve[0] = cmb_tok[3]
                prev_yst = [yst_tok[2], yst_tok[3]]

            def syf(sy):
                sy.wait_ge(sDY.h, sems["dy"].v)
            blk.sync(syf)

    es.close()
    return nc


_NC_CACHE = None


def kernel(**inputs):
    global _NC_CACHE
    per_core = _host_prep(inputs)
    if _NC_CACHE is None:
        _NC_CACHE = build_nc()
    res = run_bass_kernel_spmd(_NC_CACHE, per_core, core_ids=list(range(8)))
    out = np.zeros((B, T, C), dtype=np.float32)
    for c in range(8):
        b, k = c // 4, c % 4
        yc = np.asarray(res.results[c]["y"])
        for s in range(NSLOT):
            r0 = (4 * s + k) * 512
            out[b, r0:r0 + 512] = yc[s * 512:(s + 1) * 512]
    return out


# version 17
# speedup vs baseline: 1.7202x; 1.0038x over previous
import numpy as np
import ml_dtypes
import concourse.bass as bass
import concourse.mybir as mybir
from concourse.bass_utils import run_bass_kernel_spmd
from contextlib import ExitStack, nullcontext as _nullctx

F32 = mybir.dt.float32
BF16 = mybir.dt.bfloat16
AF = mybir.ActivationFunctionType
ALU = mybir.AluOpType
AX = mybir.AxisListType

B, T, C = 2, 8192, 2048
HQ, HKV, HD = 16, 4, 128
HF = 8192
NDC = C // 128
NHC = HF // 128
NSLOT = 4
LN_EPS = 1e-5
RMS_EPS = float(np.finfo(np.float32).eps)
QTHRESH = 0.5009765625

DEBUG = False


class Sem:
    def __init__(self, nc, es, name):
        self.h = es.enter_context(nc.semaphore(name))
        self.v = 0

    def inc(self, inst, k=1):
        inst.then_inc(self.h, k)
        self.v += k
        return self.v

    def dinc(self, inst):
        return self.inc(inst, 16)



def _rope_perm():
    p = np.empty(128, dtype=np.int64)
    p[:64] = np.arange(64) * 2
    p[64:] = np.arange(64) * 2 + 1
    return p


def _rope_tables():
    i = np.arange(0, HD, 2, dtype=np.float32) / np.float32(HD)
    theta = (np.float64(1.0) / np.float64(10000.0) ** i.astype(np.float64)).astype(np.float32)
    t = np.arange(T, dtype=np.float32)
    ang = (t[:, None] * theta[None, :]).astype(np.float32)
    cos = np.cos(ang.astype(np.float64)).astype(np.float32)
    sin = np.sin(ang.astype(np.float64)).astype(np.float32)
    Ctab = np.ascontiguousarray(np.concatenate([cos.T, cos.T], axis=0))
    Stab = np.ascontiguousarray(np.concatenate([-sin.T, sin.T], axis=0))
    return Ctab, Stab


def _masks_for_core(k):
    out = np.zeros((NSLOT, 16, 128, 512), dtype=np.float32)
    for s in range(NSLOT):
        b = 4 * s + k
        Es = 512 * (4 * s + 4)
        q_abs = 512 * b + np.arange(512)
        for idx in range(16):
            k0 = Es - 2048 + idx * 128
            key_abs = k0 + np.arange(128)
            out[s, idx] = (key_abs[:, None] <= q_abs[None, :]).astype(np.float32)
    return np.ascontiguousarray(out.astype(ml_dtypes.bfloat16))


def _host_prep(inputs):
    x = np.asarray(inputs["L0_attn_x"], dtype=np.float32)
    Wq = np.asarray(inputs["L0_attn_Wq"], dtype=np.float32)
    bq = np.asarray(inputs["L0_attn_bq"], dtype=np.float32)
    Wk = np.asarray(inputs["L0_attn_Wk"], dtype=np.float32)
    bk = np.asarray(inputs["L0_attn_bk"], dtype=np.float32)
    Wv = np.asarray(inputs["L0_attn_Wv"], dtype=np.float32)
    bv = np.asarray(inputs["L0_attn_bv"], dtype=np.float32)
    Wo = np.asarray(inputs["L0_attn_Wo"], dtype=np.float32)
    bo = np.asarray(inputs["L0_attn_bo"], dtype=np.float32)
    Wgu = np.asarray(inputs["L0_ffn_w_gate_up"], dtype=np.float32)
    Wp = np.asarray(inputs["L0_ffn_w_proj"], dtype=np.float32)

    perm = _rope_perm()
    qperm = np.concatenate([h * 128 + perm for h in range(HQ)])
    kperm = np.concatenate([h * 128 + perm for h in range(HKV)])
    Ctab, Stab = _rope_tables()

    shared = {
        "wqp": np.ascontiguousarray(Wq[:, qperm]),
        "wkp": np.ascontiguousarray(Wk[:, kperm]),
        "wv": Wv,
        "wo": Wo,
        "bqt": np.ascontiguousarray(bq[qperm].reshape(HQ, 128).T),
        "bkt": np.ascontiguousarray(bk[kperm].reshape(HKV, 128).T),
        "bvt": np.ascontiguousarray(bv.reshape(HKV, 128).T),
        "bot": np.ascontiguousarray(bo.reshape(NDC, 128).T),

        "cosk": Ctab,
        "sink": Stab,
        "identf": np.eye(128, dtype=np.float32),
        "identb": np.ascontiguousarray(np.eye(128, dtype=np.float32).astype(ml_dtypes.bfloat16)),
        "ind2": np.ascontiguousarray(
            (np.kron(np.eye(2, dtype=np.float32), np.ones((64, 1), np.float32)) / 64.0
             ).astype(ml_dtypes.bfloat16)),
        "e2": np.ascontiguousarray(
            np.kron(np.eye(2, dtype=np.float32), np.ones((1, 64), np.float32)
                    ).astype(ml_dtypes.bfloat16)),
        "ones128b": np.ones((128, 1), dtype=ml_dtypes.bfloat16),
        "onesrowf": np.ones((64, 128), dtype=np.float32),
    }

    WguT = np.ascontiguousarray(Wgu.T)
    WpT = np.ascontiguousarray(Wp.T)
    per_core = []
    for c in range(8):
        b, k = c // 4, c % 4
        own = np.concatenate([np.arange(512) + (4 * s + k) * 512 for s in range(NSLOT)])
        d = dict(shared)
        d["wgut"] = np.ascontiguousarray(WguT[:, c * 2048:(c + 1) * 2048])
        d["wpt"] = np.ascontiguousarray(WpT[c * 1024:(c + 1) * 1024, :])
        d["xfull"] = np.ascontiguousarray(x[b])
        d["xown"] = np.ascontiguousarray(x[b][own])
        d["cosq"] = np.ascontiguousarray((0.25 * Ctab[:, own]).astype(np.float32))
        d["sinq"] = np.ascontiguousarray((0.25 * Stab[:, own]).astype(np.float32))
        d["cosko"] = np.ascontiguousarray(Ctab[:, own])
        d["sinko"] = np.ascontiguousarray(Stab[:, own])
        d["masks"] = _masks_for_core(k)
        per_core.append(d)
    return per_core



def build_nc():
    nc = bass.Bass("TRN2", target_bir_lowering=False)
    es = ExitStack()

    def din(name, shape, dt=F32):
        return nc.dram_tensor(name, list(shape), dt, kind="ExternalInput").ap()

    xfull = din("xfull", [T, C])
    xown = din("xown", [NSLOT * 512, C])
    wqp = din("wqp", [C, C])
    wkp = din("wkp", [C, 512])
    wv = din("wv", [C, 512])
    wo = din("wo", [C, C])
    bqt = din("bqt", [128, HQ])
    bkt = din("bkt", [128, HKV])
    bvt = din("bvt", [128, HKV])
    bot = din("bot", [128, NDC])
    wgut = din("wgut", [C, 2048])
    wpt = din("wpt", [1024, C])
    cosk = din("cosk", [128, T])
    sink = din("sink", [128, T])
    cosq = din("cosq", [128, NSLOT * 512])
    sinq = din("sinq", [128, NSLOT * 512])
    cosko = din("cosko", [128, NSLOT * 512])
    sinko = din("sinko", [128, NSLOT * 512])
    masks = din("masks", [NSLOT, 16, 128, 512], BF16)
    identf = din("identf", [128, 128])
    identb = din("identb", [128, 128], BF16)
    ind2 = din("ind2", [128, 2], BF16)
    e2 = din("e2", [2, 128], BF16)
    ones128b = din("ones128b", [128, 1], BF16)
    onesrowf = din("onesrowf", [64, 128])

    y = nc.dram_tensor("y", [NSLOT * 512, C], F32, kind="ExternalOutput").ap()

    skind = "ExternalOutput" if DEBUG else "Internal"
    xnscr = nc.dram_tensor("xnscr", [NSLOT, 128, NDC, 512], BF16, kind=skind).ap()
    qscr = nc.dram_tensor("qscr", [NSLOT, 128, HQ, 512], BF16, kind=skind).ap()
    kshard = nc.dram_tensor("kshard", [HKV, 128, NSLOT * 512], BF16,
                            kind="Internal").ap()
    vtshard = nc.dram_tensor("vtshard", [HKV, 128, NSLOT * 512], BF16,
                             kind="Internal").ap()
    kscr = nc.dram_tensor("kscr", [4, HKV, 128, NSLOT * 512], BF16,
                          kind="Internal").ap()
    vtscr = nc.dram_tensor("vtscr", [4, HKV, 128, NSLOT * 512], BF16,
                           kind="Internal").ap()
    mkbounce = nc.dram_tensor("mkbounce", [1, 1], F32, kind="Internal").ap()
    mkout = nc.dram_tensor("mkout", [1, 1], F32, kind="Internal").ap()

    def kv_load(sy, sem, dstbuf, scr, j, E):
        tok = None
        v3 = scr[:, j].rearrange("r p (s f) -> p s r f", f=512)
        for sg in range(E // 2048):
            dst3 = dstbuf[:, sg * 2048:(sg + 1) * 2048].rearrange(
                "p (r f) -> p r f", f=512)
            tok = sem.dinc(sy.dma_start(out=dst3, in_=v3[:, sg]))
        return tok
    wguq_sh = nc.dram_tensor("wguq_sh", [16, 128, NDC, 128], BF16, kind="Internal").ap()
    wpq_sh = nc.dram_tensor("wpq_sh", [NDC, 128, 8, 128], BF16, kind="Internal").ap()
    wguq = nc.dram_tensor("wguq", [2 * HF // 128, 128, NDC, 128], BF16,
                          kind="Internal", addr_space="Shared").ap()
    wpq = nc.dram_tensor("wpq", [8, NDC, 128, 8, 128], BF16,
                         kind="Internal", addr_space="Shared").ap()
    sscr = nc.dram_tensor("sscr", [NSLOT, 4, 128, C], F32, kind=skind).ap()
    oscr = nc.dram_tensor("oscr", [NSLOT, 128, HQ * 512], BF16, kind=skind).ap() if DEBUG else None
    dnscr = nc.dram_tensor("dnscr", [NSLOT, HKV, 2, 64, 512], F32, kind=skind).ap() if DEBUG else None

    sems = {n: Sem(nc, es, n) for n in
            ["ve", "ac", "te", "gp", "dln", "dw", "dwt", "dst",
             "dk0", "dk1", "dq", "dg", "dp", "dxs", "dy"]}
    sVE, sAC, sTE, sGP = sems["ve"], sems["ac"], sems["te"], sems["gp"]
    sDLN, sDW, sDWT, sDST = sems["dln"], sems["dw"], sems["dwt"], sems["dst"]

    def gsb(name, shape, dt=F32):
        return es.enter_context(nc.sbuf_tensor(name, shape, dt))
    cbias = gsb("cbias", [128, 1])
    maxq2 = gsb("maxq2", [1, 1])
    maxk2 = gsb("maxk2", [1, 1])
    identb_sb = gsb("identb_sb", [128, 128], BF16)
    identf_sb = gsb("identf_sb", [128, 128])
    ind2_sb = gsb("ind2_sb", [128, 2], BF16)
    e2_sb = gsb("e2_sb", [2, 128], BF16)
    ones128b_sb = gsb("ones128b_sb", [128, 1], BF16)
    onesrowf_sb = gsb("onesrowf_sb", [64, 128])
    bq_sb = gsb("bq_sb", [128, HQ])
    bk_sb = gsb("bk_sb", [128, HKV])
    bv_sb = gsb("bv_sb", [128, HKV])
    bo_sb = gsb("bo_sb", [128, NDC])

    def emit_ln(blk, xbufs, stat4, xn_tm, tok_x):
        sums, ssq, mean, t1, msq, var, sd, rstd, scr = (
            stat4[k] for k in ["sums", "ssq", "mean", "t1", "msq", "var",
                               "sd", "rstd", "scratch"])

        def v1(v):
            for u in range(4):
                v.wait_ge(sDLN.h, tok_x[u])
                v.reduce_sum(out=sums[:, u:u + 1], in_=xbufs[u][:], axis=AX.X)
                v.scalar_tensor_tensor(out=scr[:], in0=xbufs[u][:], scalar=1.0,
                                       in1=xbufs[u][:], op0=ALU.mult, op1=ALU.mult,
                                       accum_out=ssq[:, u:u + 1])
            v.drain()
            v.tensor_scalar(out=mean[:], in0=sums[:], scalar1=1.0 / 2048,
                            scalar2=None, op0=ALU.mult)
            v.tensor_scalar(out=t1[:], in0=ssq[:], scalar1=1.0 / 2048,
                            scalar2=LN_EPS, op0=ALU.mult, op1=ALU.add)
            v.drain()
            v.tensor_tensor(out=msq[:], in0=mean[:], in1=mean[:], op=ALU.mult)
            v.drain()
            sVE.inc(v.tensor_tensor(out=var[:], in0=t1[:], in1=msq[:],
                                    op=ALU.subtract))
        blk.vector(v1)
        tok_var = sVE.v

        def s1(sc):
            sc.wait_ge(sVE.h, tok_var)
            sAC.inc(sc.sqrt(out=sd[:], in_=var[:]))
        blk.scalar(s1)
        tok_sd = sAC.v

        def v2(v):
            v.wait_ge(sAC.h, tok_sd)
            v.reciprocal(out=rstd[:], in_=sd[:])
            v.drain()
            for u in range(4):
                v.tensor_scalar(
                    out=xn_tm[u][:], in0=xbufs[u][:],
                    scalar1=mean[:, u:u + 1], scalar2=rstd[:, u:u + 1],
                    op0=ALU.subtract, op1=ALU.mult)
                sVE.inc(v.drain())
        blk.vector(v2)
        return sVE.v

    def emit_tm2fm(blk, xn_tm, xn_fm, tpb, tok_xn_last):
        tp_toks = []

        def tb(t):
            for u in range(4):
                t.wait_ge(sVE.h, tok_xn_last - 3 + u)
                for r in range(2):
                    if tp_toks:
                        t.wait_ge(sVE.h, ev_base + len(tp_toks))
                    for q in range(8):
                        dc = r * 8 + q
                        mm = t.transpose(tpb[:, q * 128:(q + 1) * 128],
                                         xn_tm[u][:, dc * 128:(dc + 1) * 128],
                                         identb_sb[:])
                    tp_toks.append(sTE.inc(mm))
        ev_base = sVE.v

        def vb(v):
            fmv = xn_fm[:].rearrange("p (dc f) -> p dc f", dc=NDC)
            ti = 0
            for u in range(4):
                for r in range(2):
                    v.wait_ge(sTE.h, tp_toks[ti]); ti += 1
                    dst = fmv[:, r * 8:(r + 1) * 8, u * 128:(u + 1) * 128]
                    v.tensor_copy(out=dst, in_=tpb[:].rearrange(
                        "p (q f) -> p q f", q=8))
                    sVE.inc(v.drain())
        blk.tensor(tb)
        blk.vector(vb)
        return sVE.v

    def soft_barrier(blk):
        a, v, t = sAC.v, sVE.v, sTE.v

        def tb_(e):
            e.wait_ge(sAC.h, a)
            e.wait_ge(sVE.h, v)

        def sb_(e):
            e.wait_ge(sTE.h, t)
            e.wait_ge(sVE.h, v)

        def vb_(e):
            e.wait_ge(sTE.h, t)
            e.wait_ge(sAC.h, a)
        blk.tensor(tb_)
        blk.scalar(sb_)
        blk.vector(vb_)

    blk_shared = nc.Block("phase_a")
    blk_shared.__enter__()
    xn_store_tok = {}

    with ExitStack() as ps:
        def sb(name, shape, dt=F32):
            return ps.enter_context(nc.sbuf_tensor(name, shape, dt))
        wq_sb = sb("wq_sb", [128, NDC * C], BF16)
        xbufs = [sb(f"a0xb{u}", [128, C]) for u in range(4)]
        stat4 = {k: sb("a0st_" + k, [128, 4]) for k in
                 ["sums", "ssq", "mean", "t1", "msq", "var", "sd", "rstd"]}
        stat4["scratch"] = sb("a0st_scratch", [128, C])
        xn_tm = [sb(f"a0xntm{u}", [128, C], BF16) for u in range(4)]
        xn_fm = sb("a0xnfm", [128, NDC * 512], BF16)
        qf = [sb(f"a0qf{i}", [128, 512]) for i in range(2)]
        rsw = sb("a0rsw", [128, 512])
        rt1 = sb("a0rt1", [128, 512])
        rt2 = sb("a0rt2", [128, 512])
        qstage = sb("a0qstage", [128, HQ * 512], BF16)
        sq16 = sb("a0sq16", [128, 512], BF16)
        qn_red = sb("a0qn_red", [1, 1])
        cq_sb = sb("a0cq", [128, NSLOT * 512])
        sq_sb = sb("a0sq", [128, NSLOT * 512])

        qps = [ps.enter_context(nc.psum_tensor(f"a0qps{i}", [128, 512], F32))
               for i in range(2)]
        tpb = ps.enter_context(nc.psum_tensor("a0tpb", [128, 1024], BF16))
        qn_ps = ps.enter_context(nc.psum_tensor("a0qn_ps", [1, 512], F32))

        with _nullctx(blk_shared) as blk:
            def sy0(sy):
                for dst, src in [(identb_sb, identb), (identf_sb, identf),
                                 (ind2_sb, ind2), (e2_sb, e2),
                                 (ones128b_sb, ones128b), (onesrowf_sb, onesrowf),
                                 (bq_sb, bqt), (bk_sb, bkt), (bv_sb, bvt),
                                 (bo_sb, bot), (cq_sb, cosq), (sq_sb, sinq)]:
                    sDW.dinc(sy.dma_start(out=dst[:], in_=src))
            blk.sync(sy0)
            tok_const = sDW.v

            def gp0(g):
                for dc in range(NDC):
                    sDWT.dinc(g.dma_start(out=wq_sb[:, dc * C:(dc + 1) * C],
                                          in_=wqp[dc * 128:(dc + 1) * 128, :]))
                g.memset(qn_red[:], 0.0)
                sGP.inc(g.memset(maxq2[:], 0.0))
            blk.gpsimd(gp0)
            tok_wq = sDWT.v
            tok_gz = sGP.v

            for s in range(NSLOT):
                soft_barrier(blk)
                tok_x = []
                ve_now = sVE.v

                def syx(sy, s=s, ve_now=ve_now):
                    sy.wait_ge(sVE.h, ve_now)
                    for u in range(4):
                        tok_x.append(sDLN.dinc(sy.dma_start(
                            out=xbufs[u][:],
                            in_=xown[s * 512 + u * 128: s * 512 + (u + 1) * 128, :])))
                blk.sync(syx)

                tok_xn = emit_ln(blk, xbufs, stat4, xn_tm, tok_x)
                tok_fm = emit_tm2fm(blk, xn_tm, xn_fm, tpb, tok_xn)

                def sxn(sy, s=s):
                    sy.wait_ge(sVE.h, tok_fm)
                    xn_store_tok[s] = sDST.dinc(sy.dma_start(
                        out=xnscr[s],
                        in_=xn_fm[:].rearrange("p (dc f) -> p dc f", dc=NDC)))
                blk.sync(sxn)

                qmm_tok = []
                qev_base = sAC.v
                rope_base = sVE.v

                def tq(t, s=s):
                    t.wait_ge(sDWT.h, tok_wq)
                    t.wait_ge(sVE.h, tok_fm)
                    for g in range(HQ):
                        if g >= 2:
                            t.wait_ge(sAC.h, qev_base + (g - 2) + 1)
                        for dc in range(NDC):
                            mm = t.matmul(
                                qps[g % 2][:],
                                wq_sb[:, dc * C + g * 128: dc * C + (g + 1) * 128],
                                xn_fm[:, dc * 512:(dc + 1) * 512],
                                start=(dc == 0), stop=(dc == NDC - 1))
                        qmm_tok.append(sTE.inc(mm))
                blk.tensor(tq)

                def sq(sc, s=s):
                    sc.wait_ge(sDW.h, tok_const)
                    for g in range(HQ):
                        sc.wait_ge(sTE.h, qmm_tok[g])
                        if g >= 2:
                            sc.wait_ge(sVE.h, rope_base + (g - 2) + 1)
                        sAC.inc(sc.activation(out=qf[g % 2][:], in_=qps[g % 2][:],
                                              func=AF.Identity, bias=bq_sb[:, g:g + 1],
                                              scale=1.0))
                blk.scalar(sq)

                def vq(v, s=s):
                    v.wait_ge(sDW.h, tok_const)
                    for g in range(HQ):
                        v.wait_ge(sAC.h, qev_base + g + 1)
                        q = qf[g % 2]
                        v.tensor_copy(out=rsw[0:64, :], in_=q[64:128, :])
                        v.tensor_copy(out=rsw[64:128, :], in_=q[0:64, :])
                        v.tensor_tensor(out=rt1[:], in0=q[:],
                                        in1=cq_sb[:, s * 512:(s + 1) * 512], op=ALU.mult)
                        v.tensor_tensor(out=rt2[:], in0=rsw[:],
                                        in1=sq_sb[:, s * 512:(s + 1) * 512], op=ALU.mult)
                        sVE.inc(v.tensor_tensor(
                            out=qstage[:, g * 512:(g + 1) * 512],
                            in0=rt1[:], in1=rt2[:], op=ALU.add))
                    v.drain()
                blk.vector(vq)
                tok_rope_last = sVE.v

                for g in range(HQ):
                    def s2(v, g=g, s=s):
                        v.wait_ge(sVE.h, tok_rope_last - (HQ - 1) + g)
                        if g >= 1:
                            v.wait_ge(sTE.h, qn_mm_tok[g - 1])
                        qsl = qstage[:, g * 512:(g + 1) * 512]
                        sVE.inc(v.tensor_tensor(out=sq16[:], in0=qsl, in1=qsl,
                                                op=ALU.mult))
                    blk.vector(s2)
                    tok_sq16 = sVE.v

                    def t2(t, g=g):
                        t.wait_ge(sDW.h, tok_const)
                        t.wait_ge(sVE.h, tok_sq16)
                        qn_mm_tok[g] = sTE.inc(t.matmul(
                            qn_ps[:], ones128b_sb[:], sq16[:], start=True, stop=True))
                    if g == 0:
                        qn_mm_tok = {}
                    blk.tensor(t2)

                    def v2(v, g=g):
                        v.wait_ge(sTE.h, qn_mm_tok[g])
                        v.wait_ge(sGP.h, tok_gz)
                        v.tensor_reduce(out=qn_red[:], in_=qn_ps[:], axis=AX.X,
                                        op=ALU.max)
                        v.drain()
                        sVE.inc(v.tensor_tensor(out=maxq2[:], in0=maxq2[:],
                                                in1=qn_red[:], op=ALU.max))
                        v.drain()
                    blk.vector(v2)

                def syq(sy, s=s):
                    sy.wait_ge(sVE.h, sVE.v)
                    sDST.dinc(sy.dma_start(
                        out=qscr[s],
                        in_=qstage[:].rearrange("p (g f) -> p g f", g=HQ)))
                blk.sync(syq)

    _a0_ac, _a0_ve, _a0_te = sAC.v, sVE.v, sTE.v

    with ExitStack() as ps:
        def sb(name, shape, dt=F32):
            return ps.enter_context(nc.sbuf_tensor(name, shape, dt))
        wk_sb = sb("wk_sb", [128, NDC * 512], BF16)
        wv_sb = sb("wv_sb", [128, NDC * 512], BF16)
        cko_sb = sb("cko_sb", [128, NSLOT * 512])
        sko_sb = sb("sko_sb", [128, NSLOT * 512])
        xnb1 = [sb(f"a1xnb{i}", [128, NDC * 512], BF16) for i in range(2)]
        kf = [sb(f"a1kf{i}", [128, 512]) for i in range(2)]
        rsw = sb("a1rsw", [128, 512])
        rt1 = sb("a1rt1", [128, 512])
        rt2 = sb("a1rt2", [128, 512])
        krope = [sb(f"a1krope{i}", [128, 512], BF16) for i in range(2)]
        vf16 = [sb(f"a1vf16{i}", [128, 512], BF16) for i in range(2)]
        vt_stage = [sb(f"a1vt{i}", [128, 512], BF16) for i in range(2)]
        sq16 = [sb(f"a1sq16{i}", [128, 512], BF16) for i in range(2)]
        kn_red = sb("a1kn_red", [1, 1])

        ps1 = ExitStack()
        kps = [ps1.enter_context(nc.psum_tensor(f"a1kps{i}", [128, 512], F32))
               for i in range(2)]
        tpb = ps1.enter_context(nc.psum_tensor("a1tpb", [128, 1024], BF16))
        kn_ps = ps1.enter_context(nc.psum_tensor("a1kn_ps", [1, 512], F32))

        blk = blk_shared
        def gp0(g):
            g.wait_ge(sAC.h, _a0_ac)
            g.wait_ge(sVE.h, _a0_ve)
            g.wait_ge(sTE.h, _a0_te)
            for dc in range(NDC):
                sDWT.dinc(g.dma_start(out=wk_sb[:, dc * 512:(dc + 1) * 512],
                                      in_=wkp[dc * 128:(dc + 1) * 128, :]))
                sDWT.dinc(g.dma_start(out=wv_sb[:, dc * 512:(dc + 1) * 512],
                                      in_=wv[dc * 128:(dc + 1) * 128, :]))
            g.memset(kn_red[:], 0.0)
            sGP.inc(g.memset(maxk2[:], 0.0))
        blk.gpsimd(gp0)
        tok_wkv = sDWT.v
        tok_gz = sGP.v

        def sy0(sy):
            sy.wait_ge(sAC.h, _a0_ac)
            sy.wait_ge(sVE.h, _a0_ve)
            sy.wait_ge(sTE.h, _a0_te)
            sDW.dinc(sy.dma_start(out=cko_sb[:], in_=cosko))
            sDW.dinc(sy.dma_start(out=sko_sb[:], in_=sinko))
        blk.sync(sy0)
        tok_tab = sDW.v

        xn_tok = {}
        kev_tok = {}
        krope_tok = {}
        kst_tok = {}
        vev_tok = {}
        vtp_tok = {}
        vts_tok = {}
        vst_tok = {}
        kn_mm_tok = {}
        kn_v_tok = {}
        xn_used = {}

        for s in range(NSLOT):
            def syx(sy, s=s):
                if s == 0:
                    sy.wait_ge(sAC.h, _a0_ac)
                    sy.wait_ge(sVE.h, _a0_ve)
                    sy.wait_ge(sTE.h, _a0_te)
                sy.wait_ge(sDST.h, xn_store_tok[s])
                if s >= 2:
                    sy.wait_ge(sTE.h, xn_used[s - 2])
                xn_tok[s] = sDLN.dinc(sy.dma_start(
                    out=xnb1[s % 2][:],
                    in_=xnscr[s].rearrange("p dc f -> p (dc f)")))
            blk.sync(syx)

            for j in range(HKV):
                n = s * HKV + j
                fb = n % 2

                def tk(t, s=s, j=j, n=n):
                    t.wait_ge(sDWT.h, tok_wkv)
                    t.wait_ge(sDLN.h, xn_tok[s])
                    if n - 1 in kev_tok:
                        t.wait_ge(sAC.h, kev_tok[n - 1])
                    for dc in range(NDC):
                        mm = t.matmul(
                            kps[0][:],
                            wk_sb[:, dc * 512 + j * 128: dc * 512 + (j + 1) * 128],
                            xnb1[s % 2][:, dc * 512:(dc + 1) * 512],
                            start=(dc == 0), stop=(dc == NDC - 1))
                    sTE.inc(mm)
                blk.tensor(tk)
                tok_kmm = sTE.v

                def sk(sc, j=j, n=n, fb=fb):
                    sc.wait_ge(sTE.h, tok_kmm)
                    if n - 2 in krope_tok:
                        sc.wait_ge(sVE.h, krope_tok[n - 2])
                    kev_tok[n] = sAC.inc(sc.activation(
                        out=kf[fb][:], in_=kps[0][:], func=AF.Identity,
                        bias=bk_sb[:, j:j + 1], scale=1.0))
                blk.scalar(sk)

                def vk(v, j=j, s=s, n=n, fb=fb):
                    v.wait_ge(sAC.h, kev_tok[n])
                    v.wait_ge(sDW.h, tok_tab)
                    if n - 2 in kst_tok:
                        v.wait_ge(sDST.h, kst_tok[n - 2])
                    q = kf[fb]
                    v.tensor_copy(out=rsw[0:64, :], in_=q[64:128, :])
                    v.tensor_copy(out=rsw[64:128, :], in_=q[0:64, :])
                    v.tensor_tensor(out=rt1[:], in0=q[:],
                                    in1=cko_sb[:, s * 512:(s + 1) * 512],
                                    op=ALU.mult)
                    v.tensor_tensor(out=rt2[:], in0=rsw[:],
                                    in1=sko_sb[:, s * 512:(s + 1) * 512],
                                    op=ALU.mult)
                    v.tensor_tensor(out=krope[fb][:], in0=rt1[:], in1=rt2[:],
                                    op=ALU.add)
                    if n - 2 in kn_mm_tok:
                        v.wait_ge(sTE.h, kn_mm_tok[n - 2])
                    v.tensor_tensor(out=sq16[fb][:], in0=krope[fb][:],
                                    in1=krope[fb][:], op=ALU.mult)
                    krope_tok[n] = sVE.inc(v.drain())
                blk.vector(vk)

                def syk(sy, j=j, s=s, n=n, fb=fb):
                    sy.wait_ge(sVE.h, krope_tok[n])
                    kst_tok[n] = sDST.dinc(sy.dma_start(
                        out=kshard[j][:, s * 512:(s + 1) * 512],
                        in_=krope[fb][:]))
                blk.sync(syk)

                def t2(t, n=n, fb=fb):
                    t.wait_ge(sVE.h, krope_tok[n])
                    if n - 1 in kn_v_tok:
                        t.wait_ge(sVE.h, kn_v_tok[n - 1])
                    kn_mm_tok[n] = sTE.inc(t.matmul(
                        kn_ps[:], ones128b_sb[:], sq16[fb][:],
                        start=True, stop=True))
                blk.tensor(t2)

                def v2(v, n=n):
                    v.wait_ge(sTE.h, kn_mm_tok[n])
                    v.wait_ge(sGP.h, tok_gz)
                    v.tensor_reduce(out=kn_red[:], in_=kn_ps[:], axis=AX.X,
                                    op=ALU.max)
                    v.drain()
                    kn_v_tok[n] = sVE.inc(v.tensor_tensor(
                        out=maxk2[:], in0=maxk2[:], in1=kn_red[:], op=ALU.max))
                    v.drain()
                blk.vector(v2)

                def tv(t, j=j, s=s, n=n):
                    if n - 1 in vev_tok:
                        t.wait_ge(sAC.h, vev_tok[n - 1])
                    for dc in range(NDC):
                        mm = t.matmul(
                            kps[1][:],
                            wv_sb[:, dc * 512 + j * 128: dc * 512 + (j + 1) * 128],
                            xnb1[s % 2][:, dc * 512:(dc + 1) * 512],
                            start=(dc == 0), stop=(dc == NDC - 1))
                    sTE.inc(mm)
                    xn_used[s] = sTE.v
                blk.tensor(tv)
                tok_vmm = sTE.v

                def sv(sc, j=j, n=n, fb=fb):
                    sc.wait_ge(sTE.h, tok_vmm)
                    if n - 2 in vtp_tok:
                        sc.wait_ge(sTE.h, vtp_tok[n - 2])
                    vev_tok[n] = sAC.inc(sc.activation(
                        out=vf16[fb][:], in_=kps[1][:], func=AF.Identity,
                        bias=bv_sb[:, j:j + 1], scale=1.0))
                blk.scalar(sv)

                def tv2(t, j=j, n=n, fb=fb):
                    t.wait_ge(sAC.h, vev_tok[n])
                    if n - 1 in vts_tok:
                        t.wait_ge(sVE.h, vts_tok[n - 1])
                    for q in range(4):
                        mm = t.transpose(tpb[:, q * 128:(q + 1) * 128],
                                         vf16[fb][:, q * 128:(q + 1) * 128],
                                         identb_sb[:])
                    vtp_tok[n] = sTE.inc(mm)
                blk.tensor(tv2)

                def vv(v, j=j, n=n, fb=fb):
                    v.wait_ge(sTE.h, vtp_tok[n])
                    if n - 2 in vst_tok:
                        v.wait_ge(sDST.h, vst_tok[n - 2])
                    v.tensor_copy(out=vt_stage[fb][:], in_=tpb[:, 0:512])
                    vts_tok[n] = sVE.inc(v.drain())
                blk.vector(vv)

                def syv(sy, j=j, s=s, n=n, fb=fb):
                    sy.wait_ge(sVE.h, vts_tok[n])
                    vst_tok[n] = sDST.dinc(sy.dma_start(
                        out=vtshard[j][:, s * 512:(s + 1) * 512],
                        in_=vt_stage[fb][:]))
                blk.sync(syv)

        def syb(sy):
            sy.wait_ge(sVE.h, sVE.v)
            sDST.dinc(sy.dma_start(out=mkbounce, in_=maxk2[:]))
        blk.sync(syb)

        ps1.close()

        def sb(name, shape, dt=F32):
            return ps.enter_context(nc.sbuf_tensor(name, shape, dt))
        wb = [sb(f"qwb{i}", [128, 2048], BF16) for i in range(2)]
        aw = sb("qaw", [128, 2048], BF16)
        sgn = sb("qsgn", [128, 2048], BF16)
        sc16 = sb("qsc16", [64, 2048], BF16)
        thr = sb("qthr", [128, 2048])
        scbc = sb("qscbc", [128, 2048], BF16)
        mgt = sb("qmgt", [128, 2048], BF16)
        qv = sb("qqv", [128, 2048], BF16)
        wqv = sb("qwqv", [128, 2048], BF16)
        d1t = sb("qd1", [128, 2048], BF16)
        rr = [sb(f"qrr{i}", [128, 2048], BF16) for i in range(2)]
        qctmp = sb("qctmp", [1, 1])
        qctmp2 = sb("qctmp2", [1, 1])

        scl_ps = ps.enter_context(nc.psum_tensor("qscl_ps", [2, 2048], F32))
        bc_ps = ps.enter_context(nc.psum_tensor("qbc_ps", [128, 2048], F32))
        qcb_ps = bc_ps[:, 0:1]
        _g_ac, _g_ve, _g_te = sAC.v, sVE.v, sTE.v

        def _qgate_t(t):
            t.wait_ge(sAC.h, _g_ac)
            t.wait_ge(sVE.h, _g_ve)

        def _qgate_v(v):
            v.wait_ge(sAC.h, _g_ac)
            v.wait_ge(sTE.h, _g_te)
        blk.tensor(_qgate_t)
        blk.vector(_qgate_v)

        tiles = []
        for dc in range(NDC):
            tiles.append(("gu", dc, 0))
        for hcl in range(8):
            tiles.append(("p", hcl, 0))
        NT = len(tiles)

        tok_ld = [None] * NT
        tok_sums = [None] * NT
        tok_scev = [None] * NT
        tok_bc = [None] * NT
        tok_bcev = [None] * NT
        tok_thr = [None] * NT
        tok_sgn = [None] * NT
        tok_aw = [None] * NT
        tok_q = [None] * NT
        tok_wq = [None] * NT
        tok_r = [None] * NT
        tok_st = [None] * NT

        if True:
            for ti2, (kind, a, b_) in enumerate(tiles):
                par = ti2 % 2

                def gl(g, ti2=ti2, kind=kind, a=a, b_=b_, par=par):
                    if kind == "gu":
                        src = wgut[a * 128:(a + 1) * 128, :]
                    else:
                        src = wpt[a * 128:(a + 1) * 128, :]
                    if ti2 >= 2:
                        g.wait_ge(sVE.h, tok_r[ti2 - 2])
                    tok_ld[ti2] = sDWT.dinc(g.dma_start(out=wb[par][:], in_=src))
                blk.gpsimd(gl)

                def s1(sc, ti2=ti2, par=par):
                    sc.wait_ge(sDWT.h, tok_ld[ti2])
                    if ti2 > 0:
                        sc.wait_ge(sTE.h, tok_sums[ti2 - 1])
                        sc.wait_ge(sVE.h, tok_q[ti2 - 1])
                    tok_aw[ti2] = sAC.inc(sc.activation(out=aw[:], in_=wb[par][:],
                                                        func=AF.Abs))
                    tok_sgn[ti2] = sAC.inc(sc.sign(out=sgn[:], in_=wb[par][:]))
                blk.scalar(s1)

                def t1(t, ti2=ti2):
                    t.wait_ge(sAC.h, tok_aw[ti2])
                    if ti2 > 0:
                        t.wait_ge(sVE.h, tok_scev[ti2 - 1])
                    for i in range(4):
                        mm = t.matmul(scl_ps[:, i * 512:(i + 1) * 512], ind2_sb[:],
                                      aw[:, i * 512:(i + 1) * 512],
                                      start=True, stop=True)
                    tok_sums[ti2] = sTE.inc(mm)
                blk.tensor(t1)

                def v1(v, ti2=ti2):
                    v.wait_ge(sTE.h, tok_sums[ti2])
                    if ti2 > 0:
                        v.wait_ge(sTE.h, tok_bc[ti2 - 1])
                    v.tensor_copy(out=sc16[0:2, :], in_=scl_ps[:])
                    tok_scev[ti2] = sVE.inc(v.tensor_scalar(
                        out=sc16[0:2, :], in0=sc16[0:2, :], scalar1=1e-8,
                        scalar2=None, op0=ALU.max))
                blk.vector(v1)

                def t2(t, ti2=ti2):
                    t.wait_ge(sVE.h, tok_scev[ti2])
                    if ti2 > 0:
                        t.wait_ge(sAC.h, tok_bcev[ti2 - 1])
                    for i in range(4):
                        mm = t.matmul(bc_ps[:, i * 512:(i + 1) * 512], e2_sb[:],
                                      sc16[0:2, i * 512:(i + 1) * 512],
                                      start=True, stop=True)
                    tok_bc[ti2] = sTE.inc(mm)
                blk.tensor(t2)

                def s2(sc, ti2=ti2):
                    sc.wait_ge(sTE.h, tok_bc[ti2])
                    if ti2 > 0:
                        sc.wait_ge(sVE.h, tok_wq[ti2 - 1])
                    tok_thr[ti2] = sAC.inc(sc.activation(out=thr[:], in_=bc_ps[:],
                                                         func=AF.Copy, scale=QTHRESH))
                    tok_bcev[ti2] = sAC.inc(sc.activation(out=scbc[:], in_=bc_ps[:],
                                                          func=AF.Copy))
                blk.scalar(s2)

                def v2(v, ti2=ti2, par=par):
                    v.wait_ge(sAC.h, tok_bcev[ti2])
                    v.wait_ge(sAC.h, tok_sgn[ti2])
                    if ti2 >= 2:
                        v.wait_ge(sDST.h, tok_st[ti2 - 2])
                    v.tensor_tensor(out=mgt[:], in0=aw[:], in1=thr[:], op=ALU.is_gt)
                    tok_q[ti2] = sVE.inc(v.tensor_tensor(out=qv[:], in0=mgt[:],
                                                         in1=sgn[:], op=ALU.mult))
                    tok_wq[ti2] = sVE.inc(v.tensor_tensor(out=wqv[:], in0=qv[:],
                                                          in1=scbc[:], op=ALU.mult))
                    v.tensor_tensor(out=d1t[:], in0=wqv[:], in1=wb[par][:],
                                    op=ALU.subtract)
                    tok_r[ti2] = sVE.inc(v.tensor_tensor(out=rr[par][:], in0=wb[par][:],
                                                         in1=d1t[:], op=ALU.add))
                blk.vector(v2)

                def sy1(sy, ti2=ti2, kind=kind, a=a, b_=b_, par=par):
                    sy.wait_ge(sVE.h, tok_r[ti2])
                    src = rr[par][:].rearrange("p (i f) -> p i f", i=16)
                    if kind == "gu":
                        dst = wguq_sh[:, :, a, :].transpose([1, 0, 2])
                    else:
                        dst = wpq_sh[:, :, a, :].transpose([1, 0, 2])
                    tok_st[ti2] = sDST.dinc(sy.dma_start(out=dst, in_=src))
                blk.sync(sy1)

            def gcc(g):
                g.wait_ge(sDST.h, sDST.v)
                grp4 = [[0, 1, 2, 3], [4, 5, 6, 7]]
                g.collective_compute(
                    "AllGather", ALU.bypass, replica_groups=grp4,
                    ins=[kshard.rearrange("j p f -> (j p f)").unsqueeze(0).opt()],
                    outs=[kscr.rearrange("r j p f -> r (j p f)").opt()],
                ).then_inc(sGP.h, 1)
                sGP.v += 1
                g.collective_compute(
                    "AllGather", ALU.bypass, replica_groups=grp4,
                    ins=[vtshard.rearrange("j p f -> (j p f)").unsqueeze(0).opt()],
                    outs=[vtscr.rearrange("r j p f -> r (j p f)").opt()],
                ).then_inc(sGP.h, 1)
                sGP.v += 1
                g.collective_compute(
                    "AllReduce", ALU.max, replica_groups=[list(range(8))],
                    ins=[mkbounce.opt()], outs=[mkout.opt()],
                ).then_inc(sGP.h, 1)
                sGP.v += 1
                g.collective_compute(
                    "AllGather", ALU.bypass,
                    replica_groups=[list(range(8))],
                    ins=[wguq_sh.rearrange("a p c f -> a (p c f)").opt()],
                    outs=[wguq.rearrange("a p c f -> a (p c f)").opt()],
                ).then_inc(sGP.h, 1)
                sGP.v += 1
                g.collective_compute(
                    "AllGather", ALU.bypass,
                    replica_groups=[list(range(8))],
                    ins=[wpq_sh.rearrange("d p c f -> (d p c f)").unsqueeze(0).opt()],
                    outs=[wpq.rearrange("s d p c f -> s (d p c f)").opt()],
                ).then_inc(sGP.h, 1)
                sGP.v += 1
                g.wait_ge(sGP.h, sGP.v)
            blk.gpsimd(gcc)
            tok_cc = sGP.v

            def syb2(sy):
                sy.wait_ge(sGP.h, tok_cc)
                sDW.dinc(sy.dma_start(out=maxk2[:], in_=mkout))
            blk.sync(syb2)
            tok_mk = sDW.v

            def sc_c(sc):
                sc.wait_ge(sDW.h, tok_mk)
                sAC.inc(sc.sqrt(out=qctmp[:], in_=maxq2[:]))
                sAC.inc(sc.sqrt(out=qctmp2[:], in_=maxk2[:]))
            blk.scalar(sc_c)
            tok_cs = sAC.v

            def v_c(v):
                v.wait_ge(sAC.h, tok_cs)
                v.tensor_tensor(out=qctmp[:], in0=qctmp[:], in1=qctmp2[:],
                                op=ALU.mult)
                v.drain()
                v.tensor_scalar(out=qctmp[:], in0=qctmp[:], scalar1=-1.0,
                                scalar2=None, op0=ALU.mult)
                sVE.inc(v.drain())
            blk.vector(v_c)
            tok_cn = sVE.v

            def t_c(t):
                t.wait_ge(sVE.h, tok_cn)
                sTE.inc(t.matmul(qcb_ps, onesrowf_sb[0:1, :], qctmp[:],
                                 start=True, stop=True))
            blk.tensor(t_c)
            tok_cb = sTE.v

            def v_c2(v):
                v.wait_ge(sTE.h, tok_cb)
                sVE.inc(v.tensor_copy(out=cbias[:], in_=qcb_ps))
            blk.vector(v_c2)

            def syf(sy):
                sy.wait_ge(sDST.h, sDST.v)
                sy.wait_ge(sGP.h, sGP.v)
                sy.wait_ge(sVE.h, sVE.v)
            blk.sync(syf)

    blk_shared.__exit__(None, None, None)

    with ExitStack() as ps:
        def sb(name, shape, dt=F32):
            return ps.enter_context(nc.sbuf_tensor(name, shape, dt))
        kbuf = [sb(f"bk{i}", [128, T], BF16) for i in range(2)]
        vtbuf = [sb(f"bvt{i}", [128, T], BF16) for i in range(2)]
        qbuf = sb("bq", [128, HQ * 512], BF16)
        mask_sb = sb("bmask", [128, 16 * 512], BF16)
        pbuf = sb("bp", [128, 2048], BF16)
        o_all = sb("boall", [128, HQ * 512], BF16)
        den2 = sb("bden2", [64, 8 * 512])
        wo_sb = [sb(f"bwo{i}", [128, NDC * 512], BF16) for i in range(2)]
        sf_ev = sb("bsfev", [128, 512])
        s_tm = sb("bstm", [128, 4 * C])

        qk_ps = ps.enter_context(nc.psum_tensor("bqk", [128, 2048], F32))
        pv_ps = [ps.enter_context(nc.psum_tensor(f"bpv{i}", [128, 512], F32))
                 for i in range(2)]
        den_ps = ps.enter_context(nc.psum_tensor("bdenp", [128, 512], F32))
        tpf_ps = den_ps

        sK = [sems["dk0"], sems["dk1"]]
        sDQ = sems["dq"]
        it_list = [(s, j) for s in range(NSLOT) for j in range(HKV)]
        kv_tok = {}
        q_tok = {}
        it_done = {}

        with nc.Block("b") as blk:
            def pro(sy):
                E0 = 2048
                kv_load(sy, sK[0], kbuf[0], kscr, 0, E0)
                kv_load(sy, sK[0], vtbuf[0], vtscr, 0, E0)
                kv_load(sy, sK[1], kbuf[1], kscr, 1, E0)
                kv_load(sy, sK[1], vtbuf[1], vtscr, 1, E0)
                sDQ.dinc(sy.dma_start(out=qbuf[:],
                                      in_=qscr[0].rearrange("p g f -> p (g f)")))
                sDQ.dinc(sy.dma_start(
                    out=mask_sb[:].rearrange("p (i f) -> p i f", i=16),
                    in_=masks[0].transpose([1, 0, 2])))
            blk.sync(pro)
            kv_tok[0] = (sK[0], sK[0].v)
            kv_tok[1] = (sK[1], sK[1].v)
            q_tok[0] = sDQ.v

            pair_done_ve = [None]
            wo_done_te = [None]
            prev_s_store = [None]
            norm_done_ve = [None]

            for n, (s, j) in enumerate(it_list):
                if j == 0:
                    soft_barrier(blk)
                E = 512 * (4 * s + 4)
                nsub = E // 128
                par = n % 2
                masked = list(range(max(0, nsub - 16), nsub))

                for p in range(2):
                    exp_base = sAC.v
                    pvd_base = sTE.v
                    msk_base = sVE.v
                    msk_tok = {}
                    mi = 0
                    for i in masked:
                        msk_tok[i] = msk_base + mi + 1
                        mi += 1

                    def tok_qk(i, pvd_base=pvd_base):
                        return pvd_base + (1 if i == 0 else 2 * i)

                    def tok_pvd(i, pvd_base=pvd_base, nsub=nsub):
                        return pvd_base + min(2 * i + 3, 2 * nsub)

                    def tok_exp(i, exp_base=exp_base):
                        return exp_base + i + 1

                    def tb(t, s=s, j=j, p=p, E=E, nsub=nsub, par=par, n=n,
                           exp_base=exp_base, pvd_base=pvd_base, msk_tok=msk_tok):
                        ksem, ktok = kv_tok[n]
                        t.wait_ge(ksem.h, ktok)
                        if p == 0:
                            t.wait_ge(sDQ.h, q_tok[s])
                        def emit_qk(i):
                            ip = i % 2
                            if i >= 2:
                                t.wait_ge(sAC.h, tok_exp(i - 2))
                            elif exp_base + i - 1 >= 1:
                                t.wait_ge(sAC.h, exp_base + i - 1)
                            t.ldweights(kbuf[par][:, i * 128:(i + 1) * 128])
                            for gg in range(2):
                                h = (2 * p + gg) * 4 + j
                                mm = t.matmul(
                                    qk_ps[:, ip * 1024 + gg * 512:
                                          ip * 1024 + (gg + 1) * 512],
                                    kbuf[par][:, i * 128:(i + 1) * 128],
                                    qbuf[:, h * 512:(h + 1) * 512],
                                    start=True, stop=True)
                            sTE.inc(mm)
                        emit_qk(0)
                        for i in range(nsub):
                            ip = i % 2
                            if i + 1 < nsub:
                                emit_qk(i + 1)
                            if i == 0 and pair_done_ve[0] is not None:
                                t.wait_ge(sVE.h, pair_done_ve[0])
                            if i in msk_tok:
                                t.wait_ge(sVE.h, msk_tok[i])
                            else:
                                t.wait_ge(sAC.h, tok_exp(i))
                            t.ldweights(vtbuf[par][:, i * 128:(i + 1) * 128])
                            for gg in range(2):
                                t.matmul(pv_ps[gg][:],
                                         vtbuf[par][:, i * 128:(i + 1) * 128],
                                         pbuf[:, ip * 1024 + gg * 512:
                                              ip * 1024 + (gg + 1) * 512],
                                         start=(i == 0), stop=(i == nsub - 1))
                            t.ldweights(ones128b_sb[:], tile_position=(0, 0))
                            t.ldweights(ones128b_sb[:], tile_position=(0, 32))
                            for gg in range(2):
                                mm = t.matmul(
                                    den_ps[gg * 32:gg * 32 + 1, :],
                                    ones128b_sb[:],
                                    pbuf[:, ip * 1024 + gg * 512:
                                         ip * 1024 + (gg + 1) * 512],
                                    start=(i == 0), stop=(i == nsub - 1),
                                    tile_position=(0, gg * 32))
                            sTE.inc(mm)
                    blk.tensor(tb)
                    assert sTE.v == pvd_base + 2 * nsub

                    def sb_(sc, nsub=nsub, exp_base=exp_base, pvd_base=pvd_base):
                        for i in range(nsub):
                            ip = i % 2
                            sc.wait_ge(sTE.h, tok_qk(i))
                            if i >= 2:
                                sc.wait_ge(sTE.h, tok_pvd(i - 2))
                            elif pvd_base + i - 1 >= 1:
                                sc.wait_ge(sTE.h, pvd_base + i - 1)
                            sAC.inc(sc.activation(
                                out=pbuf[:, ip * 1024:(ip + 1) * 1024],
                                in_=qk_ps[:, ip * 1024:(ip + 1) * 1024],
                                func=AF.Exp, bias=cbias[:], scale=1.0))
                    blk.scalar(sb_)
                    assert sAC.v == exp_base + nsub

                    def vb(v, s=s, nsub=nsub, exp_base=exp_base, masked=masked):
                        v.wait_ge(sDQ.h, q_tok[s])
                        for i in masked:
                            ip = i % 2
                            idx = i - (nsub - 16)
                            v.wait_ge(sAC.h, tok_exp(i))
                            v.tensor_tensor(
                                out=pbuf[:, ip * 1024:ip * 1024 + 512],
                                in0=pbuf[:, ip * 1024:ip * 1024 + 512],
                                in1=mask_sb[:, idx * 512:(idx + 1) * 512],
                                op=ALU.mult)
                            sVE.inc(v.tensor_tensor(
                                out=pbuf[:, ip * 1024 + 512:ip * 1024 + 1024],
                                in0=pbuf[:, ip * 1024 + 512:ip * 1024 + 1024],
                                in1=mask_sb[:, idx * 512:(idx + 1) * 512],
                                op=ALU.mult))
                    blk.vector(vb)
                    assert sVE.v == msk_base + len(masked)

                    tok_all_pvd = sTE.v

                    def vep(v, s=s, j=j, p=p, tok_all_pvd=tok_all_pvd):
                        v.wait_ge(sTE.h, tok_all_pvd)
                        if wo_done_te[0] is not None:
                            v.wait_ge(sTE.h, wo_done_te[0])
                        col = (p * 4 + j) * 512
                        v.tensor_copy(out=den2[0:1, col:col + 512],
                                      in_=den_ps[0:1, :])
                        v.tensor_copy(out=den2[32:33, col:col + 512],
                                      in_=den_ps[32:33, :])
                        for gg in range(2):
                            h = (2 * p + gg) * 4 + j
                            mm = v.tensor_copy(out=o_all[:, h * 512:(h + 1) * 512],
                                               in_=pv_ps[gg][:])
                        sVE.inc(mm)
                    blk.vector(vep)
                    pair_done_ve[0] = sVE.v

                it_done[n] = sTE.v

                if n + 2 < len(it_list):
                    s2_, j2_ = it_list[n + 2]
                    E2 = 512 * (4 * s2_ + 4)

                    def syk(sy, s2_=s2_, j2_=j2_, E2=E2, par=par, n=n):
                        sy.wait_ge(sTE.h, it_done[n])
                        kv_load(sy, sK[par], kbuf[par], kscr, j2_, E2)
                        kv_load(sy, sK[par], vtbuf[par], vtscr, j2_, E2)
                    blk.sync(syk)
                    kv_tok[n + 2] = (sK[par], sK[par].v)

                if j == HKV - 1:
                    tok_pairs = sVE.v

                    def vnr(v, s=s, tok_pairs=tok_pairs):
                        v.wait_ge(sVE.h, tok_pairs)
                        v.reciprocal(out=den2[0:1, :], in_=den2[0:1, :])
                        v.drain()
                        v.reciprocal(out=den2[32:33, :], in_=den2[32:33, :])
                        sVE.inc(v.drain())
                    blk.vector(vnr)
                    tok_rd = sVE.v

                    nrm_mm_tok = {}
                    nrm_mu_tok = {}
                    for rnd in range(4):
                        def tnr(t, s=s, rnd=rnd):
                            t.wait_ge(sVE.h, tok_rd)
                            if rnd == 0:
                                pass
                            else:
                                t.wait_ge(sVE.h, nrm_mu_tok[rnd - 1])
                            for q4 in range(4):
                                h = rnd * 4 + q4
                                gg = h // 8
                                ggr = (h // 4) % 2
                                pcol = ((h // 8) * 4 + (h % 4)) * 512
                                mm = t.matmul(qk_ps[:, q4 * 512:(q4 + 1) * 512],
                                              onesrowf_sb[ggr * 32:ggr * 32 + 1, :],
                                              den2[ggr * 32:ggr * 32 + 1,
                                                   pcol:pcol + 512],
                                              start=True, stop=True)
                            nrm_mm_tok[rnd] = sTE.inc(mm)
                        blk.tensor(tnr)

                        def vnr2(v, s=s, rnd=rnd):
                            v.wait_ge(sTE.h, nrm_mm_tok[rnd])
                            v.tensor_tensor(
                                out=o_all[:, rnd * 2048:(rnd + 1) * 2048],
                                in0=o_all[:, rnd * 2048:(rnd + 1) * 2048],
                                in1=qk_ps[:], op=ALU.mult)
                            nrm_mu_tok[rnd] = sVE.inc(v.drain())
                        blk.vector(vnr2)
                    tok_oall = sVE.v
                    norm_done_ve[0] = tok_oall
                    if DEBUG:
                        def syo(sy, s=s):
                            sy.wait_ge(sVE.h, tok_oall)
                            sDST.dinc(sy.dma_start(out=oscr[s], in_=o_all[:]))
                        blk.sync(syo)

                    wo_tok = {}
                    wo_used = {}
                    sev_tok = {}
                    tp_tok = {}
                    tr_tok = {}

                    def gw0(g, s=s):
                        if wo_done_te[0] is not None:
                            g.wait_ge(sTE.h, wo_done_te[0])
                        wov = wo.rearrange("(dc p) m -> p dc m", p=128)
                        for blki in range(2):
                            wo_tok[blki] = sDWT.dinc(g.dma_start(
                                out=wo_sb[blki][:].rearrange(
                                    "p (dc f) -> p dc f", dc=NDC),
                                in_=wov[:, :, blki * 512:(blki + 1) * 512]))
                    blk.gpsimd(gw0)

                    for blki in range(4):
                        bpar = blki % 2
                        if blki >= 2:
                            def gw(g, blki=blki, bpar=bpar, s=s):
                                g.wait_ge(sTE.h, wo_used[blki - 2])
                                wov = wo.rearrange("(dc p) m -> p dc m", p=128)
                                wo_tok[blki] = sDWT.dinc(g.dma_start(
                                    out=wo_sb[bpar][:].rearrange(
                                        "p (dc f) -> p dc f", dc=NDC),
                                    in_=wov[:, :, blki * 512:(blki + 1) * 512]))
                            blk.gpsimd(gw)

                        ppieces = [qk_ps[:, 0:512], qk_ps[:, 512:1024],
                                   qk_ps[:, 1024:1536], qk_ps[:, 1536:2048]]
                        mm_base = sTE.v

                        def tw(t, s=s, blki=blki, bpar=bpar, tok_oall=tok_oall):
                            t.wait_ge(sDWT.h, wo_tok[blki])
                            t.wait_ge(sVE.h, tok_oall)
                            for dl in range(4):
                                if blki > 0:
                                    t.wait_ge(sAC.h, sev_tok[(blki - 1, dl)])
                                for dc in range(NDC):
                                    mm = t.matmul(
                                        ppieces[dl],
                                        wo_sb[bpar][:, dc * 512 + dl * 128:
                                                    dc * 512 + (dl + 1) * 128],
                                        o_all[:, dc * 512:(dc + 1) * 512],
                                        start=(dc == 0), stop=(dc == NDC - 1))
                                sTE.inc(mm)
                            wo_used[blki] = sTE.v
                        blk.tensor(tw)
                        assert sTE.v == mm_base + 4

                        tp_base = sTE.v
                        for dl in range(4):
                            tp_tok[(blki, dl)] = tp_base + dl + 1
                        ev_base = sAC.v
                        for dl in range(4):
                            sev_tok[(blki, dl)] = ev_base + dl + 1
                        trv_base = sVE.v
                        for dl in range(4):
                            tr_tok[(blki, dl)] = trv_base + dl + 1

                        def sw(sc, s=s, blki=blki, mm_base=mm_base,
                               ppieces=ppieces):
                            for dl in range(4):
                                dout = blki * 4 + dl
                                sc.wait_ge(sTE.h, mm_base + dl + 1)
                                if (blki, dl) != (0, 0):
                                    prev = (blki, dl - 1) if dl > 0 else (blki - 1, 3)
                                    sc.wait_ge(sTE.h, tp_tok[prev])
                                sAC.inc(sc.activation(
                                    out=sf_ev[:], in_=ppieces[dl], func=AF.Identity,
                                    bias=bo_sb[:, dout:dout + 1], scale=1.0))
                        blk.scalar(sw)
                        assert sAC.v == ev_base + 4

                        def tw2(t, s=s, blki=blki):
                            for dl in range(4):
                                t.wait_ge(sAC.h, sev_tok[(blki, dl)])
                                if (blki, dl) != (0, 0):
                                    prev = (blki, dl - 1) if dl > 0 else (blki - 1, 3)
                                    t.wait_ge(sVE.h, tr_tok[prev])
                                else:
                                    t.wait_ge(sVE.h, pair_done_ve[0])
                                for u in range(4):
                                    mm = t.transpose(tpf_ps[:, u * 128:(u + 1) * 128],
                                                     sf_ev[:, u * 128:(u + 1) * 128],
                                                     identf_sb[:])
                                sTE.inc(mm)
                        blk.tensor(tw2)
                        assert sTE.v == tp_base + 4

                        def vw(v, s=s, blki=blki):
                            stv = s_tm[:].rearrange("p (u f) -> p u f", u=4)
                            for dl in range(4):
                                dout = blki * 4 + dl
                                v.wait_ge(sTE.h, tp_tok[(blki, dl)])
                                if blki == 0 and dl == 0 and prev_s_store[0] is not None:
                                    v.wait_ge(sDST.h, prev_s_store[0])
                                dst = stv[:, :, dout * 128:(dout + 1) * 128]
                                if dl == 3:
                                    v.tensor_copy(
                                        out=dst, in_=tpf_ps[:].rearrange(
                                            "p (u f) -> p u f", u=4))
                                    sVE.inc(v.drain())
                                else:
                                    sVE.inc(v.tensor_copy(
                                        out=dst, in_=tpf_ps[:].rearrange(
                                            "p (u f) -> p u f", u=4)))
                        blk.vector(vw)
                        assert sVE.v == trv_base + 4

                    def sys(sy, s=s):
                        sy.wait_ge(sVE.h, tr_tok[(3, 3)])
                        for u in range(4):
                            sDST.dinc(sy.dma_start(out=sscr[s][u],
                                                   in_=s_tm[:, u * C:(u + 1) * C]))
                    blk.sync(sys)
                    prev_s_store[0] = sDST.v
                    wo_done_te[0] = wo_used[3]

                    if s + 1 < NSLOT:
                        def syq(sy, s=s):
                            sy.wait_ge(sTE.h, it_done[n])
                            sDQ.dinc(sy.dma_start(
                                out=qbuf[:],
                                in_=qscr[s + 1].rearrange("p g f -> p (g f)")))
                            sDQ.dinc(sy.dma_start(
                                out=mask_sb[:].rearrange("p (i f) -> p i f", i=16),
                                in_=masks[s + 1].transpose([1, 0, 2])))
                        blk.sync(syq)
                        q_tok[s + 1] = sDQ.v

            def syf(sy):
                sy.wait_ge(sDST.h, sDST.v)
            blk.sync(syf)

    with ExitStack() as ps:
        def sb(name, shape, dt=F32):
            return ps.enter_context(nc.sbuf_tensor(name, shape, dt))
        xnb = sb("cxnb", [128, NDC * 512], BF16)
        wg_g = [sb(f"cwg{i}", [128, 2048], BF16) for i in range(2)]
        wg_u = [sb(f"cwu{i}", [128, 2048], BF16) for i in range(2)]
        sg = [sb(f"csg{i}", [128, 512]) for i in range(2)]
        hsq = [sb(f"chsq{i}", [128, 512], BF16) for i in range(2)]
        h_all = sb("chall", [128, NHC * 512], BF16)
        wp_sb = [sb(f"cwp{i}", [128, 32 * 128], BF16) for i in range(2)]
        rs1 = sb("crs1", [1, 512])
        rs2 = sb("crs2", [1, 512])
        rdbc = sb("crdbc", [128, 512])
        tff = sb("ctff", [128, 512])
        gfm = [sb(f"cgfm{i}", [128, 512]) for i in range(2)]
        g_tm = sb("cgtm", [128, 4 * C])
        xs = sb("cxs", [128, C])
        ss = sb("css", [128, C])
        ys = [sb(f"cys{i}", [128, C]) for i in range(2)]

        gps = [ps.enter_context(nc.psum_tensor(f"cgps{i}", [128, 512], F32))
               for i in range(2)]
        ups = [ps.enter_context(nc.psum_tensor(f"cups{i}", [128, 512], F32))
               for i in range(2)]
        h2_ps = ps.enter_context(nc.psum_tensor("ch2", [1, 512], F32))
        pps = [ps.enter_context(nc.psum_tensor(f"cpps{i}", [128, 512], F32))
               for i in range(2)]
        tpf2_ps = ps.enter_context(nc.psum_tensor("ctpf2", [128, 512], F32))

        sDG, sDP, sDXS, sDY = sems["dg"], sems["dp"], sems["dxs"], sems["dy"]

        with nc.Block("c") as blk:
            prev_slot_gu_te = [None]
            prev_slot_pj_te = [None]
            prev_slot_cb_ve = [None]
            prev_yst = [None, None]

            for s in range(NSLOT):
                soft_barrier(blk)

                def syxn(sy, s=s):
                    if prev_slot_gu_te[0] is not None:
                        sy.wait_ge(sTE.h, prev_slot_gu_te[0])
                    sDQ_ = sems["dq"]
                    sDQ_.dinc(sy.dma_start(
                        out=xnb[:], in_=xnscr[s].rearrange("p dc f -> p (dc f)")))
                blk.sync(syxn)
                tok_xnb = sems["dq"].v

                gld_tok = {}
                gmm_tok = {}
                umm_tok = {}
                silu_tok = {}
                h_tok = {}
                hsq_tok = {}
                h2_tok = {}

                def syg0(sy, s=s):
                    if prev_slot_gu_te[0] is not None:
                        sy.wait_ge(sTE.h, prev_slot_gu_te[0])
                    guv = wguq.rearrange("fc p dc f -> fc p (dc f)")
                    sDG.dinc(sy.dma_start(out=wg_g[0][:], in_=guv[0]))
                    gld_tok[0] = sDG.dinc(sy.dma_start(out=wg_u[0][:], in_=guv[64]))
                blk.sync(syg0)

                for fc in range(NHC):
                    fpar = fc % 2
                    if fc + 1 < NHC:
                        def syg(sy, s=s, fc=fc):
                            if fc >= 1:
                                sy.wait_ge(sTE.h, umm_tok[fc - 1])
                            guv = wguq.rearrange("fc p dc f -> fc p (dc f)")
                            sDG.dinc(sy.dma_start(out=wg_g[(fc + 1) % 2][:],
                                                  in_=guv[fc + 1]))
                            gld_tok[fc + 1] = sDG.dinc(sy.dma_start(
                                out=wg_u[(fc + 1) % 2][:], in_=guv[64 + fc + 1]))
                        blk.sync(syg)

                    def tg(t, s=s, fc=fc, fpar=fpar):
                        t.wait_ge(sDG.h, gld_tok[fc])
                        t.wait_ge(sems["dq"].h, tok_xnb)
                        if prev_slot_pj_te[0] is not None and fc == 0:
                            pass
                        if fc >= 2:
                            t.wait_ge(sAC.h, silu_tok[fc - 2])
                            t.wait_ge(sVE.h, h_tok[fc - 2])
                        for dc in range(NDC):
                            mm = t.matmul(gps[fpar][:],
                                          wg_g[fpar][:, dc * 128:(dc + 1) * 128],
                                          xnb[:, dc * 512:(dc + 1) * 512],
                                          start=(dc == 0), stop=(dc == NDC - 1))
                        gmm_tok[fc] = sTE.inc(mm)
                        for dc in range(NDC):
                            mm = t.matmul(ups[fpar][:],
                                          wg_u[fpar][:, dc * 128:(dc + 1) * 128],
                                          xnb[:, dc * 512:(dc + 1) * 512],
                                          start=(dc == 0), stop=(dc == NDC - 1))
                        umm_tok[fc] = sTE.inc(mm)
                        if fc >= 2:
                            t.wait_ge(sVE.h, hsq_tok[fc - 2])
                            h2_tok[fc - 2] = sTE.inc(t.matmul(
                                h2_ps[:], ones128b_sb[:], hsq[(fc - 2) % 2][:],
                                start=(fc - 2 == 0), stop=False))
                    blk.tensor(tg)

                    def sg_(sc, s=s, fc=fc, fpar=fpar):
                        sc.wait_ge(sTE.h, gmm_tok[fc])
                        if fc >= 2:
                            sc.wait_ge(sVE.h, h_tok[fc - 2])
                        silu_tok[fc] = sAC.inc(sc.activation(
                            out=sg[fpar][:], in_=gps[fpar][:], func=AF.Silu))
                    blk.scalar(sg_)

                    def vg(v, s=s, fc=fc, fpar=fpar):
                        v.wait_ge(sAC.h, silu_tok[fc])
                        v.wait_ge(sTE.h, umm_tok[fc])
                        if prev_slot_pj_te[0] is not None:
                            v.wait_ge(sTE.h, prev_slot_pj_te[0])
                        h_tok[fc] = sVE.inc(v.tensor_tensor(
                            out=h_all[:, fc * 512:(fc + 1) * 512],
                            in0=sg[fpar][:], in1=ups[fpar][:], op=ALU.mult))
                        if fc >= 1:
                            if fc >= 3:
                                v.wait_ge(sTE.h, h2_tok[fc - 3])
                            hsl = h_all[:, (fc - 1) * 512:fc * 512]
                            hsq_tok[fc - 1] = sVE.inc(v.tensor_tensor(
                                out=hsq[(fc - 1) % 2][:], in0=hsl, in1=hsl,
                                op=ALU.mult))
                    blk.vector(vg)

                def sgt(v, s=s):
                    v.wait_ge(sTE.h, h2_tok[NHC - 3])
                    hsl = h_all[:, (NHC - 1) * 512:NHC * 512]
                    hsq_tok[NHC - 1] = sVE.inc(v.tensor_tensor(
                        out=hsq[(NHC - 1) % 2][:], in0=hsl, in1=hsl, op=ALU.mult))
                blk.vector(sgt)

                def tgt(t, s=s):
                    t.wait_ge(sVE.h, hsq_tok[NHC - 2])
                    h2_tok[NHC - 2] = sTE.inc(t.matmul(
                        h2_ps[:], ones128b_sb[:], hsq[(NHC - 2) % 2][:],
                        start=False, stop=False))
                    t.wait_ge(sVE.h, hsq_tok[NHC - 1])
                    h2_tok[NHC - 1] = sTE.inc(t.matmul(
                        h2_ps[:], ones128b_sb[:], hsq[(NHC - 1) % 2][:],
                        start=False, stop=True))
                blk.tensor(tgt)
                tok_h2 = sTE.v

                def vr(v, s=s):
                    v.wait_ge(sTE.h, tok_h2)
                    sVE.inc(v.tensor_scalar(out=rs1[:], in0=h2_ps[:],
                                            scalar1=1.0 / HF, scalar2=RMS_EPS,
                                            op0=ALU.mult, op1=ALU.add))
                    v.drain()
                blk.vector(vr)
                tok_rs = sVE.v

                def sr(sc, s=s):
                    sc.wait_ge(sVE.h, tok_rs)
                    sAC.inc(sc.sqrt(out=rs2[:], in_=rs1[:]))
                blk.scalar(sr)
                tok_rsq = sAC.v

                def vr2(v, s=s):
                    v.wait_ge(sAC.h, tok_rsq)
                    sVE.inc(v.reciprocal(out=rs1[:], in_=rs2[:]))
                    v.drain()
                blk.vector(vr2)
                tok_rr = sVE.v

                def tr(t, s=s):
                    t.wait_ge(sVE.h, tok_rr)
                    sTE.inc(t.matmul(tpf2_ps[:], onesrowf_sb[0:1, :], rs1[:],
                                     start=True, stop=True))
                blk.tensor(tr)
                tok_rbc = sTE.v

                def vr3(v, s=s):
                    v.wait_ge(sTE.h, tok_rbc)
                    sVE.inc(v.tensor_copy(out=rdbc[:], in_=tpf2_ps[:]))
                blk.vector(vr3)
                tok_rdbc = sVE.v

                pld_tok = {}
                pmm_tok = {}
                half_mm_tok = {}
                tffv_tok = {}
                gsil_tok = {}
                gtp_tok = {}
                gtr_tok = {}
                def wp_src(do, hh):
                    return wpq[hh * 4:(hh + 1) * 4, do].transpose([1, 0, 2, 3])

                def wp_dst(buf):
                    return buf[:].rearrange("p (a b f) -> p a b f", a=4, b=8)

                def syp0(sy, s=s):
                    if prev_slot_pj_te[0] is not None:
                        sy.wait_ge(sTE.h, prev_slot_pj_te[0])
                    sDP.dinc(sy.dma_start(out=wp_dst(wp_sb[0]), in_=wp_src(0, 0)))
                    pld_tok[0] = sDP.v
                    sDP.dinc(sy.dma_start(out=wp_dst(wp_sb[1]), in_=wp_src(0, 1)))
                    pld_tok[1] = sDP.v
                blk.sync(syp0)

                for do in range(NDC):
                    ppar = do % 2
                    for hh in range(2):
                        un = 2 * do + hh
                        buf = wp_sb[un % 2]
                        if un + 2 < 2 * NDC:
                            do2, hh2 = (un + 2) // 2, (un + 2) % 2

                            def syp(sy, s=s, un=un, do2=do2, hh2=hh2):
                                sy.wait_ge(sTE.h, half_mm_tok[un])
                                sDP.dinc(sy.dma_start(
                                    out=wp_dst(wp_sb[un % 2]),
                                    in_=wp_src(do2, hh2)))
                                pld_tok[un + 2] = sDP.v

                        def tp(t, s=s, do=do, hh=hh, un=un, ppar=ppar, buf=buf):
                            t.wait_ge(sDP.h, pld_tok[un])
                            if hh == 0 and do >= 2:
                                t.wait_ge(sVE.h, tffv_tok[do - 2])
                            for hl in range(32):
                                hc = hh * 32 + hl
                                mm = t.matmul(pps[ppar][:],
                                              buf[:, hl * 128:(hl + 1) * 128],
                                              h_all[:, hc * 512:(hc + 1) * 512],
                                              start=(hc == 0), stop=(hc == NHC - 1))
                            half_mm_tok[un] = sTE.inc(mm)
                        blk.tensor(tp)
                        if un + 2 < 2 * NDC:
                            blk.sync(syp)
                    pmm_tok[do] = half_mm_tok[2 * do + 1]

                    def vp(v, s=s, do=do, ppar=ppar):
                        v.wait_ge(sTE.h, pmm_tok[do])
                        v.wait_ge(sVE.h, tok_rdbc)
                        if do >= 1:
                            v.wait_ge(sAC.h, gsil_tok[do - 1])
                        v.tensor_tensor(out=tff[:], in0=pps[ppar][:],
                                        in1=rdbc[:], op=ALU.mult)
                        tffv_tok[do] = sVE.inc(v.drain())
                    blk.vector(vp)

                    def sp(sc, s=s, do=do):
                        sc.wait_ge(sVE.h, tffv_tok[do])
                        if do >= 2:
                            sc.wait_ge(sTE.h, gtp_tok[do - 2])
                        gsil_tok[do] = sAC.inc(sc.activation(
                            out=gfm[do % 2][:], in_=tff[:], func=AF.Silu))
                    blk.scalar(sp)

                    def tp2(t, s=s, do=do):
                        t.wait_ge(sAC.h, gsil_tok[do])
                        if do >= 1:
                            t.wait_ge(sVE.h, gtr_tok[do - 1])
                        else:
                            t.wait_ge(sVE.h, tok_rdbc)
                        if prev_slot_cb_ve[0] is not None:
                            t.wait_ge(sVE.h, prev_slot_cb_ve[0])
                        for u in range(4):
                            mm = t.transpose(tpf2_ps[:, u * 128:(u + 1) * 128],
                                             gfm[do % 2][:, u * 128:(u + 1) * 128],
                                             identf_sb[:])
                        gtp_tok[do] = sTE.inc(mm)
                    blk.tensor(tp2)

                    def vp2(v, s=s, do=do):
                        gv = g_tm[:].rearrange("p (u f) -> p u f", u=4)
                        v.wait_ge(sTE.h, gtp_tok[do])
                        gtr_tok[do] = sVE.inc(v.tensor_copy(
                            out=gv[:, :, do * 128:(do + 1) * 128],
                            in_=tpf2_ps[:].rearrange("p (u f) -> p u f", u=4)))
                    blk.vector(vp2)

                prev_slot_gu_te[0] = umm_tok[NHC - 1]
                prev_slot_pj_te[0] = pmm_tok[NDC - 1]

                cmb_tok = {}
                yst_tok = {}
                for u in range(4):
                    def syc1(sy, s=s, u=u):
                        if u >= 1:
                            sy.wait_ge(sVE.h, cmb_tok[u - 1])
                        elif prev_slot_cb_ve[0] is not None:
                            sy.wait_ge(sVE.h, prev_slot_cb_ve[0])
                        sDXS.dinc(sy.dma_start(
                            out=xs[:], in_=xown[s * 512 + u * 128:
                                              s * 512 + (u + 1) * 128, :]))
                        sDXS.dinc(sy.dma_start(out=ss[:], in_=sscr[s][u]))
                    blk.sync(syc1)
                    tok_ld2 = sDXS.v

                    def vc(v, s=s, u=u, tok_ld2=tok_ld2):
                        v.wait_ge(sDXS.h, tok_ld2)
                        v.wait_ge(sVE.h, gtr_tok[NDC - 1])
                        if u >= 2:
                            v.wait_ge(sDY.h, yst_tok[u - 2])
                        elif prev_yst[u] is not None:
                            v.wait_ge(sDY.h, prev_yst[u])
                        v.tensor_tensor(out=ss[:], in0=ss[:],
                                        in1=g_tm[:, u * C:(u + 1) * C], op=ALU.mult)
                        v.drain()
                        v.tensor_tensor(out=ys[u % 2][:], in0=xs[:], in1=ss[:],
                                        op=ALU.add)
                        cmb_tok[u] = sVE.inc(v.drain())
                    blk.vector(vc)

                    def syy(sy, s=s, u=u):
                        sy.wait_ge(sVE.h, cmb_tok[u])
                        yst_tok[u] = sDY.dinc(sy.dma_start(
                            out=y[s * 512 + u * 128: s * 512 + (u + 1) * 128, :],
                            in_=ys[u % 2][:]))
                    blk.sync(syy)
                prev_slot_cb_ve[0] = cmb_tok[3]
                prev_yst = [yst_tok[2], yst_tok[3]]

            def syf(sy):
                sy.wait_ge(sDY.h, sems["dy"].v)
            blk.sync(syf)

    es.close()
    return nc


_NC_CACHE = None


def kernel(**inputs):
    global _NC_CACHE
    per_core = _host_prep(inputs)
    if _NC_CACHE is None:
        _NC_CACHE = build_nc()
    res = run_bass_kernel_spmd(_NC_CACHE, per_core, core_ids=list(range(8)))
    out = np.zeros((B, T, C), dtype=np.float32)
    for c in range(8):
        b, k = c // 4, c % 4
        yc = np.asarray(res.results[c]["y"])
        for s in range(NSLOT):
            r0 = (4 * s + k) * 512
            out[b, r0:r0 + 512] = yc[s * 512:(s + 1) * 512]
    return out
